# version 04
# baseline (speedup 1.0000x reference)
import contextlib
import numpy as np
import concourse.bass as bass
import concourse.mybir as mybir
from concourse.bass_utils import run_bass_kernel_spmd

F32 = mybir.dt.float32
BF16 = mybir.dt.bfloat16
AF = mybir.ActivationFunctionType
ALU = mybir.AluOpType

PART = 128
T = 2048
D = 1024
ND = 8
SECW = T + 1
NTT = 16
EPS = 1e-5

STAGES = 4
DBG = ()


def _host_consts():
    import ml_dtypes
    bf = ml_dtypes.bfloat16
    c = {}
    c["c_ident"] = np.eye(PART, dtype=np.float32)
    c["c_ones_col_bf"] = np.ones((PART, 1), dtype=bf)
    c["c_ones_row_f"] = np.ones((1, PART), dtype=np.float32)
    e2 = np.zeros((PART, 2), dtype=np.float32)
    e2[0:64, 0] = 1.0
    e2[64:128, 1] = 1.0
    c["c_E2"] = e2
    c["c_E2X"] = np.ascontiguousarray(e2.T)
    qi = np.arange(128)[None, :]
    ki = np.arange(128)[:, None]
    c["c_maskP1"] = (qi >= ki).astype(bf)
    c["c_maskP2"] = (qi <= ki).astype(bf)
    DH, WINDOW = 64, 512
    half = np.arange(0, DH, 2, dtype=np.float64)
    inv_freq = 1.0 / 10000.0 ** (half / DH)
    tt = np.arange(T, dtype=np.float64)
    freqs = np.concatenate([tt[:, None] * inv_freq[None, :]] * 2, axis=-1)
    base = (half + 0.4 * DH) / (1.4 * DH)
    power = (tt - T // 2) / float(WINDOW // 2)
    scale = np.concatenate([base[None, :] ** power[:, None]] * 2, axis=-1)
    cos, sin = np.cos(freqs), np.sin(freqs)
    sgn = np.ones(64)
    sgn[:32] = -1.0
    t1q = (cos * scale).T
    t2q = sgn[:, None] * (sin * scale).T
    t1k = (cos / scale).T
    t2k = sgn[:, None] * (sin / scale).T
    c["c_T1Q"] = np.concatenate([t1q, t1q], 0).astype(np.float32)
    c["c_T2Q"] = np.concatenate([t2q, t2q], 0).astype(np.float32)
    c["c_T1K"] = np.concatenate([t1k, t1k], 0).astype(np.float32)
    c["c_T2K"] = np.concatenate([t2k, t2k], 0).astype(np.float32)
    return c


class Ctx:
    def __init__(self, nc, blk):
        self.nc = nc
        self.blk = blk
        self.sem = {}
        self.cnt = {}
        self.water = {}
        for name in ("pe", "ve", "sc", "gp", "dq", "dg", "da"):
            self.sem[name] = nc.alloc_semaphore("sem_" + name)
            self.cnt[name] = 0

    def tok(self, name):
        return (name, self.cnt[name])

    def _waits(self, engname, eng, deps):
        need = {}
        for d in deps:
            if d is None:
                continue
            s, v = d
            if v > 0 and need.get(s, 0) < v:
                need[s] = v
        for s, v in need.items():
            if self.water.get((engname, s), 0) < v:
                eng.wait_ge(self.sem[s], v)
                self.water[(engname, s)] = v

    def sec(self, engname, fn, deps=(), tag=None, by=1):
        out_tok = [None]

        def body(eng):
            self._waits(engname, eng, deps)
            last = fn(eng)
            if tag is not None:
                assert last is not None, f"sec({engname}) returned None with tag"
                last.then_inc(self.sem[tag], by)
                self.cnt[tag] += by
                out_tok[0] = (tag, self.cnt[tag])

        getattr(self.blk, engname)(body)
        return out_tok[0]

    def reserve(self, tag, n=1):
        self.cnt[tag] += n
        return (tag, self.cnt[tag])

    def mseq(self, engname, steps):
        def body(eng):
            for deps, fn, tok in steps:
                self._waits(engname, eng, deps)
                last = fn(eng)
                if tok is not None:
                    last.then_inc(self.sem[tok[0]], 1)
        getattr(self.blk, engname)(body)

    def dma(self, qeng, sem, out, in_, deps=(), slow=False):
        def body(eng):
            self._waits(qeng, eng, deps)
            eng.dma_start(out, in_,
                          allow_slow_non_contiguous=slow).then_inc(self.sem[sem], 16)
        getattr(self.blk, qeng)(body)
        self.cnt[sem] += 16
        return (sem, self.cnt[sem])

    def barrier(self):
        toks = [self.tok(s) for s in ("pe", "ve", "sc", "gp", "dq", "dg", "da")]

        def mk(engname):
            def body(eng):
                self._waits(engname, eng, toks)
            return body
        for e in ("tensor", "vector", "scalar", "gpsimd", "sync"):
            getattr(self.blk, e)(mk(e))


def build_program(stages=STAGES, dbg=DBG, sub4="full"):
    if isinstance(stages, int):
        stages = set(range(1, stages + 1))
    nc = bass.Bass("TRN2", target_bir_lowering=False)
    NM = {}

    def din(name, shape, dtype=F32):
        NM[name] = nc.dram_tensor(name, shape, dtype, kind="ExternalInput").ap()
        return NM[name]

    x_in = din("x", [T, D])
    y_out = nc.dram_tensor("y", [T, D], F32, kind="ExternalOutput").ap()

    for n, s in [("ln1_w", [D]), ("ln1_b", [D]), ("ln2_w", [D]), ("ln2_b", [D]),
                 ("time_decay", [D]), ("time_first", [D]),
                 ("mix_k", [D]), ("mix_v", [D]), ("mix_r", [D]),
                 ("w_key", [D, D]), ("w_value", [D, D]), ("w_recept", [D, D]),
                 ("w_output", [D, D]),
                 ("f_mix_k", [D]), ("f_mix_r", [D]),
                 ("f_w_key", [D, 4 * D]), ("f_w_recept", [D, D]),
                 ("f_w_value", [4 * D, D]),
                 ("U1", [D, 256]), ("V1", [256, 4 * D]), ("b1", [4 * D]),
                 ("U2", [4 * D, 256]), ("V2", [256, D]), ("b2", [D]),
                 ("Wq", [D, D]), ("Wk", [D, D]), ("Wv", [D, D]), ("Wo", [D, D]),
                 ("bq", [D]), ("bk", [D]), ("bv", [D]), ("bo", [D]),
                 ("wqkv", [2, D, 1536]), ("q_scale", [2, 64]), ("k_scale", [2, 64]),
                 ("l1wo", [2, 512, D]), ("ln_g", [2, D]), ("ln_b", [2, D]),
                 ("l1w1", [2, D, 2 * 2816]), ("l1b1", [2, 2 * 2816]),
                 ("l1w2", [2, 2816, D]), ("l1b2", [2, D])]:
        din(n, s)
    for n, s, dt in [("c_ident", [PART, PART], F32),
                     ("c_ones_col_bf", [PART, 1], BF16),
                     ("c_ones_row_f", [1, PART], F32),
                     ("c_E2", [PART, 2], F32), ("c_E2X", [2, PART], F32),
                     ("c_maskP1", [PART, PART], BF16),
                     ("c_maskP2", [PART, PART], BF16),
                     ("c_T1Q", [PART, T], F32), ("c_T2Q", [PART, T], F32),
                     ("c_T1K", [PART, T], F32), ("c_T2K", [PART, T], F32)]:
        din(n, s, dt)

    x0_spill = nc.dram_tensor("x0_spill", [PART, ND * T], F32).ap()

    dbg_out = {}
    dbg_shapes = {"mixl": [PART, ND * 1024], "qtr4": [PART, 4 * 512], "ktr4": [PART, 4 * T],
                  "otl4": [PART, 4 * 512], "den4": [PART, 2048],
                  "corr": [PART, ND], "ktb": [PART, ND * T], "qtb": [PART, ND * 512],
                  "probs": [PART, NTT * 512], "outT": [PART, ND * 512],
                  "x1": [PART, ND * SECW]}
    dbg_dt = {"mixl": BF16, "qtr4": BF16, "ktr4": BF16, "otl4": BF16, "den4": F32,
              "corr": F32, "ktb": BF16, "qtb": BF16, "probs": BF16, "outT": BF16,
              "x1": F32}
    for n in dbg:
        dbg_out[n] = nc.dram_tensor("dbg_" + n, dbg_shapes[n], dbg_dt[n],
                                    kind="ExternalOutput").ap()

    with contextlib.ExitStack() as _st:
        _e = _st.enter_context
        XB = _e(nc.sbuf_tensor("XB", [PART, ND * SECW], F32))
        IDN = _e(nc.sbuf_tensor("IDN", [PART, PART], F32))
        CCOL = _e(nc.sbuf_tensor("CCOL", [PART, 512], F32))
        CBF = _e(nc.sbuf_tensor("CBF", [PART, 4], BF16))
        CROWF = _e(nc.sbuf_tensor("CROWF", [1, PART], F32))
        E2S = _e(nc.sbuf_tensor("E2S", [PART, 2], F32))
        E2XS = _e(nc.sbuf_tensor("E2XS", [2, PART], F32))
        ROWS = _e(nc.sbuf_tensor("ROWS", [PART, 2048], F32))
        PSA = _e(nc.psum_tensor("PSA", [PART, 2048], F32))
        PSB = _e(nc.psum_tensor("PSB", [PART, 2048], F32))
        blk = _e(nc.Block())
        ctx = Ctx(nc, blk)

        def xsec(dt, t0, n, shift=0, buf=None):
            b = XB if buf is None else buf
            base = dt * SECW + 1 + t0 - shift
            return b[:, base:base + n]

        def xmsec(t0, n, shift=0):
            r = XB[:].rearrange("p (s c) -> p s c", s=ND)
            return r[:, :, 1 + t0 - shift:1 + t0 - shift + n]

        def csec(buf, dt, ch, n=None):
            n = ch if n is None else n
            return buf[:, dt * ch:dt * ch + n]

        def cmsec(buf, ch, n=None, nd=ND):
            n = ch if n is None else n
            return buf[:].rearrange("p (s c) -> p s c", s=nd)[:, :, 0:n]

        tload = []
        tload.append(ctx.dma("sync", "dq", IDN[:], NM["c_ident"]))
        tload.append(ctx.dma("sync", "dq", CBF[:, 0:1], NM["c_ones_col_bf"]))
        tload.append(ctx.dma("sync", "dq", CROWF[:], NM["c_ones_row_f"]))
        tload.append(ctx.dma("sync", "dq", E2S[:], NM["c_E2"]))
        tload.append(ctx.dma("sync", "dq", E2XS[:], NM["c_E2X"]))

        CC = {}
        _ccn = [0]

        def load_dcol(name, src_ap, nsec=ND):
            j = _ccn[0]
            _ccn[0] += nsec
            CC[name] = j
            ap = src_ap.rearrange("(s p) -> p s", p=PART)
            return ctx.dma("sync", "dq", CCOL[:, j:j + nsec], ap, slow=True)

        for n in ("ln1_w", "ln1_b", "ln2_w", "ln2_b", "time_decay", "time_first",
                  "mix_k", "mix_v", "mix_r", "f_mix_k", "f_mix_r"):
            tload.append(load_dcol(n, NM[n]))

        def ccol(name, dt):
            return CCOL[:, CC[name] + dt:CC[name] + dt + 1]

        CC["d"] = _ccn[0]; _ccn[0] += ND
        CC["eu"] = _ccn[0]; _ccn[0] += ND

        def _derived(s):
            s.activation(CCOL[:, CC["d"]:CC["d"] + ND],
                         CCOL[:, CC["time_decay"]:CC["time_decay"] + ND],
                         AF.Exp, bias=0.0, scale=1.0)
            s.activation(CCOL[:, CC["d"]:CC["d"] + ND],
                         CCOL[:, CC["d"]:CC["d"] + ND], AF.Exp, bias=0.0,
                         scale=-1.0)
            return s.activation(CCOL[:, CC["eu"]:CC["eu"] + ND],
                                CCOL[:, CC["time_first"]:CC["time_first"] + ND],
                                AF.Exp, bias=0.0, scale=1.0)
        t_der = ctx.sec("scalar", _derived, deps=tload, tag="sc")
        t_eps = ctx.sec("vector",
                        lambda v: v.memset(CCOL[:, 511:512], EPS),
                        deps=tload, tag="ve")
        eps_ap = CCOL[0:1, 511:512]

        _rowptr = [0]

        def rows3(n):
            o = _rowptr[0]
            if o + 3 * n > 2048:
                o = 0
            _rowptr[0] = o + 3 * n
            return (ROWS[0:1, o:o + n], ROWS[0:1, o + n:o + 2 * n],
                    ROWS[0:1, o + 2 * n:o + 3 * n])

        def ln_chunk(deps, src3d, n, xbf, sqb, ch):
            tc = ctx.sec("vector",
                         lambda v: v.tensor_copy(cmsec(xbf, ch, n), src3d),
                         deps=deps, tag="ve")
            tsq = ctx.sec("vector",
                          lambda v: v.tensor_mul(cmsec(sqb, ch, n),
                                                 cmsec(xbf, ch, n),
                                                 cmsec(xbf, ch, n)),
                          deps=[tc], tag="ve")
            p_sum = PSB[0:1, 0:n]
            p_sq = PSB[0:1, 512:512 + n]

            def mm(t):
                for dt in range(ND):
                    t.matmul(p_sum, CBF[:, 0:1], csec(xbf, dt, ch, n),
                             start=(dt == 0), stop=(dt == ND - 1))
                for dt in range(ND):
                    last = t.matmul(p_sq, CBF[:, 0:1], csec(sqb, dt, ch, n),
                                    start=(dt == 0), stop=(dt == ND - 1))
                return last
            tmm = ctx.sec("tensor", mm, deps=[tc, tsq], tag="pe")

            r_m, r_rs, r_mrs = rows3(n)
            def rowv(v):
                v.tensor_scalar(r_m, p_sum, 1.0 / D, None, ALU.mult)
                v.tensor_mul(r_rs, r_m, r_m)
                return v.scalar_tensor_tensor(r_mrs, p_sq, 1.0 / D, r_rs,
                                              ALU.mult, ALU.subtract)
            t3 = ctx.sec("vector", rowv, deps=[tmm], tag="ve")
            def rowsc(s):
                s.activation(r_rs, r_mrs, AF.Ln, bias=eps_ap, scale=1.0)
                return s.activation(r_rs, r_rs, AF.Exp, bias=0.0, scale=-0.5)
            t5 = ctx.sec("scalar", rowsc, deps=[t3, t_eps], tag="sc")
            t6 = ctx.sec("vector", lambda v: v.tensor_mul(r_mrs, r_m, r_rs),
                         deps=[t5], tag="ve")
            rs_b = PSB[:, 512:512 + n]
            mrs_b = PSB[:, 1024:1024 + n]

            def bc(t):
                t.matmul(rs_b, CROWF[:], r_rs, start=True, stop=True)
                return t.matmul(mrs_b, CROWF[:], r_mrs, start=True, stop=True)
            tbc = ctx.sec("tensor", bc, deps=[t5, t6], tag="pe")
            return tbc, rs_b, mrs_b, tc

        def ln_apply_wb(n, xn_buf, chw, t0, rs_b, mrs_b, wname, bname, deps):
            def ap1(v):
                for dt in range(ND):
                    v.scalar_tensor_tensor(
                        xn_buf[:, dt * (chw + 1) + 1:dt * (chw + 1) + 1 + n],
                        xsec(dt, t0, n), ccol(wname, dt), rs_b,
                        ALU.mult, ALU.mult)
                for dt in range(ND):
                    last = v.scalar_tensor_tensor(
                        xn_buf[:, dt * (chw + 1) + 1:dt * (chw + 1) + 1 + n],
                        mrs_b, ccol(wname, dt),
                        xn_buf[:, dt * (chw + 1) + 1:dt * (chw + 1) + 1 + n],
                        ALU.mult, ALU.subtract)
                return last
            tv = ctx.sec("vector", ap1, deps=deps, tag="ve")
            ts = ctx.sec("scalar",
                         lambda s: [s.activation(
                             xn_buf[:, dt * (chw + 1) + 1:dt * (chw + 1) + 1 + n],
                             xn_buf[:, dt * (chw + 1) + 1:dt * (chw + 1) + 1 + n],
                             AF.Identity, bias=ccol(bname, dt), scale=-1.0)
                             for dt in range(ND)][-1],
                         deps=[tv], tag="sc")
            return tv, ts

        with contextlib.ExitStack() as _s4:
            tstage = _s4.enter_context(nc.sbuf_tensor("tstage", [PART, 2 * D], F32))
            tts = []
            for tt in range(NTT):
                sl = tstage[:, (tt % 2) * D:(tt % 2) * D + D]
                dep = [tts[tt - 2][1]] if tt >= 2 else []
                tdma = ctx.dma("sync", "dq", sl,
                               x_in[tt * PART:(tt + 1) * PART, :], deps=dep)

                def tr(t, sl=sl):
                    for dt in range(ND):
                        last = t.transpose(PSA[:, dt * 128:dt * 128 + 128],
                                           sl[:, dt * 128:(dt + 1) * 128],
                                           IDN[:])
                    return last
                dtr = [tdma, tload[0]] + ([tts[tt - 1][1]] if tt else [])
                tpe = ctx.sec("tensor", tr, deps=dtr, tag="pe")

                def cp2(v, tt=tt):
                    dst = xmsec(tt * PART, PART)
                    src = PSA[:, 0:ND * 128].rearrange("p (s c) -> p s c", s=ND)
                    return v.tensor_copy(dst, src)
                tve = ctx.sec("vector", cp2, deps=[tpe], tag="ve")
                tts.append((tpe, tve))

            def zpad(v):
                r = XB[:].rearrange("p (s c) -> p s c", s=ND)
                return v.memset(r[:, :, 0:1], 0.0)
            ctx.sec("vector", zpad, deps=[], tag="ve")
        ctx.barrier()

        if 1 in stages:
            CH = 256
            with contextlib.ExitStack() as _s1:
                _f = _s1.enter_context
                wk_sb = _f(nc.sbuf_tensor("wk_sb", [PART, ND * D], BF16))
                wv_sb = _f(nc.sbuf_tensor("wv_sb", [PART, ND * D], BF16))
                wr_sb = _f(nc.sbuf_tensor("wr_sb", [PART, ND * D], BF16))
                wo_sb = _f(nc.sbuf_tensor("wo_sb", [PART, ND * D], BF16))
                xn1 = _f(nc.sbuf_tensor("xn1", [PART, ND * (CH + 1)], F32))
                ek = _f(nc.sbuf_tensor("ek", [PART, ND * CH], F32))
                ekv = _f(nc.sbuf_tensor("ekv", [PART, ND * CH], F32))
                srb = _f(nc.sbuf_tensor("srb", [PART, ND * CH], F32))
                Asc = _f(nc.sbuf_tensor("Asc", [PART, ND * (CH + 1)], F32))
                Bsc = _f(nc.sbuf_tensor("Bsc", [PART, ND * (CH + 1)], F32))
                mixA = _f(nc.sbuf_tensor("mixA", [PART, ND * CH], BF16))
                mixB = _f(nc.sbuf_tensor("mixB", [PART, ND * CH], BF16))
                mixC = _f(nc.sbuf_tensor("mixC", [PART, ND * CH], BF16))
                bnd = _f(nc.sbuf_tensor("bnd", [PART, 16], F32))
                gscr = _f(nc.sbuf_tensor("gscr", [PART, CH], F32))
                wtoks = []
                for wname, wsb in [("w_key", wk_sb), ("w_value", wv_sb),
                                   ("w_recept", wr_sb), ("w_output", wo_sb)]:
                    src = NM[wname].rearrange("(s p) o -> p s o", p=PART)
                    dst = wsb[:].rearrange("p (s o) -> p s o", s=ND)
                    wtoks.append(ctx.dma("gpsimd", "dg", dst, src))

                def slot(i, n=CH):
                    if i < 4:
                        return PSA[:, i * 512:i * 512 + n]
                    return PSB[:, (i - 4) * 512:(i - 4) * 512 + n]

                prev_ta1 = None
                for c in range(T // CH):
                    t0 = c * CH
                    dd = list(wtoks) + ([prev_ta1] if prev_ta1 else [])
                    tbc, rs_b, mrs_b, tc = ln_chunk(dd, xmsec(t0, CH), CH,
                                                    mixC, mixB, CH)

                    def bsave(v):
                        if c == 0:
                            return v.memset(bnd[:, 0:ND], 0.0)
                        r = xn1[:].rearrange("p (s c) -> p s c", s=ND)
                        return v.tensor_copy(
                            bnd[:, 0:ND],
                            r[:, :, CH:CH + 1].squeeze())
                    tbs = ctx.sec("vector", bsave, deps=[], tag="ve")

                    tap1, tap2 = ln_apply_wb(CH, xn1, CH, t0, rs_b, mrs_b,
                                             "ln1_w", "ln1_b", [tbc, tbs])
                    tpad = ctx.sec("vector",
                                   lambda v: v.tensor_copy(
                                       xn1[:].rearrange("p (s c) -> p s c", s=ND)[:, :, 0:1].squeeze(),
                                       bnd[:, 0:ND]),
                                   deps=[tap2], tag="ve")

                    xn1r = xn1[:].rearrange("p (s c) -> p s c", s=ND)
                    tam = ctx.sec("vector",
                                  lambda v: v.tensor_sub(cmsec(ekv, CH),
                                                         xn1r[:, :, 1:1 + CH],
                                                         xn1r[:, :, 0:CH]),
                                  deps=[tap2, tpad], tag="ve")

                    def mix(eng, nm, dst):
                        for dt in range(ND):
                            last = eng.scalar_tensor_tensor(
                                csec(dst, dt, CH), csec(ekv, dt, CH),
                                ccol(nm, dt),
                                xn1[:, dt * (CH + 1):dt * (CH + 1) + CH],
                                ALU.mult, ALU.add)
                        return last

                    def mix_g(g, nm, dst):
                        for dt in range(ND):
                            g.tensor_scalar(gscr[:], csec(ekv, dt, CH),
                                            ccol(nm, dt), None, ALU.mult)
                            last = g.tensor_tensor(
                                csec(dst, dt, CH), gscr[:],
                                xn1[:, dt * (CH + 1):dt * (CH + 1) + CH],
                                ALU.add)
                        return last
                    tmk = ctx.sec("vector", lambda v: mix(v, "mix_k", mixA),
                                  deps=[tam, tc], tag="ve")
                    tmv = ctx.sec("gpsimd", lambda g: mix_g(g, "mix_v", mixB),
                                  deps=[tam, tbc], tag="gp")
                    tmr = ctx.sec("gpsimd", lambda g: mix_g(g, "mix_r", mixC),
                                  deps=[tam, tc, tmv], tag="gp")

                    def mmround(t, wsb, mbf, r):
                        for m in range(r * 4, r * 4 + 4):
                            for kt in range(ND):
                                last = t.matmul(
                                    slot(m),
                                    wsb[:, kt * D + m * 128:kt * D + (m + 1) * 128],
                                    csec(mbf, kt, CH), start=(kt == 0),
                                    stop=(kt == ND - 1))
                        return last

                    tk0 = ctx.sec("tensor", lambda t: mmround(t, wk_sb, mixA, 0),
                                  deps=[tmk, tbc, tap1], tag="pe")
                    tk1 = ctx.sec("tensor", lambda t: mmround(t, wk_sb, mixA, 1),
                                  deps=[tk0], tag="pe")
                    tek0 = ctx.reserve("sc")
                    tek1 = ctx.reserve("sc")
                    sc1_steps = [
                        ([tk0], lambda s: [s.activation(csec(ek, m, CH), slot(m),
                                                        AF.Exp, bias=0.0,
                                                        scale=1.0)
                                           for m in range(4)][-1], tek0),
                        ([tk1], lambda s: [s.activation(csec(ek, m, CH), slot(m),
                                                        AF.Exp, bias=0.0,
                                                        scale=1.0)
                                           for m in range(4, 8)][-1], tek1),
                    ]

                    tv0 = ctx.sec("tensor", lambda t: mmround(t, wv_sb, mixB, 0),
                                  deps=[tmv, tek0, tmk], tag="pe")
                    tev0 = ctx.sec("vector",
                                   lambda v: [v.tensor_mul(csec(ekv, m, CH),
                                                           csec(ek, m, CH),
                                                           slot(m))
                                              for m in range(4)][-1],
                                   deps=[tv0, tek0, tmr], tag="ve")
                    tv1 = ctx.sec("tensor", lambda t: mmround(t, wv_sb, mixB, 1),
                                  deps=[tek1], tag="pe")
                    tev1 = ctx.sec("vector",
                                   lambda v: [v.tensor_mul(csec(ekv, m, CH),
                                                           csec(ek, m, CH),
                                                           slot(m))
                                              for m in range(4, 8)][-1],
                                   deps=[tv1, tek1], tag="ve")

                    tr0 = ctx.sec("tensor", lambda t: mmround(t, wr_sb, mixC, 0),
                                  deps=[tmr, tev0], tag="pe")
                    tr1 = ctx.sec("tensor", lambda t: mmround(t, wr_sb, mixC, 1),
                                  deps=[tev1], tag="pe")
                    tsr0 = ctx.reserve("sc")
                    tsr1 = ctx.reserve("sc")
                    sc1_steps += [
                        ([tr0], lambda s: [s.activation(csec(srb, m, CH), slot(m),
                                                        AF.Sigmoid, bias=0.0,
                                                        scale=1.0)
                                           for m in range(4)][-1], tsr0),
                        ([tr1], lambda s: [s.activation(csec(srb, m, CH), slot(m),
                                                        AF.Sigmoid, bias=0.0,
                                                        scale=1.0)
                                           for m in range(4, 8)][-1], tsr1),
                    ]
                    ctx.mseq("scalar", sc1_steps)

                    def carry(eng, buf):
                        r = buf[:].rearrange("p (s c) -> p s c", s=ND)
                        if c == 0:
                            return eng.memset(r[:, :, 0:1], 0.0)
                        return eng.tensor_copy(
                            r[:, :, 0:1].squeeze(),
                            r[:, :, CH:CH + 1].squeeze())
                    tca = ctx.sec("vector", lambda v: carry(v, Asc), deps=[],
                                  tag="ve")
                    tsa = ctx.sec("vector",
                                  lambda v: [v.tensor_tensor_scan(
                                      Asc[:, dt * (CH + 1) + 1:dt * (CH + 1) + 1 + CH],
                                      ccol("d", dt).broadcast_to([PART, CH]),
                                      csec(ekv, dt, CH),
                                      Asc[:, dt * (CH + 1):dt * (CH + 1) + 1],
                                      ALU.mult, ALU.add) for dt in range(ND)][-1],
                                  deps=[tev1, tca, t_der], tag="ve")
                    tcb = ctx.sec("vector", lambda v: carry(v, Bsc), deps=[],
                                  tag="ve")
                    tsb = ctx.sec("vector",
                                  lambda v: [v.tensor_tensor_scan(
                                      Bsc[:, dt * (CH + 1) + 1:dt * (CH + 1) + 1 + CH],
                                      ccol("d", dt).broadcast_to([PART, CH]),
                                      csec(ek, dt, CH),
                                      Bsc[:, dt * (CH + 1):dt * (CH + 1) + 1],
                                      ALU.mult, ALU.add) for dt in range(ND)][-1],
                                  deps=[tek1, tcb, t_der], tag="ve")

                    tnum = ctx.sec("vector",
                                   lambda v: [v.scalar_tensor_tensor(
                                       csec(ekv, dt, CH), csec(ekv, dt, CH),
                                       ccol("eu", dt),
                                       Asc[:, dt * (CH + 1):dt * (CH + 1) + CH],
                                       ALU.mult, ALU.add) for dt in range(ND)][-1],
                                   deps=[tsa], tag="ve")
                    def den_g(g):
                        for dt in range(ND):
                            g.tensor_scalar(csec(ek, dt, CH), csec(ek, dt, CH),
                                            ccol("eu", dt), None, ALU.mult)
                            last = g.tensor_tensor(
                                csec(ek, dt, CH), csec(ek, dt, CH),
                                Bsc[:, dt * (CH + 1):dt * (CH + 1) + CH],
                                ALU.add)
                        return last
                    tden = ctx.sec("gpsimd", den_g, deps=[tsb, tek1, tmr],
                                   tag="gp")
                    def rden_s(s):
                        s.activation(ek[:], ek[:], AF.Ln, bias=0.0, scale=1.0)
                        return s.activation(ek[:], ek[:], AF.Exp, bias=0.0,
                                            scale=-1.0)
                    trc = ctx.sec("scalar", rden_s, deps=[tden], tag="sc")

                    tyy = ctx.sec("vector",
                                  lambda v: v.tensor_mul(ekv[:], ekv[:], ek[:]),
                                  deps=[trc, tnum], tag="ve")
                    tyb = ctx.sec("vector",
                                  lambda v: v.tensor_mul(mixA[:], ekv[:], srb[:]),
                                  deps=[tyy, tsr1, tk1], tag="ve")

                    tspill = ctx.dma("gpsimd", "dg",
                                     x0_spill[:, c * ND * CH:(c + 1) * ND * CH].rearrange("p (s c) -> p s c", s=ND),
                                     xmsec(t0, CH), deps=[tden])
                    tw0 = ctx.sec("tensor", lambda t: mmround(t, wo_sb, mixA, 0),
                                  deps=[tyb, tsr1], tag="pe")
                    ta0 = ctx.sec("vector",
                                  lambda v: [v.tensor_add(xsec(m, t0, CH),
                                                          xsec(m, t0, CH),
                                                          slot(m))
                                             for m in range(4)][-1],
                                  deps=[tw0, tspill], tag="ve")
                    tw1 = ctx.sec("tensor", lambda t: mmround(t, wo_sb, mixA, 1),
                                  deps=[tw0], tag="pe")
                    ta1 = ctx.sec("vector",
                                  lambda v: [v.tensor_add(xsec(m, t0, CH),
                                                          xsec(m, t0, CH),
                                                          slot(m))
                                             for m in range(4, 8)][-1],
                                  deps=[tw1, tspill, ta0], tag="ve")
                    prev_ta1 = ta1

            ctx.barrier()

            CH2 = 512
            with contextlib.ExitStack() as _s2:
                _f = _s2.enter_context
                fwr_sb = _f(nc.sbuf_tensor("fwr_sb", [PART, ND * D], BF16))
                wstr = _f(nc.sbuf_tensor("wstr", [PART, 8192], BF16))
                relu2 = _f(nc.sbuf_tensor("relu2", [PART, 32 * CH2], BF16))
                relu_f = _f(nc.sbuf_tensor("relu_f", [PART, 2048], F32))
                sfr = _f(nc.sbuf_tensor("sfr", [PART, ND * CH2], F32))
                xn2 = _f(nc.sbuf_tensor("xn2", [PART, ND * (CH2 + 1)], F32))
                fmk = _f(nc.sbuf_tensor("fmk", [PART, ND * CH2], BF16))
                fmr = _f(nc.sbuf_tensor("fmr", [PART, ND * CH2], BF16))
                gsc = _f(nc.sbuf_tensor("gsc", [PART, CH2], F32))
                bnd2 = _f(nc.sbuf_tensor("bnd2", [PART, 16], F32))
                x0c2 = wstr.bitcast(F32)
                tfwr = ctx.dma("gpsimd", "dg",
                               fwr_sb[:].rearrange("p (s o) -> p s o", s=ND),
                               NM["f_w_recept"].rearrange("(s p) o -> p s o",
                                                          p=PART))

                def slot8(i, n=CH2):
                    if i < 4:
                        return PSA[:, i * 512:i * 512 + n]
                    return PSB[:, (i - 4) * 512:(i - 4) * 512 + n]

                prev_ev = None
                for c in range(T // CH2):
                    t0 = c * CH2
                    dd = [tfwr] + ([prev_ev] if prev_ev else [])
                    def bsave2(v):
                        if c == 0:
                            return v.memset(bnd2[:, 0:ND], 0.0)
                        r = xn2[:].rearrange("p (s c) -> p s c", s=ND)
                        return v.tensor_copy(
                            bnd2[:, 0:ND],
                            r[:, :, CH2:CH2 + 1].squeeze())
                    tbs2 = ctx.sec("vector", bsave2, deps=[], tag="ve")

                    tbc2, rs_b2, mrs_b2, tc2 = ln_chunk(dd, xmsec(t0, CH2), CH2,
                                                        fmr, fmk, CH2)
                    tap1b, tap2b = ln_apply_wb(CH2, xn2, CH2, t0, rs_b2, mrs_b2,
                                               "ln2_w", "ln2_b", [tbc2, tbs2])
                    tpad2 = ctx.sec("vector",
                                    lambda v: v.tensor_copy(
                                        xn2[:].rearrange("p (s c) -> p s c", s=ND)[:, :, 0:1].squeeze(),
                                        bnd2[:, 0:ND]),
                                    deps=[tap2b], tag="ve")

                    xn2w = lambda dt, sh=0: xn2[:, dt * (CH2 + 1) + 1 - sh:dt * (CH2 + 1) + 1 - sh + CH2]

                    def fmixv(v):
                        for dt in range(ND):
                            v.tensor_sub(relu_f[:, 0:CH2], xn2w(dt), xn2w(dt, 1))
                            last = v.scalar_tensor_tensor(
                                csec(fmk, dt, CH2), relu_f[:, 0:CH2],
                                ccol("f_mix_k", dt), xn2w(dt, 1), ALU.mult,
                                ALU.add)
                        return last

                    def fmixg(g):
                        for dt in range(ND):
                            g.tensor_sub(gsc[:], xn2w(dt), xn2w(dt, 1))
                            g.tensor_scalar(gsc[:], gsc[:], ccol("f_mix_r", dt),
                                            None, ALU.mult)
                            last = g.tensor_tensor(csec(fmr, dt, CH2), gsc[:],
                                                   xn2w(dt, 1), ALU.add)
                        return last
                    tfmk = ctx.sec("vector", fmixv, deps=[tpad2, tap2b], tag="ve")
                    tfmr = ctx.sec("gpsimd", fmixg, deps=[tpad2, tap2b, tc2],
                                   tag="gp")

                    def frr(t, r):
                        for m in range(r * 4, r * 4 + 4):
                            for kt in range(ND):
                                last = t.matmul(
                                    PSA[:, (m % 4) * 512:(m % 4) * 512 + CH2],
                                    fwr_sb[:, kt * D + m * 128:kt * D + (m + 1) * 128],
                                    csec(fmr, kt, CH2), start=(kt == 0),
                                    stop=(kt == ND - 1))
                        return last
                    tfr0 = ctx.sec("tensor", lambda t: frr(t, 0),
                                   deps=[tfmr, tbc2], tag="pe")
                    tsf0 = ctx.reserve("sc")
                    tsf1 = ctx.reserve("sc")
                    tfr1 = ctx.sec("tensor", lambda t: frr(t, 1), deps=[tsf0],
                                   tag="pe")
                    ctx.mseq("scalar", [
                        ([tfr0], lambda s: [s.activation(csec(sfr, m, CH2),
                                                         PSA[:, (m % 4) * 512:(m % 4) * 512 + CH2],
                                                         AF.Sigmoid, bias=0.0,
                                                         scale=1.0)
                                            for m in range(4)][-1], tsf0),
                        ([tfr1], lambda s: [s.activation(csec(sfr, m, CH2),
                                                         PSA[:, (m % 4) * 512:(m % 4) * 512 + CH2],
                                                         AF.Sigmoid, bias=0.0,
                                                         scale=1.0)
                                            for m in range(4, 8)][-1], tsf1),
                    ])

                    ev = [tsf1, tap1b]
                    grp_ev = {0: tsf1, 1: tap1b}
                    tw_prev = {}
                    tsq_last = None
                    for j in range(8):
                        src = NM["f_w_key"][:, j * 512:(j + 1) * 512]
                        src = src.rearrange("(s p) o -> p s o", p=PART)
                        wslice = wstr[:, (j % 2) * 4096:(j % 2) * 4096 + 4096]
                        depw = [tw_prev[j - 2]] if j >= 2 else []
                        twj = ctx.dma("gpsimd", "dg",
                                      wslice.rearrange("p (s o) -> p s o", s=ND),
                                      src, deps=depw + [tfmr])
                        g = j % 2

                        def fkr(t, j=j, wslice=wslice, g=g):
                            for mi in range(4):
                                ps = (PSA if g == 0 else PSB)[:, mi * 512:mi * 512 + CH2]
                                for kt in range(ND):
                                    last = t.matmul(
                                        ps,
                                        wslice[:, kt * 512 + mi * 128:kt * 512 + (mi + 1) * 128],
                                        csec(fmk, kt, CH2), start=(kt == 0),
                                        stop=(kt == ND - 1))
                            return last
                        tfkr = ctx.sec("tensor", fkr,
                                       deps=[twj, tfmk, grp_ev[g]], tag="pe")
                        def rl(v, g=g):
                            ps = PSA if g == 0 else PSB
                            return v.tensor_scalar(relu_f[:], ps[:], 0.0, None,
                                                   ALU.max)
                        trl = ctx.sec("vector", rl, deps=[tfkr], tag="ve")
                        def sq(g, j=j):
                            return g.tensor_mul(
                                relu2[:, j * 4 * CH2:(j + 1) * 4 * CH2],
                                relu_f[:], relu_f[:])
                        tsq_last = ctx.sec("gpsimd", sq, deps=[trl], tag="gp")
                        grp_ev[g] = tsq_last
                        tw_prev[j] = tsq_last

                    tacc = None
                    twv_prev = {}
                    for j in range(8):
                        src = NM["f_w_value"][j * 512:(j + 1) * 512, :]
                        src = src.rearrange("(s p) o -> p s o", p=PART)
                        wslice = wstr[:, (j % 2) * 4096:(j % 2) * 4096 + 4096]
                        depw = [twv_prev[j - 2]] if j >= 2 else [tw_prev[6 + j]]
                        twv = ctx.dma("gpsimd", "dg",
                                      wslice.rearrange("p (s o) -> p s o", s=4),
                                      src, deps=depw)

                        def vkr(t, j=j, wslice=wslice):
                            for m in range(8):
                                for kt in range(4):
                                    last = t.matmul(
                                        slot8(m),
                                        wslice[:, kt * D + m * 128:kt * D + (m + 1) * 128],
                                        relu2[:, (j * 4 + kt) * CH2:(j * 4 + kt + 1) * CH2],
                                        start=(j == 0 and kt == 0),
                                        stop=(j == 7 and kt == 3))
                            return last
                        dep_r = [twv, tsq_last] + ([tacc] if tacc else [])
                        tacc = ctx.sec("tensor", vkr, deps=dep_r, tag="pe")
                        twv_prev[j] = tacc

                    def vke(v):
                        for grp in range(2):
                            ps = PSA if grp == 0 else PSB
                            sfr3 = cmsec(sfr, CH2)[:, grp * 4:grp * 4 + 4, :]
                            rf3 = relu_f[:].rearrange("p (s c) -> p s c", s=4)
                            v.tensor_mul(rf3, sfr3, ps[:].rearrange("p (s c) -> p s c", s=4))
                            last = v.tensor_add(
                                xmsec(t0, CH2)[:, grp * 4:grp * 4 + 4, :],
                                xmsec(t0, CH2)[:, grp * 4:grp * 4 + 4, :], rf3)
                        return last
                    tvke = ctx.sec("vector", vke, deps=[tacc, tsf1], tag="ve")
                    tx0a = ctx.dma("gpsimd", "dg",
                                   x0c2[:].rearrange("p (a s c) -> p a s c", a=2, s=ND)[:, 0],
                                   x0_spill[:, (2 * c) * ND * 256:(2 * c + 1) * ND * 256].rearrange("p (s c) -> p s c", s=ND),
                                   deps=[tacc])
                    tx0b = ctx.dma("gpsimd", "dg",
                                   x0c2[:].rearrange("p (a s c) -> p a s c", a=2, s=ND)[:, 1],
                                   x0_spill[:, (2 * c + 1) * ND * 256:(2 * c + 2) * ND * 256].rearrange("p (s c) -> p s c", s=ND),
                                   deps=[tacc])
                    x0v = x0c2[:].rearrange("p (a s c) -> p a s c", a=2, s=ND)
                    x0m = x0v.transpose(0, 2, 1, 3) if False else None
                    def zadd(v):
                        for half in range(2):
                            last = v.tensor_add(
                                xmsec(t0 + half * 256, 256),
                                xmsec(t0 + half * 256, 256),
                                x0v[:, half])
                        return last
                    tz2 = ctx.sec("vector", zadd, deps=[tx0a, tx0b, tvke], tag="ve")
                    tbc3, rs_b3, mrs_b3, _tc3 = ln_chunk([tz2], xmsec(t0, CH2),
                                                         CH2, fmr, fmk, CH2)
                    def ap3(v):
                        for dt in range(ND):
                            v.tensor_mul(xsec(dt, t0, CH2), xsec(dt, t0, CH2),
                                         rs_b3)
                        for dt in range(ND):
                            last = v.tensor_sub(xsec(dt, t0, CH2),
                                                xsec(dt, t0, CH2), mrs_b3)
                        return last
                    prev_ev = ctx.sec("vector", ap3, deps=[tbc3, tz2], tag="ve")

            ctx.barrier()

            if "x1" in dbg_out:
                ctx.dma("sync", "dq", dbg_out["x1"], XB[:])


        if 2 in stages:
            CHF = 512
            NCF = T // CHF
            with contextlib.ExitStack() as _sf:
                _f = _sf.enter_context
                u1_sb = _f(nc.sbuf_tensor("u1_sb", [PART, ND * 256], BF16))
                v1_sb = _f(nc.sbuf_tensor("v1_sb", [PART, 2 * 4096], BF16))
                u2_sb = _f(nc.sbuf_tensor("u2_sb", [PART, 32 * 256], BF16))
                v2_sb = _f(nc.sbuf_tensor("v2_sb", [PART, 2 * 1024], BF16))
                xbff = _f(nc.sbuf_tensor("xbff", [PART, 2 * ND * CHF], BF16))
                sqbf = _f(nc.sbuf_tensor("sqbf", [PART, ND * CHF], BF16))
                p_bf = _f(nc.sbuf_tensor("p_bf", [PART, 2 * 2 * CHF], BF16))
                h_bf = _f(nc.sbuf_tensor("h_bf", [PART, 2 * 32 * CHF], BF16))
                fwt = []
                for nm, wsb, sct in [("U1", u1_sb, 8), ("V1", v1_sb, 2),
                                     ("U2", u2_sb, 32), ("V2", v2_sb, 2)]:
                    srcw = NM[nm].rearrange("(s p) o -> p s o", p=PART)
                    dstw = wsb[:].rearrange("p (s o) -> p s o", s=sct)
                    fwt.append(ctx.dma("gpsimd", "dg", dstw, srcw))
                fwt.append(load_dcol("b1", NM["b1"], 32))
                fwt.append(load_dcol("b2", NM["b2"]))

                front_done = {}
                back_q_done = {-2: None, -1: None}
                prev_post = [None]

                def front(c):
                    par = c % 2
                    t0 = c * CHF
                    xv = xbff[:, par * ND * CHF:(par + 1) * ND * CHF]
                    pv_ = p_bf[:, par * 2 * CHF:(par + 1) * 2 * CHF]
                    hv = h_bf[:, par * 32 * CHF:(par + 1) * 32 * CHF]
                    dd = fwt + [back_q_done[c - 2], prev_post[0]]
                    tcf = ctx.sec("vector",
                                  lambda v: v.tensor_copy(
                                      xv.rearrange("p (s c) -> p s c", s=ND),
                                      xmsec(t0, CHF)),
                                  deps=dd, tag="ve")

                    def pmm(t):
                        for m in range(2):
                            for kt in range(ND):
                                last = t.matmul(
                                    PSA[:, m * 512:m * 512 + CHF],
                                    u1_sb[:, kt * 256 + m * 128:kt * 256 + (m + 1) * 128],
                                    xv[:, kt * CHF:(kt + 1) * CHF],
                                    start=(kt == 0), stop=(kt == ND - 1))
                        return last
                    dprev = ([front_done[c - 1][0], front_done[c - 1][1]]
                             if (c - 1) in front_done else [])
                    tpm = ctx.sec("tensor", pmm, deps=[tcf] + fwt + dprev,
                                  tag="pe")
                    tpe_ = ctx.sec("vector",
                                   lambda v: v.tensor_copy(
                                       pv_.rearrange("p (s c) -> p s c", s=2),
                                       PSA[:, 0:1024].rearrange("p (s c) -> p s c", s=2)),
                                   deps=[tpm], tag="ve")
                    grp_ev = {0: tpe_, 1: tpe_}
                    hv_steps = []
                    for rnd in range(8):
                        g = rnd % 2
                        def hmm(t, rnd=rnd, g=g, pv_=pv_):
                            for mi in range(4):
                                m = rnd * 4 + mi
                                for kt in range(2):
                                    last = t.matmul(
                                        PSA[:, (g * 2 + mi % 2) * 512:(g * 2 + mi % 2) * 512 + CHF] if False else PSA[:, ((g * 4 + mi) % 4) * 512 // 1:((g * 4 + mi) % 4) * 512 + CHF],
                                        v1_sb[:, kt * 4096 + m * 128:kt * 4096 + (m + 1) * 128],
                                        pv_[:, kt * CHF:(kt + 1) * CHF],
                                        start=(kt == 0), stop=(kt == 1))
                            return last
                        def hmm2(t, rnd=rnd, pv_=pv_):
                            for mi in range(4):
                                m = rnd * 4 + mi
                                for kt in range(2):
                                    last = t.matmul(
                                        PSA[:, mi * 512:mi * 512 + CHF],
                                        v1_sb[:, kt * 4096 + m * 128:kt * 4096 + (m + 1) * 128],
                                        pv_[:, kt * CHF:(kt + 1) * CHF],
                                        start=(kt == 0), stop=(kt == 1))
                            return last
                        thm = ctx.sec("tensor", hmm2,
                                      deps=[tpe_, grp_ev[1 - g]], tag="pe")
                        def hev(s, rnd=rnd, hv=hv):
                            for mi in range(4):
                                m = rnd * 4 + mi
                                last = s.activation(
                                    hv[:, m * CHF:(m + 1) * CHF],
                                    PSA[:, mi * 512:mi * 512 + CHF],
                                    AF.Gelu_apprx_tanh,
                                    bias=CCOL[:, CC["b1"] + m:CC["b1"] + m + 1],
                                    scale=1.0)
                            return last
                        gtok = ctx.reserve("sc")
                        hv_steps.append(([thm], hev, gtok))
                        grp_ev[g] = gtok
                    ctx.mseq("scalar", hv_steps)
                    front_done[c] = grp_ev
                    return grp_ev

                def back(c):
                    par = c % 2
                    t0 = c * CHF
                    hv = h_bf[:, par * 32 * CHF:(par + 1) * 32 * CHF]
                    pv_ = p_bf[:, par * 2 * CHF:(par + 1) * 2 * CHF]
                    ge = front_done[c]

                    def qmm(t):
                        for m in range(2):
                            for kt in range(32):
                                last = t.matmul(
                                    PSB[:, m * 512:m * 512 + CHF],
                                    u2_sb[:, kt * 256 + m * 128:kt * 256 + (m + 1) * 128],
                                    hv[:, kt * CHF:(kt + 1) * CHF],
                                    start=(kt == 0), stop=(kt == 31))
                        return last
                    tqm = ctx.sec("tensor", qmm,
                                  deps=[ge[0], ge[1], prev_post[0]], tag="pe")
                    tqe = ctx.sec("vector",
                                  lambda v: v.tensor_copy(
                                      pv_.rearrange("p (s c) -> p s c", s=2),
                                      PSB[:, 0:1024].rearrange("p (s c) -> p s c", s=2)),
                                  deps=[tqm], tag="ve")
                    back_q_done[c] = tqe
                    tout_ev = tqe
                    for rnd in range(2):
                        def omm(t, rnd=rnd, pv_=pv_):
                            for mi in range(4):
                                m = rnd * 4 + mi
                                for kt in range(2):
                                    last = t.matmul(
                                        PSB[:, mi * 512:mi * 512 + CHF],
                                        v2_sb[:, kt * 1024 + m * 128:kt * 1024 + (m + 1) * 128],
                                        pv_[:, kt * CHF:(kt + 1) * CHF],
                                        start=(kt == 0), stop=(kt == 1))
                            return last
                        tom = ctx.sec("tensor", omm, deps=[tout_ev], tag="pe")
                        def oev(v, rnd=rnd):
                            for mi in range(4):
                                m = rnd * 4 + mi
                                last = v.scalar_tensor_tensor(
                                    xsec(m, t0, CHF),
                                    PSB[:, mi * 512:mi * 512 + CHF],
                                    CCOL[:, CC["b2"] + m:CC["b2"] + m + 1],
                                    xsec(m, t0, CHF), ALU.add, ALU.add)
                            return last
                        tout_ev = ctx.sec("vector", oev, deps=[tom], tag="ve")
                    xv = xbff[:, par * ND * CHF:(par + 1) * ND * CHF]
                    tbcf, rs_bf_, mrs_bf_, _tcf2 = ln_chunk(
                        [tout_ev], xmsec(t0, CHF), CHF, xv, sqbf, CHF)
                    def apf(v):
                        for dt in range(ND):
                            v.tensor_mul(xsec(dt, t0, CHF), xsec(dt, t0, CHF),
                                         rs_bf_)
                        for dt in range(ND):
                            last = v.tensor_sub(xsec(dt, t0, CHF),
                                                xsec(dt, t0, CHF), mrs_bf_)
                        return last
                    prev_post[0] = ctx.sec("vector", apf, deps=[tbcf, tout_ev],
                                           tag="ve")
                    return prev_post[0]

                for c in range(NCF + 1):
                    if c < NCF:
                        front(c)
                    if c >= 1:
                        back(c - 1)
            ctx.barrier()

        if 3 in stages:
            CHA = 512
            NH = 16
            wrow_bf = nc.dram_tensor("wrow_bf", [4 * D, D], BF16).ap()
            with contextlib.ExitStack() as _sa:
                _f = _sa.enter_context
                wpc = _f(nc.sbuf_tensor("wpc", [PART, ND * 1024], BF16))
                woT = _f(nc.sbuf_tensor("woT", [PART, ND * 1024], BF16))
                KTb = _f(nc.sbuf_tensor("KTb", [PART, ND * T], BF16))
                V65 = _f(nc.sbuf_tensor("V65", [PART, NTT * NH * 65], BF16))
                probs = _f(nc.sbuf_tensor("probs", [PART, 8 * CHA], BF16))
                qtb = _f(nc.sbuf_tensor("qtb", [PART, ND * CHA], BF16))
                xab = _f(nc.sbuf_tensor("xab", [PART, ND * CHA], BF16))
                outT = _f(nc.sbuf_tensor("outT", [PART, ND * CHA], BF16))
                todd = xab
                rdsb = _f(nc.sbuf_tensor("rdsb", [PART, CHA], F32))
                corr = _f(nc.sbuf_tensor("corr", [PART, ND], F32))
                bvb = _f(nc.sbuf_tensor("bvb", [PART, ND], BF16))

                tbounce = []
                tpc = {}
                for wi, wname in enumerate(("Wq", "Wk", "Wv", "Wo")):
                    tw = None
                    for kt in range(ND):
                        idx = wi * ND + kt
                        half = wpc[:, (idx % 2) * 1024:(idx % 2) * 1024 + 1024]
                        dep = [tpc[idx - 2]] if idx >= 2 else []
                        tcast = ctx.dma("gpsimd", "dg", half,
                                        NM[wname][kt * 128:(kt + 1) * 128, :],
                                        deps=dep)
                        tw = ctx.dma("sync", "dq",
                                     wrow_bf[wi * D + kt * 128:wi * D + (kt + 1) * 128, :],
                                     half, deps=[tcast])
                        tpc[idx] = tw
                    tbounce.append(tw)
                t_bq = load_dcol("bq", NM["bq"])
                t_bk = load_dcol("bk", NM["bk"])
                t_bo = load_dcol("bo", NM["bo"])
                t_bv = ctx.dma("gpsimd", "dg", bvb[:],
                               NM["bv"].rearrange("(s p) -> p s", p=PART),
                               slow=True)

                def wpiece(wi, kt, deps, buf=None):
                    b = wpc if buf is None else buf
                    dst = b[:, kt * 1024:(kt + 1) * 1024]
                    srcp = wrow_bf[wi * D:(wi + 1) * D, kt * 128:(kt + 1) * 128]

                    def body(eng):
                        ctx._waits("sync", eng, deps)
                        eng.dma_start(dst, srcp, transpose=True).then_inc(
                            ctx.sem["dq"], 16)
                    blk.sync(body)
                    ctx.cnt["dq"] += 16
                    return ("dq", ctx.cnt["dq"])

                def wload_full(wi, deps, buf=None):
                    toks = [wpiece(wi, kt, deps if kt == 0 else [], buf=buf)
                            for kt in range(ND)]
                    return toks[-1]

                prev_use = {}
                def proj_dmajor(widx, dstbuf, biasname, extra_dep):
                    toks = []
                    tw = {}
                    for kt in range(ND):
                        dep = [tbounce[widx]] + ([tw[kt - 2]] if kt >= 2 else [])
                        tw[kt] = wpiece(widx, kt, kt % 2, dep)
                    return None
                def proj_full(widx, bias_ap_fn, dst_write, dep0, moving_buf,
                              mtiles=8):
                    twf = wload_full(widx, [tbounce[widx], dep0])
                    tprev_ev = dep0
                    for c in range(T // CHA):
                        t0 = c * CHA
                        tcx = ctx.sec("vector",
                                      lambda v: v.tensor_copy(cmsec(moving_buf, CHA),
                                                              xmsec(t0, CHA)),
                                      deps=[dep0, tprev_ev], tag="ve")

                        def pm(t):
                            for kt in range(ND):
                                for m in range(mtiles):
                                    last = t.matmul(
                                        (PSA if m < 4 else PSB)[:, (m % 4) * 512:(m % 4) * 512 + CHA],
                                        wpc[:, kt * 1024 + m * 128:kt * 1024 + (m + 1) * 128],
                                        csec(moving_buf, kt, CHA),
                                        start=(kt == 0), stop=(kt == ND - 1))
                            return last
                        tmmp = ctx.sec("tensor", pm, deps=[twf, tcx, tprev_ev],
                                       tag="pe")
                        tprev_ev = dst_write(c)
                    return tprev_ev

                def k_write(c):
                    def ev(s):
                        for m in range(ND):
                            ps = (PSA if m < 4 else PSB)[:, (m % 4) * 512:(m % 4) * 512 + CHA]
                            last = s.activation(
                                KTb[:, m * T + c * CHA:m * T + (c + 1) * CHA],
                                ps, AF.Identity,
                                bias=ccol("bk", m), scale=1.0)
                        return last
                    return ctx.sec("scalar", ev, deps=[ctx.tok("pe")], tag="sc")
                tk_done = proj_full(1, None, k_write, t_bk, xab)
                if "ktb" in dbg_out:
                    tk_done = ctx.dma("sync", "dq", dbg_out["ktb"], KTb[:],
                                      deps=[tk_done])

                twvf = wload_full(2, [tbounce[2], tk_done])
                tprev_v = tk_done
                twv = {}
                for c in range(T // CHA):
                    t0 = c * CHA
                    tcx = ctx.sec("vector",
                                  lambda v: v.tensor_copy(cmsec(xab, CHA),
                                                          xmsec(t0, CHA)),
                                  deps=[tprev_v], tag="ve")

                    def vm(t):
                        for kt in range(ND):
                            for ti in range(4):
                                for ec in range(2):
                                    last = t.matmul(
                                        (PSA if ti < 2 else PSB)[:, ((ti % 2) * 2 + ec) * 512:((ti % 2) * 2 + ec) * 512 + 512],
                                        xab[:, kt * CHA + ti * 128:kt * CHA + (ti + 1) * 128],
                                        wpc[:, kt * 1024 + ec * 512:kt * 1024 + (ec + 1) * 512],
                                        start=(kt == 0), stop=(kt == ND - 1))
                        return last
                    twv[c * ND + ND - 1] = ctx.sec("tensor", vm,
                                                   deps=[twvf, tcx], tag="pe")

                    def vev(v, c=c):
                        for ti in range(4):
                            tt = c * 4 + ti
                            for ec in range(2):
                                ps = (PSA if ti < 2 else PSB)[:, ((ti % 2) * 2 + ec) * 512:((ti % 2) * 2 + ec) * 512 + 512]
                                dst = V65[:, tt * NH * 65 + ec * 8 * 65:]
                                dst = dst.rearrange("p (h e) -> p h e", h=8)[:, :, 0:64] if False else V65[:].rearrange(
                                    "p (a h e) -> p a h e", a=NTT, h=NH)[:, tt, ec * 8:(ec + 1) * 8, 0:64]
                                last = v.tensor_copy(dst,
                                                     ps.rearrange("p (h e) -> p h e", h=8))
                        return last
                    tprev_v = ctx.sec("vector", vev, deps=[twv[c * ND + ND - 1]],
                                      tag="ve")
                tones = ctx.sec("vector",
                                lambda v: v.memset(
                                    V65[:].rearrange("p (a h e) -> p a h e",
                                                     a=NTT, h=NH)[:, :, :, 64:65],
                                    1.0),
                                deps=[], tag="ve")

                twoT = wload_full(3, [tbounce[3], tprev_v], buf=woT)
                tzrow = ctx.sec("vector",
                                lambda v: v.memset(ROWS[0:1, 1040:1048], 0.0),
                                deps=[], tag="ve")
                tprime = ctx.sec("tensor",
                                 lambda t: t.matmul(PSB[:, 1536:1536 + ND],
                                                    CROWF[:],
                                                    ROWS[0:1, 1040:1048],
                                                    start=True, stop=False),
                                 deps=[tzrow, tprev_v], tag="pe")

                def cm(t):
                    for kt in range(ND):
                        for m in range(ND):
                            last = t.matmul(PSB[0:PART, 1536 + m:1536 + m + 1],
                                            woT[:, kt * 1024 + m * 128:kt * 1024 + (m + 1) * 128][:, 0:1] if False else woT[:, kt * 1024 + m * 128:kt * 1024 + (m + 1) * 128],
                                            bvb[:, kt:kt + 1],
                                            start=False, stop=(kt == ND - 1))
                    return last
                tcorr_mm = ctx.sec("tensor", cm, deps=[twoT, tprime, t_bv],
                                   tag="pe")
                tcorr = ctx.sec("vector",
                                lambda v: v.tensor_copy(corr[:],
                                                        PSB[:, 1536:1536 + ND]),
                                deps=[tcorr_mm], tag="ve")
                tcorr2 = ctx.sec("vector",
                                 lambda v: [v.tensor_add(corr[:, m:m + 1],
                                                         corr[:, m:m + 1],
                                                         ccol("bo", m))
                                            for m in range(ND)][-1],
                                 deps=[tcorr, t_bo], tag="ve")

                if "corr" in dbg_out:
                    tcorr2 = ctx.dma("sync", "dq", dbg_out["corr"], corr[:],
                                     deps=[tcorr2])
                twqf = wload_full(0, [tbounce[0], tcorr2])
                prev_main = tcorr2
                for c in range(T // CHA):
                    t0 = c * CHA
                    tcq = ctx.sec("vector",
                                  lambda v: v.tensor_copy(cmsec(xab, CHA),
                                                          xmsec(t0, CHA)),
                                  deps=[prev_main], tag="ve")
                    def qm(t):
                        for kt in range(ND):
                            for m in range(ND):
                                last = t.matmul(
                                    (PSA if m < 4 else PSB)[:, (m % 4) * 512:(m % 4) * 512 + CHA],
                                    wpc[:, kt * 1024 + m * 128:kt * 1024 + (m + 1) * 128],
                                    csec(xab, kt, CHA),
                                    start=(kt == 0), stop=(kt == ND - 1))
                        return last
                    twq = {ND - 1: ctx.sec("tensor", qm, deps=[twqf, tcq], tag="pe")}
                    tqe = ctx.sec("scalar",
                                  lambda s: [s.activation(
                                      csec(qtb, m, CHA),
                                      (PSA if m < 4 else PSB)[:, (m % 4) * 512:(m % 4) * 512 + CHA],
                                      AF.Identity, bias=ccol("bq", m), scale=1.0)
                                      for m in range(ND)][-1],
                                  deps=[twq[ND - 1], t_bq], tag="sc")
                    if c == 0 and "qtb" in dbg_out:
                        tqe = ctx.dma("sync", "dq", dbg_out["qtb"], qtb[:],
                                      deps=[tqe])

                    ev_odd = []
                    tprev_pv = tqe
                    tprev_exp = tqe
                    tprev_cpd = [None]
                    fin_par = {0: tqe, 1: tqe}
                    pv_done = {0: None, 1: None}
                    sc_steps = []
                    for h in range(NH):
                        sec_ = h // 2
                        pb = 64 * (h % 2)
                        pvb = (h % 2) * 512
                        tgrp = {}
                        tpv = None
                        for grp in range(4):
                            def sm(t, grp=grp, sec_=sec_, pb=pb):
                                for i in range(4):
                                    kt = grp * 4 + i
                                    last = t.matmul(
                                        PSA[:, i * 512:i * 512 + CHA],
                                        KTb[pb:pb + 64, sec_ * T + kt * 128:sec_ * T + (kt + 1) * 128],
                                        qtb[pb:pb + 64, sec_ * CHA:(sec_ + 1) * CHA],
                                        start=True, stop=True)
                                return last
                            dep_s = [tqe, tk_done] + ([tgrp[grp - 1]] if grp else [tprev_exp])
                            if grp == 3:
                                dep_s.append(tprev_cpd[0])
                            tsg = ctx.sec("tensor", sm, deps=dep_s, tag="pe")
                            texp = ctx.reserve("sc")
                            sc_steps.append(([tsg] + ([pv_done[grp % 2]] if pv_done[grp % 2] else []),
                                             (lambda s, grp=grp: s.activation(
                                                 probs[:, (grp % 2) * 4 * CHA:((grp % 2) + 1) * 4 * CHA],
                                                 PSA[:], AF.Exp, bias=0.0,
                                                 scale=0.125)), texp))
                            tgrp[grp] = texp
                            def pvg(t, h=h, pvb=pvb, grp=grp):
                                for i in range(4):
                                    kt = grp * 4 + i
                                    last = t.matmul(
                                        PSB[0:65, pvb:pvb + CHA],
                                        V65[:, kt * NH * 65 + h * 65:kt * NH * 65 + (h + 1) * 65],
                                        probs[:, ((grp % 2) * 4 + i) * CHA:((grp % 2) * 4 + i + 1) * CHA],
                                        start=(grp == 0 and i == 0),
                                        stop=(grp == 3 and i == 3))
                                return last
                            dpv = [texp, tones, tprev_v]
                            if grp == 0:
                                dpv.append(fin_par[h % 2])
                            if tpv is not None:
                                dpv.append(tpv)
                            tpv = ctx.sec("tensor", pvg, deps=dpv, tag="pe")
                            pv_done[grp % 2] = tpv
                        tprev_exp = tgrp[3]
                        r_rd = ROWS[0:1, 512:512 + CHA]
                        trd2 = ctx.reserve("sc")
                        def rden_fn(s, r_rd=r_rd, pvb=pvb):
                            s.activation(r_rd, PSB[64:65, pvb:pvb + CHA], AF.Ln,
                                         bias=0.0, scale=1.0)
                            return s.activation(r_rd, r_rd, AF.Exp, bias=0.0,
                                                scale=-1.0)
                        sc_steps.append(([tpv], rden_fn, trd2))
                        tbcd = ctx.sec("tensor",
                                       lambda t: t.matmul(PSA[0:64, 1536:1536 + CHA],
                                                          CROWF[0:1, 0:64], r_rd,
                                                          start=True, stop=True),
                                       deps=[trd2, tprev_cpd[0]], tag="pe")
                        tcpd = ctx.sec("vector",
                                       lambda v: v.tensor_copy(rdsb[0:64, :],
                                                               PSA[0:64, 1536:1536 + CHA]),
                                       deps=[tbcd], tag="ve")
                        tprev_cpd[0] = tcpd
                        if h % 2 == 0:
                            def fin(v, sec_=sec_, pvb=pvb):
                                return v.tensor_mul(
                                    outT[0:64, sec_ * CHA:(sec_ + 1) * CHA],
                                    PSB[0:64, pvb:pvb + CHA], rdsb[0:64, :])
                        else:
                            def fin(v, sec_=sec_, pvb=pvb):
                                return v.tensor_mul(
                                    todd[0:64, sec_ * CHA:(sec_ + 1) * CHA],
                                    PSB[0:64, pvb:pvb + CHA], rdsb[0:64, :])
                        tprev_pv = ctx.sec("vector", fin, deps=[tcpd, tpv],
                                           tag="ve")
                        fin_par[h % 2] = tprev_pv
                        if h % 2 == 1:
                            ev_odd.append(tprev_pv)
                    ctx.mseq("scalar", sc_steps)
                    tmv_odd = ctx.dma("gpsimd", "dg",
                                      outT[64:128, :].rearrange("p (s c) -> p s c", s=ND),
                                      todd[0:64, :].rearrange("p (s c) -> p s c", s=ND),
                                      deps=ev_odd)
                    if c == 0 and "outT" in dbg_out:
                        tmv_odd = ctx.dma("sync", "dq", dbg_out["outT"], outT[:],
                                          deps=[tmv_odd, tprev_pv])
                    def om(t):
                        for kt in range(ND):
                            for m in range(ND):
                                last = t.matmul(
                                    (PSA if m < 4 else PSB)[:, (m % 4) * 512:(m % 4) * 512 + CHA],
                                    woT[:, kt * 1024 + m * 128:kt * 1024 + (m + 1) * 128],
                                    csec(outT, kt, CHA),
                                    start=(kt == 0), stop=(kt == ND - 1))
                        return last
                    two = {ND - 1: ctx.sec("tensor", om,
                                           deps=[twoT, tmv_odd, tprev_pv],
                                           tag="pe")}
                    tfin = ctx.sec("vector",
                                   lambda v: [v.scalar_tensor_tensor(
                                       xsec(m, t0, CHA),
                                       (PSA if m < 4 else PSB)[:, (m % 4) * 512:(m % 4) * 512 + CHA],
                                       corr[:, m:m + 1], xsec(m, t0, CHA),
                                       ALU.add, ALU.add) for m in range(ND)][-1],
                                   deps=[two[ND - 1], tcorr2], tag="ve")
                    tbca, rs_ba, mrs_ba, _tca = ln_chunk([tfin], xmsec(t0, CHA),
                                                         CHA, xab, qtb, CHA)
                    def apa(v):
                        for dt in range(ND):
                            v.tensor_mul(xsec(dt, t0, CHA), xsec(dt, t0, CHA),
                                         rs_ba)
                        for dt in range(ND):
                            last = v.tensor_sub(xsec(dt, t0, CHA),
                                                xsec(dt, t0, CHA), mrs_ba)
                        return last
                    prev_main = ctx.sec("vector", apa, deps=[tbca, tfin], tag="ve")
            ctx.barrier()


        if 4 in stages:
            CH4 = 512
            NHL = 8
            HIDT = 22
            with contextlib.ExitStack() as _s4a:
                _f = _s4a.enter_context
                qsc = _f(nc.sbuf_tensor("qsc", [PART, 4], F32))
                mP1 = _f(nc.sbuf_tensor("mP1", [PART, PART], BF16))
                mP2 = _f(nc.sbuf_tensor("mP2", [PART, PART], BF16))
                bndX = _f(nc.sbuf_tensor("bndX", [PART, ND], F32))

                tmsk = [ctx.dma("sync", "dq", mP1[:], NM["c_maskP1"]),
                        ctx.dma("sync", "dq", mP2[:], NM["c_maskP2"])]

                prev_l = None
                _layers = [0] if sub4 in ("a0", "f0") else [0, 1]
                for l in _layers:
                    _satt = contextlib.ExitStack()
                    def _ab(name, shape, dt_):
                        return _satt.enter_context(
                            nc.sbuf_tensor(name + "_%d" % l, shape, dt_))
                    wqkv_sb = _ab("wqkv_sb", [PART, ND * 1536], BF16)
                    wol_sb = _ab("wol_sb", [PART, 4 * D], BF16)
                    Qf = _ab("Qf", [PART, 4 * CH4], F32)
                    Kf = _ab("Kf", [PART, 4 * CH4], F32)
                    Qsw = _ab("Qsw", [PART, 4 * CH4], F32)
                    QTr = _ab("QTr", [PART, 4 * CH4], BF16)
                    KTr = _ab("KTr", [PART, 4 * T], BF16)
                    V65L = _ab("V65L", [PART, NTT * NHL * 65], BF16)
                    hbf4 = _ab("hbf4", [PART, ND * CH4], BF16)
                    probsL = _ab("probsL", [PART, 5 * 128], BF16)
                    outTL = _ab("outTL", [PART, 4 * CH4], BF16)
                    toddL = _ab("toddL", [PART, 4 * CH4], BF16)
                    rdsbL = _ab("rdsbL", [PART, CH4], F32)
                    tbl = _ab("tbl", [PART, 4 * CH4], F32)
                    tones4 = ctx.sec("vector",
                                     lambda v: v.memset(
                                         V65L[:].rearrange("p (a h e) -> p a h e",
                                                           a=NTT, h=NHL)[:, :, :, 64:65],
                                         1.0),
                                     deps=[prev_l] if prev_l else [], tag="ve")
                    twq4 = ctx.dma("gpsimd", "dg",
                                   wqkv_sb[:].rearrange("p (s o) -> p s o", s=ND),
                                   NM["wqkv"][l].rearrange("(s p) o -> p s o", p=PART),
                                   deps=[prev_l] if prev_l else [])
                    two4 = ctx.dma("gpsimd", "dg",
                                   wol_sb[:].rearrange("p (s o) -> p s o", s=4),
                                   NM["l1wo"][l].rearrange("(s p) o -> p s o", p=PART),
                                   deps=[prev_l] if prev_l else [])
                    tql = []
                    for nm, col in (("q_scale", 0), ("k_scale", 2)):
                        for half in range(2):
                            tql.append(ctx.dma("sync", "dq",
                                               qsc[half * 64:(half + 1) * 64, col:col + 1],
                                               NM[nm][l:l + 1, :].rearrange("a b -> b a"),
                                               deps=[prev_l] if prev_l else [],
                                               slow=True))
                        for qtr in range(4):
                            s0q = 32 if qtr % 2 == 0 else 0
                            tql.append(ctx.dma("sync", "dq",
                                               qsc[qtr * 32:(qtr + 1) * 32, col + 1:col + 2],
                                               NM[nm][l:l + 1, s0q:s0q + 32].rearrange("a b -> b a"),
                                               deps=[prev_l] if prev_l else [],
                                               slow=True))
                    tlg = load_dcol("lng_%d" % l, NM["ln_g"][l])
                    tlb = load_dcol("lnb_%d" % l, NM["ln_b"][l])
                    tb1 = load_dcol("b1p_%d" % l, NM["l1b1"][l], 44)
                    tb2 = load_dcol("b2p_%d" % l, NM["l1b2"][l])
                    twl = [twq4, two4, tlg, tlb, tb1, tb2] + tql + tmsk

                    prev_ch = prev_l
                    for c in range(4):
                        t0 = c * CH4
                        dd4 = twl + ([prev_ch] if prev_ch else [])
                        def shcast(v):
                            r = hbf4[:].rearrange("p (s c) -> p s c", s=ND)
                            v.tensor_copy(r[:, 0:4, :], xmsec(t0, CH4)[:, 0:4, :])
                            last = v.tensor_copy(r[:, 4:8, :],
                                                 xmsec(t0, CH4, shift=1)[:, 4:8, :])
                            if c > 0:
                                last = v.tensor_copy(r[:, 4:8, 0:1].squeeze(),
                                                     bndX[:, 4:8])
                            return last
                        thc = ctx.sec("vector", shcast, deps=dd4, tag="ve")
                        def qkmm(t):
                            for m in range(8):
                                for kt in range(ND):
                                    last = t.matmul(
                                        (PSA if m < 4 else PSB)[:, (m % 4) * 512:(m % 4) * 512 + CH4],
                                        wqkv_sb[:, kt * 1536 + m * 128:kt * 1536 + (m + 1) * 128],
                                        csec(hbf4, kt, CH4),
                                        start=(kt == 0), stop=(kt == ND - 1))
                            return last
                        tqk = ctx.sec("tensor", qkmm, deps=[thc] + dd4, tag="pe")
                        tqke = ctx.sec("vector",
                                       lambda v: [v.tensor_copy(
                                           (Qf if m < 4 else Kf)[:, (m % 4) * CH4:(m % 4 + 1) * CH4],
                                           (PSA if m < 4 else PSB)[:, (m % 4) * 512:(m % 4) * 512 + CH4])
                                           for m in range(8)][-1],
                                       deps=[tqk], tag="ve")
                        def vmm(t):
                            for ti in range(4):
                                for kt in range(ND):
                                    last = t.matmul(
                                        PSA[:, ti * 512:ti * 512 + 512],
                                        hbf4[:, kt * CH4 + ti * 128:kt * CH4 + (ti + 1) * 128],
                                        wqkv_sb[:, kt * 1536 + 1024:kt * 1536 + 1536],
                                        start=(kt == 0), stop=(kt == ND - 1))
                            return last
                        tvm = ctx.sec("tensor", vmm, deps=[tqke], tag="pe")
                        def vev4(v):
                            for ti in range(4):
                                tt = c * 4 + ti
                                dst = V65L[:].rearrange("p (a h e) -> p a h e",
                                                        a=NTT, h=NHL)[:, tt, :, 0:64]
                                last = v.tensor_copy(
                                    dst, PSA[:, ti * 512:ti * 512 + 512].rearrange("p (h e) -> p h e", h=NHL))
                            return last
                        tve4 = ctx.sec("vector", vev4, deps=[tvm], tag="ve")

                        def ss_rn(buf, dep_in):
                            tsqr = ctx.sec("vector",
                                           lambda v: v.tensor_mul(Qsw[:], buf[:],
                                                                  buf[:]),
                                           deps=[dep_in], tag="ve")
                            tss_prev = tsqr
                            for dt in range(4):
                                bk = (dt % 2)
                                tssm = ctx.sec("tensor",
                                               lambda t, dt=dt, bk=bk: t.matmul(
                                                   PSB[0:2, bk * 512:bk * 512 + CH4],
                                                   E2S[:], Qsw[:, dt * CH4:(dt + 1) * CH4],
                                                   start=True, stop=True),
                                               deps=[tss_prev, tve4], tag="pe")
                                tss_prev = ctx.sec("vector",
                                                   lambda v, dt=dt, bk=bk: v.tensor_copy(
                                                       ROWS[0:2, dt * CH4:(dt + 1) * CH4],
                                                       PSB[0:2, bk * 512:bk * 512 + CH4]),
                                                   deps=[tssm], tag="ve")
                            tmx4 = ctx.sec("vector",
                                           lambda v: v.tensor_scalar(
                                               ROWS[0:2, 0:2048], ROWS[0:2, 0:2048],
                                               1e-24, None, ALU.max),
                                           deps=[tss_prev], tag="ve")
                            def lnexp(s):
                                s.activation(ROWS[0:2, 0:2048], ROWS[0:2, 0:2048],
                                             AF.Ln, bias=0.0, scale=1.0)
                                return s.activation(ROWS[0:2, 0:2048],
                                                    ROWS[0:2, 0:2048],
                                                    AF.Exp, bias=0.0, scale=-0.5)
                            return ctx.sec("scalar", lnexp, deps=[tmx4], tag="sc")
                        tex4 = ss_rn(Qf, tqke)

                        def rot(tensor_j, srcf, swp, dstb, t1nm, t2nm, scol,
                                dep_in):
                            tt1 = ctx.dma("sync", "dq", tbl[:, 0:CH4],
                                          NM[t1nm][:, t0:t0 + CH4], deps=dep_in)
                            tt2 = ctx.dma("sync", "dq", tbl[:, CH4:2 * CH4],
                                          NM[t2nm][:, t0:t0 + CH4], deps=dep_in)
                            te1 = ctx.sec("vector",
                                          lambda v: v.tensor_scalar(
                                              tbl[:, 2 * CH4:3 * CH4],
                                              tbl[:, 0:CH4],
                                              qsc[:, scol:scol + 1], None,
                                              ALU.mult),
                                          deps=[tt1] + tql, tag="ve")
                            te2 = ctx.sec("vector",
                                          lambda v: v.tensor_scalar(
                                              tbl[:, 3 * CH4:4 * CH4],
                                              tbl[:, CH4:2 * CH4],
                                              qsc[:, scol + 1:scol + 2], None,
                                              ALU.mult),
                                          deps=[tt2] + tql, tag="ve")
                            tsw = []
                            for bb, (d0, s0) in enumerate([(0, 32), (32, 0),
                                                           (64, 96), (96, 64)]):
                                tsw.append(ctx.dma("gpsimd", "dg",
                                                   swp[d0:d0 + 32, :],
                                                   srcf[s0:s0 + 32, :],
                                                   deps=dep_in))
                            t1b = tbl[:, 2 * CH4:3 * CH4].unsqueeze(1).broadcast_to([PART, 4, CH4])
                            t2b = tbl[:, 3 * CH4:4 * CH4].unsqueeze(1).broadcast_to([PART, 4, CH4])
                            ta = ctx.sec("vector",
                                         lambda v: v.tensor_mul(
                                             srcf[:].rearrange("p (s c) -> p s c", s=4),
                                             srcf[:].rearrange("p (s c) -> p s c", s=4),
                                             t1b),
                                         deps=[te1] + tsw, tag="ve")
                            tb_ = ctx.sec("gpsimd",
                                          lambda g: g.tensor_mul(
                                              swp[:].rearrange("p (s c) -> p s c", s=4),
                                              swp[:].rearrange("p (s c) -> p s c", s=4),
                                              t2b),
                                          deps=[te2] + tsw, tag="gp")
                            tc_ = ctx.sec("vector",
                                          lambda v: v.tensor_add(
                                              srcf[:], srcf[:], swp[:]),
                                          deps=[ta, tb_], tag="ve")
                            tprev = tc_
                            for dt in range(4):
                                bk2 = 2 + (dt % 2)
                                rowb = 0 if tensor_j == 0 else 32
                                tbc4 = ctx.sec("tensor",
                                               lambda t, dt=dt, bk2=bk2, rowb=rowb: t.matmul(
                                                   PSB[:, bk2 * 512:bk2 * 512 + CH4],
                                                   E2XS[:],
                                                   ROWS[rowb:rowb + 2, dt * CH4:(dt + 1) * CH4],
                                                   start=True, stop=True),
                                               deps=[tex4, tprev], tag="pe")
                                tprev = ctx.sec("vector",
                                                lambda v, dt=dt, bk2=bk2: v.tensor_mul(
                                                    dstb[:, dt * CH4:(dt + 1) * CH4],
                                                    srcf[:, dt * CH4:(dt + 1) * CH4],
                                                    PSB[:, bk2 * 512:bk2 * 512 + CH4]),
                                                deps=[tbc4], tag="ve")
                            return tprev

                        trq = rot(0, Qf, Qsw, QTr, "c_T1Q", "c_T2Q", 0,
                                  [tqke, tex4])
                        if l == 0 and c == 0 and "qtr4" in dbg_out:
                            trq = ctx.dma("sync", "dq", dbg_out["qtr4"], QTr[:],
                                          deps=[trq])
                        tex4k = ss_rn(Kf, trq)
                        tt1k = ctx.dma("sync", "dq", tbl[:, 0:CH4],
                                       NM["c_T1K"][:, t0:t0 + CH4], deps=[trq])
                        tt2k = ctx.dma("sync", "dq", tbl[:, CH4:2 * CH4],
                                       NM["c_T2K"][:, t0:t0 + CH4], deps=[trq])
                        te1k = ctx.sec("vector",
                                       lambda v: v.tensor_scalar(
                                           tbl[:, 2 * CH4:3 * CH4], tbl[:, 0:CH4],
                                           qsc[:, 2:3], None, ALU.mult),
                                       deps=[tt1k] + tql, tag="ve")
                        te2k = ctx.sec("vector",
                                       lambda v: v.tensor_scalar(
                                           tbl[:, 3 * CH4:4 * CH4],
                                           tbl[:, CH4:2 * CH4],
                                           qsc[:, 3:4], None, ALU.mult),
                                       deps=[tt2k] + tql, tag="ve")
                        tswk = []
                        for bb, (d0, s0) in enumerate([(0, 32), (32, 0),
                                                       (64, 96), (96, 64)]):
                            tswk.append(ctx.dma("gpsimd", "dg",
                                                Qsw[d0:d0 + 32, :],
                                                Kf[s0:s0 + 32, :],
                                                deps=[tex4k]))
                        t1bk = tbl[:, 2 * CH4:3 * CH4].unsqueeze(1).broadcast_to([PART, 4, CH4])
                        t2bk = tbl[:, 3 * CH4:4 * CH4].unsqueeze(1).broadcast_to([PART, 4, CH4])
                        tak = ctx.sec("vector",
                                      lambda v: v.tensor_mul(
                                          Kf[:].rearrange("p (s c) -> p s c", s=4),
                                          Kf[:].rearrange("p (s c) -> p s c", s=4),
                                          t1bk),
                                      deps=[te1k] + tswk, tag="ve")
                        tbk_ = ctx.sec("gpsimd",
                                       lambda g: g.tensor_mul(
                                           Qsw[:].rearrange("p (s c) -> p s c", s=4),
                                           Qsw[:].rearrange("p (s c) -> p s c", s=4),
                                           t2bk),
                                       deps=[te2k] + tswk, tag="gp")
                        tck = ctx.sec("vector",
                                      lambda v: v.tensor_add(Kf[:], Kf[:], Qsw[:]),
                                      deps=[tak, tbk_], tag="ve")
                        tprevk = tck
                        for dt in range(4):
                            bk2 = 2 + (dt % 2)
                            tbc4k = ctx.sec("tensor",
                                            lambda t, dt=dt, bk2=bk2: t.matmul(
                                                PSB[:, bk2 * 512:bk2 * 512 + CH4],
                                                E2XS[:],
                                                ROWS[0:2, dt * CH4:(dt + 1) * CH4],
                                                start=True, stop=True),
                                            deps=[tex4k, tprevk], tag="pe")
                            tprevk = ctx.sec("vector",
                                             lambda v, dt=dt, bk2=bk2: v.tensor_mul(
                                                 KTr[:, dt * T + t0:dt * T + t0 + CH4],
                                                 Kf[:, dt * CH4:(dt + 1) * CH4],
                                                 PSB[:, bk2 * 512:bk2 * 512 + CH4]),
                                             deps=[tbc4k], tag="ve")

                        if l == 0 and c == 0 and "ktr4" in dbg_out:
                            tprevk = ctx.dma("sync", "dq", dbg_out["ktr4"],
                                             KTr[:], deps=[tprevk])
                        tprev_exp4 = tprevk
                        tprev_fin4 = trq
                        ev_odd4 = []
                        sc4_steps = []
                        for h in range(NHL):
                            sec4 = h // 2
                            pb4 = 64 * (h % 2)
                            tpv4 = None
                            for qt in range(4):
                                a = c * 4 + qt
                                kts = list(range(max(0, a - 4), a + 1))
                                def sm4(t, a=a, kts=kts, sec4=sec4, pb4=pb4, qt=qt):
                                    for i, kt in enumerate(kts):
                                        last = t.matmul(
                                            PSA[:, i * 128:(i + 1) * 128],
                                            KTr[pb4:pb4 + 64, sec4 * T + kt * 128:sec4 * T + (kt + 1) * 128],
                                            QTr[pb4:pb4 + 64, sec4 * CH4 + qt * 128:sec4 * CH4 + (qt + 1) * 128],
                                            start=True, stop=True)
                                    return last
                                dep_s4 = [tprevk, trq] + ([tprev_exp4] if tprev_exp4 else []) + ([tprev_fin4] if tprev_fin4 else [])
                                tsg4 = ctx.sec("tensor", sm4, deps=dep_s4, tag="pe")
                                nk = len(kts)
                                texp4 = ctx.reserve("sc")
                                sc4_steps.append(([tsg4],
                                                  (lambda s, nk=nk: s.activation(
                                                      probsL[:, 0:nk * 128],
                                                      PSA[:, 0:nk * 128], AF.Exp,
                                                      bias=0.0, scale=8.0)),
                                                  texp4))
                                def msk(v, nk=nk, a=a):
                                    first = None
                                    if a >= 4:
                                        first = v.tensor_mul(probsL[:, 0:128],
                                                             probsL[:, 0:128],
                                                             mP2[:])
                                    last = v.tensor_mul(
                                        probsL[:, (nk - 1) * 128:nk * 128],
                                        probsL[:, (nk - 1) * 128:nk * 128],
                                        mP1[:])
                                    return last
                                tmk4 = ctx.sec("vector", msk,
                                               deps=[texp4] + tmsk, tag="ve")
                                def pv4(t, kts=kts, h=h, qt=qt):
                                    for i, kt in enumerate(kts):
                                        last = t.matmul(
                                            PSB[0:65, qt * 512:qt * 512 + 128],
                                            V65L[:, kt * NHL * 65 + h * 65:kt * NHL * 65 + (h + 1) * 65],
                                            probsL[:, i * 128:(i + 1) * 128],
                                            start=(i == 0), stop=(i == len(kts) - 1))
                                    return last
                                tpv4 = ctx.sec("tensor", pv4,
                                               deps=[tmk4, tve4, tones4] + ([tpv4] if tpv4 else []),
                                               tag="pe")
                                tprev_exp4 = tmk4
                            r_rd4 = ROWS[0:1, 0:CH4]
                            den_ap = PSB[64:65, 0:2048].rearrange("p (b c) -> p b c", b=4)[:, :, 0:128]
                            trdb = ctx.reserve("sc")
                            def rden4(s, r_rd4=r_rd4, den_ap=den_ap):
                                s.activation(r_rd4.rearrange("p (b c) -> p b c", b=4),
                                             den_ap, AF.Ln, bias=0.0, scale=1.0)
                                return s.activation(r_rd4, r_rd4, AF.Exp,
                                                    bias=0.0, scale=-1.0)
                            sc4_steps.append(([tpv4], rden4, trdb))
                            tbcd4 = ctx.sec("tensor",
                                            lambda t, r_rd4=r_rd4: t.matmul(
                                                PSA[0:64, 1536:1536 + CH4],
                                                CROWF[0:1, 0:64], r_rd4,
                                                start=True, stop=True),
                                            deps=[trdb, tprev_fin4], tag="pe")
                            tcpd4 = ctx.sec("vector",
                                            lambda v: v.tensor_copy(
                                                rdsbL[0:64, :],
                                                PSA[0:64, 1536:1536 + CH4]),
                                            deps=[tbcd4], tag="ve")
                            pv_str = PSB[0:64, 0:2048].rearrange("p (b c) -> p b c", b=4)[:, :, 0:128]
                            if h % 2 == 0:
                                def fin4(v, sec4=sec4, pv_str=pv_str):
                                    return v.tensor_mul(
                                        outTL[0:64, sec4 * CH4:(sec4 + 1) * CH4].rearrange("p (b c) -> p b c", b=4),
                                        pv_str, rdsbL[0:64, :].rearrange("p (b c) -> p b c", b=4))
                            else:
                                def fin4(v, sec4=sec4, pv_str=pv_str):
                                    return v.tensor_mul(
                                        toddL[0:64, sec4 * CH4:(sec4 + 1) * CH4].rearrange("p (b c) -> p b c", b=4),
                                        pv_str, rdsbL[0:64, :].rearrange("p (b c) -> p b c", b=4))
                            tprev_fin4 = ctx.sec("vector", fin4,
                                                 deps=[tcpd4, tpv4], tag="ve")
                            if h % 2 == 1:
                                ev_odd4.append(tprev_fin4)
                        ctx.mseq("scalar", sc4_steps)
                        todd4 = ctx.dma("gpsimd", "dg",
                                        outTL[64:128, :].rearrange("p (s c) -> p s c", s=4),
                                        toddL[0:64, :].rearrange("p (s c) -> p s c", s=4),
                                        deps=ev_odd4)
                        tbsx = ctx.sec("vector",
                                       lambda v: v.tensor_copy(
                                           bndX[:, 0:ND],
                                           xmsec(t0, CH4)[:, :, CH4 - 1:CH4].squeeze()),
                                       deps=[], tag="ve")
                        tsp4 = None
                        if l == 0:
                            tsp4 = ctx.dma("gpsimd", "dg",
                                           x0_spill[:, c * ND * CH4:(c + 1) * ND * CH4].rearrange("p (s c) -> p s c", s=ND),
                                           xmsec(t0, CH4), deps=[tbsx])
                        tfin_ev = None
                        for rnd in range(2):
                            def wom(t, rnd=rnd):
                                for mi in range(4):
                                    m = rnd * 4 + mi
                                    for kt in range(4):
                                        last = t.matmul(
                                            PSA[:, mi * 512:mi * 512 + CH4],
                                            wol_sb[:, kt * D + m * 128:kt * D + (m + 1) * 128],
                                            csec(outTL, kt, CH4),
                                            start=(kt == 0), stop=(kt == 3))
                                return last
                            twom = ctx.sec("tensor", wom,
                                           deps=[todd4, tprev_fin4] +
                                                ([tfin_ev] if tfin_ev else []),
                                           tag="pe")
                            def wev(v, rnd=rnd):
                                for mi in range(4):
                                    m = rnd * 4 + mi
                                    last = v.tensor_add(xsec(m, t0, CH4),
                                                        xsec(m, t0, CH4),
                                                        PSA[:, mi * 512:mi * 512 + CH4])
                                return last
                            tfin_ev = ctx.sec("vector", wev,
                                              deps=[twom, tbsx] +
                                                   ([tsp4] if tsp4 else []),
                                              tag="ve")
                        prev_ch = tfin_ev
                    ctx.barrier()
                    _satt.close()
                    if sub4 == "a0" or (sub4 == "a1" and l == 1):
                        prev_l = prev_ch
                        continue
                    with contextlib.ExitStack() as _s4f:
                        _g = _s4f.enter_context
                        mixL = _g(nc.sbuf_tensor("mixL_%d" % l, [PART, ND * 1024], BF16))
                        prodb = _g(nc.sbuf_tensor("prodb_%d" % l, [PART, HIDT * 1024], BF16))
                        wstr4 = _g(nc.sbuf_tensor("wstr4_%d" % l, [PART, 16384], BF16))
                        ggsb = _g(nc.sbuf_tensor("ggsb_%d" % l, [PART, 1024], F32))
                        hbf2 = _g(nc.sbuf_tensor("hbf2_%d" % l, [PART, ND * 1024], BF16))
                        CHF4 = 1024
                        prev_f = prev_ch
                        for c in range(2):
                            t0 = c * CHF4
                            def shcast2(v, c=c):
                                r = hbf2[:].rearrange("p (s c) -> p s c", s=ND)
                                v.tensor_copy(r[:, 0:4, :],
                                              xmsec(t0, CHF4)[:, 0:4, :])
                                last = v.tensor_copy(r[:, 4:8, :],
                                                     xmsec(t0, CHF4, shift=1)[:, 4:8, :])
                                if c > 0:
                                    last = v.tensor_copy(r[:, 4:8, 0:1].squeeze(),
                                                         bndX[:, 4:8])
                                return last
                            thc2 = ctx.sec("vector", shcast2,
                                           deps=[prev_f], tag="ve")
                            tlnap = thc2
                            for hh in range(2):
                                th0 = t0 + hh * 512
                                hb3 = hbf2[:].rearrange("p (s c) -> p s c", s=ND)[:, :, hh * 512:(hh + 1) * 512]
                                tsq2 = ctx.sec("vector",
                                               lambda v, hb3=hb3: v.tensor_mul(
                                                   cmsec(mixL, 1024)[:, :, hh * 512:(hh + 1) * 512] if False else mixL[:].rearrange("p (s c) -> p s c", s=ND)[:, :, hh * 512:(hh + 1) * 512],
                                                   hb3, hb3),
                                               deps=[thc2, tlnap], tag="ve")
                                def stm2(t, hh=hh, hb3=hb3):
                                    for dt in range(ND):
                                        t.matmul(PSB[0:1, 0:512], CBF[:, 0:1],
                                                 hbf2[:, dt * CHF4 + hh * 512:dt * CHF4 + (hh + 1) * 512],
                                                 start=(dt == 0), stop=(dt == ND - 1))
                                    for dt in range(ND):
                                        last = t.matmul(
                                            PSB[0:1, 512:1024], CBF[:, 0:1],
                                            mixL[:, dt * 1024 + hh * 512:dt * 1024 + (hh + 1) * 512],
                                            start=(dt == 0), stop=(dt == ND - 1))
                                    return last
                                tst2 = ctx.sec("tensor", stm2,
                                               deps=[tsq2], tag="pe")
                                r_m4, r_rs4, r_mrs4 = rows3(512)
                                def rowv4(v):
                                    v.tensor_scalar(r_m4, PSB[0:1, 0:512],
                                                    1.0 / D, None, ALU.mult)
                                    v.tensor_mul(r_rs4, r_m4, r_m4)
                                    return v.scalar_tensor_tensor(
                                        r_mrs4, PSB[0:1, 512:1024], 1.0 / D,
                                        r_rs4, ALU.mult, ALU.subtract)
                                t34 = ctx.sec("vector", rowv4, deps=[tst2],
                                              tag="ve")
                                def rowsc4(s):
                                    s.activation(r_rs4, r_mrs4, AF.Ln,
                                                 bias=eps_ap, scale=1.0)
                                    return s.activation(r_rs4, r_rs4, AF.Exp,
                                                        bias=0.0, scale=-0.5)
                                t54 = ctx.sec("scalar", rowsc4,
                                              deps=[t34, t_eps], tag="sc")
                                t64 = ctx.sec("vector",
                                              lambda v: v.tensor_mul(r_mrs4, r_m4,
                                                                     r_rs4),
                                              deps=[t54], tag="ve")
                                tbc44 = ctx.sec("tensor",
                                                lambda t: [t.matmul(PSB[:, 512:1024],
                                                                    CROWF[:], r_rs4,
                                                                    start=True,
                                                                    stop=True),
                                                           t.matmul(PSB[:, 1024:1536],
                                                                    CROWF[:], r_mrs4,
                                                                    start=True,
                                                                    stop=True)][-1],
                                                deps=[t54, t64], tag="pe")
                                def lap1(v, hh=hh):
                                    for dt in range(ND):
                                        src = xsec(dt, th0, 512, shift=(1 if dt >= 4 else 0))
                                        v.scalar_tensor_tensor(
                                            mixL[:, dt * 1024 + hh * 512:dt * 1024 + (hh + 1) * 512] if False else ggsb[:, 0:512],
                                            src, ccol("lng_%d" % l, dt),
                                            PSB[:, 512:1024], ALU.mult, ALU.mult)
                                        last = v.scalar_tensor_tensor(
                                            ggsb[:, 0:512], PSB[:, 1024:1536],
                                            ccol("lng_%d" % l, dt), ggsb[:, 0:512],
                                            ALU.mult, ALU.subtract)
                                        last = v.tensor_scalar(
                                            ggsb[:, 0:512], ggsb[:, 0:512], -1.0,
                                            None, ALU.mult)
                                        last = v.tensor_scalar(
                                            mixL[:, dt * 1024 + hh * 512:dt * 1024 + (hh + 1) * 512],
                                            ggsb[:, 0:512],
                                            ccol("lnb_%d" % l, dt), None,
                                            ALU.add)
                                    return last
                                tlnap = ctx.sec("vector", lap1,
                                                deps=[tbc44, tsq2], tag="ve")
                                if c > 0 and hh == 0:
                                    def bfix(v):
                                        for dt in range(4, ND):
                                            t1 = ggsb[:, 0:1]
                                            t2 = ggsb[:, 1:2]
                                            v.scalar_tensor_tensor(
                                                t1, bndX[:, dt:dt + 1],
                                                ccol("lng_%d" % l, dt),
                                                PSB[:, 512:513], ALU.mult,
                                                ALU.mult)
                                            v.tensor_scalar(
                                                t2, PSB[:, 1024:1025],
                                                ccol("lng_%d" % l, dt), None,
                                                ALU.mult)
                                            last = v.scalar_tensor_tensor(
                                                mixL[:, dt * 1024:dt * 1024 + 1],
                                                t1, ccol("lnb_%d" % l, dt), t2,
                                                ALU.add, ALU.subtract)
                                        return last
                                    tlnap = ctx.sec("vector", bfix,
                                                    deps=[tlnap], tag="ve")
                            if l == 0 and c == 1 and "mixl" in dbg_out:
                                tlnap = ctx.dma("sync", "dq", dbg_out["mixl"],
                                                mixL[:], deps=[tlnap])
                            tbs4 = ctx.sec("vector",
                                           lambda v: v.tensor_copy(
                                               bndX[:, 0:ND],
                                               xmsec(t0, CHF4)[:, :, CHF4 - 1:CHF4].squeeze()),
                                           deps=[tlnap], tag="ve")
                            tsq_prev = tlnap
                            sq_par = {0: None, 1: None}
                            gg_steps = []
                            twpc = {}
                            for jg in range(6):
                                npair = 4 if jg < 5 else 2
                                srcA = NM["l1w1"][l][:, jg * 512:jg * 512 + npair * 128]
                                srcG = NM["l1w1"][l][:, 2816 + jg * 512:2816 + jg * 512 + npair * 128]
                                half = (jg % 2) * 8192
                                tA = ctx.dma("gpsimd", "dg",
                                             wstr4[:, half:half + npair * 128 * ND].rearrange("p (s o) -> p s o", s=ND),
                                             srcA.rearrange("(s p) o -> p s o", p=PART),
                                             deps=[twpc[jg - 2]] if jg >= 2 else [prev_f])
                                tG = ctx.dma("gpsimd", "dg",
                                             wstr4[:, half + 4096:half + 4096 + npair * 128 * ND].rearrange("p (s o) -> p s o", s=ND),
                                             srcG.rearrange("(s p) o -> p s o", p=PART),
                                             deps=[twpc[jg - 2]] if jg >= 2 else [prev_f])
                                tlast = None
                                for jj in range(npair):
                                    j = jg * 4 + jj
                                    for nc2 in range(2):
                                        jp = (jg * 4 + jj) % 2
                                        psJ = PSA if jp == 0 else PSB
                                        def amm(t, jj=jj, nc2=nc2, half=half, psJ=psJ):
                                            for kt in range(ND):
                                                last = t.matmul(
                                                    psJ[:, nc2 * 512:nc2 * 512 + 512],
                                                    wstr4[:, half + kt * (npair * 128) + jj * 128: half + kt * (npair * 128) + (jj + 1) * 128],
                                                    mixL[:, kt * 1024 + nc2 * 512:kt * 1024 + (nc2 + 1) * 512],
                                                    start=(kt == 0),
                                                    stop=(kt == ND - 1))
                                            return last
                                        def gmm(t, jj=jj, nc2=nc2, half=half, psJ=psJ):
                                            for kt in range(ND):
                                                last = t.matmul(
                                                    psJ[:, 1024 + nc2 * 512:1024 + nc2 * 512 + 512],
                                                    wstr4[:, half + 4096 + kt * (npair * 128) + jj * 128: half + 4096 + kt * (npair * 128) + (jj + 1) * 128],
                                                    mixL[:, kt * 1024 + nc2 * 512:kt * 1024 + (nc2 + 1) * 512],
                                                    start=(kt == 0),
                                                    stop=(kt == ND - 1))
                                            return last
                                        dmm = [tA, tG, tlnap] + ([sq_par[jp]] if sq_par[jp] else [])
                                        ta_mm = ctx.sec("tensor", lambda t, f=amm: f(t),
                                                        deps=dmm, tag="pe")
                                        tg_mm = ctx.sec("tensor", lambda t, f=gmm: f(t),
                                                        deps=[ta_mm], tag="pe")
                                        tgg = ctx.reserve("sc")
                                        gg_half = ggsb[:, jp * 512:(jp + 1) * 512]
                                        gg_steps.append(([tg_mm],
                                                         (lambda s, j=j, nc2=nc2, psJ=psJ, gg_half=gg_half: s.activation(
                                                             gg_half,
                                                             psJ[:, 1024 + nc2 * 512:1024 + nc2 * 512 + 512],
                                                             AF.Gelu,
                                                             bias=CCOL[:, CC["b1p_%d" % l] + 22 + j:CC["b1p_%d" % l] + 22 + j + 1],
                                                             scale=1.0)), tgg))
                                        tsq_prev = ctx.sec("vector",
                                                           lambda v, j=j, nc2=nc2, psJ=psJ, gg_half=gg_half: v.scalar_tensor_tensor(
                                                               prodb[:, j * 1024 + nc2 * 512:j * 1024 + (nc2 + 1) * 512],
                                                               psJ[:, nc2 * 512:nc2 * 512 + 512],
                                                               CCOL[:, CC["b1p_%d" % l] + j:CC["b1p_%d" % l] + j + 1],
                                                               gg_half, ALU.add,
                                                               ALU.mult),
                                                           deps=[tgg], tag="ve")
                                        sq_par[jp] = tsq_prev
                                        tlast = tsq_prev
                                twpc[jg] = tlast
                            ctx.mseq("scalar", gg_steps)
                            twv4 = {}
                            tacc4 = None
                            tevac4 = None
                            for nc2 in range(2):
                                tacc4 = None
                                for pp in range(11):
                                    src2 = NM["l1w2"][l][pp * 256:(pp + 1) * 256, :]
                                    half = (pp % 2) * 8192
                                    dep2 = [twv4[(nc2, pp - 2)]] if pp >= 2 else ([sq_par[0], sq_par[1]] if nc2 == 0 else [tevac4])
                                    t2p = ctx.dma("gpsimd", "dg",
                                                  wstr4[:, half:half + 2048].rearrange("p (s o) -> p s o", s=2),
                                                  src2.rearrange("(s p) o -> p s o", p=PART),
                                                  deps=dep2)
                                    def w2mm(t, pp=pp, half=half, nc2=nc2):
                                        for m in range(8):
                                            for k2 in range(2):
                                                kt = pp * 2 + k2
                                                last = t.matmul(
                                                    (PSA if m < 4 else PSB)[:, (m % 4) * 512:(m % 4) * 512 + 512],
                                                    wstr4[:, half + k2 * 1024 + m * 128:half + k2 * 1024 + (m + 1) * 128],
                                                    prodb[:, kt * 1024 + nc2 * 512:kt * 1024 + (nc2 + 1) * 512],
                                                    start=(pp == 0 and k2 == 0),
                                                    stop=(pp == 10 and k2 == 1))
                                        return last
                                    tacc4 = ctx.sec("tensor", w2mm,
                                                    deps=[t2p, sq_par[0], sq_par[1]] +
                                                         ([tacc4] if tacc4 else []) +
                                                         ([tevac4] if tevac4 else []),
                                                    tag="pe")
                                    twv4[(nc2, pp)] = tacc4
                                tevac4 = ctx.sec("vector",
                                                 lambda v, nc2=nc2: [v.scalar_tensor_tensor(
                                                     xsec(m, t0 + nc2 * 512, 512),
                                                     (PSA if m < 4 else PSB)[:, (m % 4) * 512:(m % 4) * 512 + 512],
                                                     ccol("b2p_%d" % l, m),
                                                     xsec(m, t0 + nc2 * 512, 512),
                                                     ALU.add, ALU.add)
                                                     for m in range(8)][-1],
                                                 deps=[tacc4], tag="ve")
                            if l == 1:
                                x0v4 = wstr4.bitcast(F32)
                                x0r = x0v4[:].rearrange("p (a s c) -> p a s c",
                                                        a=2, s=ND)[:, :, :, 0:512]
                                tx4a = ctx.dma("gpsimd", "dg", x0r[:, 0],
                                               x0_spill[:, (2 * c) * ND * 512:(2 * c + 1) * ND * 512].rearrange("p (s c) -> p s c", s=ND),
                                               deps=[tacc4])
                                tx4b = ctx.dma("gpsimd", "dg", x0r[:, 1],
                                               x0_spill[:, (2 * c + 1) * ND * 512:(2 * c + 2) * ND * 512].rearrange("p (s c) -> p s c", s=ND),
                                               deps=[tacc4])
                                def zadd4(v):
                                    for half in range(2):
                                        last = v.tensor_add(
                                            xmsec(t0 + half * 512, 512),
                                            xmsec(t0 + half * 512, 512),
                                            x0r[:, half])
                                    return last
                                tz4m = ctx.sec("vector", zadd4,
                                               deps=[tx4a, tx4b, tevac4],
                                               tag="ve")
                                hb512 = hbf2[:, 0:ND * 512]
                                ml512 = mixL[:, 0:ND * 512]
                                for half in range(2):
                                    th0p = t0 + half * 512
                                    tbcp, rs_bp, mrs_bp, _tcp = ln_chunk(
                                        [tz4m], xmsec(th0p, 512), 512, hb512,
                                        ml512, 512)
                                    def app4(v, th0p=th0p, rs_bp=rs_bp,
                                             mrs_bp=mrs_bp):
                                        for dt in range(ND):
                                            v.tensor_mul(xsec(dt, th0p, 512),
                                                         xsec(dt, th0p, 512),
                                                         rs_bp)
                                        for dt in range(ND):
                                            last = v.tensor_sub(
                                                xsec(dt, th0p, 512),
                                                xsec(dt, th0p, 512), mrs_bp)
                                        return last
                                    tz4m = ctx.sec("vector", app4,
                                                   deps=[tbcp], tag="ve")
                                tevac4 = tz4m
                            prev_f = tevac4
                        prev_l = prev_f
                if sub4 != "full":
                    pass
            ctx.barrier()

        with contextlib.ExitStack() as _s5:
            ostage = _s5.enter_context(nc.sbuf_tensor("ostage", [PART, 2 * D], F32))
            prevv = []
            for tt in range(NTT):
                def tro(t, tt=tt):
                    for dt in range(ND):
                        last = t.transpose(PSB[:, dt * 128:dt * 128 + 128],
                                           xsec(dt, tt * PART, PART), IDN[:])
                    return last
                tpe = ctx.sec("tensor", tro, deps=(prevv[-1:] if prevv else []),
                              tag="pe")
                sl = ostage[:, (tt % 2) * D:(tt % 2) * D + D]

                def cpo(v, sl=sl):
                    src = PSB[:, 0:ND * 128].rearrange("p (s c) -> p s c", s=ND)
                    return v.tensor_copy(sl.rearrange("p (s c) -> p s c", s=ND),
                                         src)
                tve = ctx.sec("vector", cpo, deps=[tpe], tag="ve")
                ctx.dma("sync", "dq", y_out[tt * PART:(tt + 1) * PART, :], sl,
                        deps=[tve])
                prevv.append(tve)
            ctx.sec("sync", lambda e: None, deps=[ctx.tok("dq")])

    return nc


HID = 2730
HIDP = 2816


def _padw1(w1):
    out = np.zeros((2, 1024, 2 * HIDP), np.float32)
    out[:, :, :HID] = w1[:, :, :HID]
    out[:, :, HIDP:HIDP + HID] = w1[:, :, HID:]
    return out


def _padb1(b1):
    out = np.zeros((2, 2 * HIDP), np.float32)
    out[:, :HID] = b1[:, :HID]
    out[:, HIDP:HIDP + HID] = b1[:, HID:]
    return out


def _padw2(w2):
    out = np.zeros((2, HIDP, 1024), np.float32)
    out[:, :HID, :] = w2
    return out


def _in_maps(inputs, extra_outputs=False):
    consts = _host_consts()
    nmap = {
        "ln1_w": inputs["L0_attn_ln1_w"], "ln1_b": inputs["L0_attn_ln1_b"],
        "ln2_w": inputs["L0_attn_ln2_w"], "ln2_b": inputs["L0_attn_ln2_b"],
        "time_decay": inputs["L0_attn_time_decay"],
        "time_first": inputs["L0_attn_time_first"],
        "mix_k": np.ascontiguousarray(inputs["L0_attn_time_mix_k"]).reshape(-1),
        "mix_v": np.ascontiguousarray(inputs["L0_attn_time_mix_v"]).reshape(-1),
        "mix_r": np.ascontiguousarray(inputs["L0_attn_time_mix_r"]).reshape(-1),
        "w_key": inputs["L0_attn_w_key"], "w_value": inputs["L0_attn_w_value"],
        "w_recept": inputs["L0_attn_w_recept"],
        "w_output": inputs["L0_attn_w_output"],
        "f_mix_k": np.ascontiguousarray(inputs["L0_attn_f_time_mix_k"]).reshape(-1),
        "f_mix_r": np.ascontiguousarray(inputs["L0_attn_f_time_mix_r"]).reshape(-1),
        "f_w_key": inputs["L0_attn_f_w_key"],
        "f_w_recept": inputs["L0_attn_f_w_recept"],
        "f_w_value": inputs["L0_attn_f_w_value"],
        "U1": inputs["L0_ffn_U1"], "V1": inputs["L0_ffn_V1"],
        "b1": inputs["L0_ffn_b1"], "U2": inputs["L0_ffn_U2"],
        "V2": inputs["L0_ffn_V2"], "b2": inputs["L0_ffn_b2"],
        "Wq": inputs["L1_attn_Wq"], "Wk": inputs["L1_attn_Wk"],
        "Wv": inputs["L1_attn_Wv"], "Wo": inputs["L1_attn_Wo"],
        "bq": inputs["L1_attn_bq"], "bk": inputs["L1_attn_bk"],
        "bv": inputs["L1_attn_bv"], "bo": inputs["L1_attn_bo"],
        "wqkv": inputs["L1_ffn_wqkv"], "q_scale": inputs["L1_ffn_q_scale"],
        "k_scale": inputs["L1_ffn_k_scale"], "l1wo": inputs["L1_ffn_wo"],
        "ln_g": inputs["L1_ffn_ln_g"], "ln_b": inputs["L1_ffn_ln_b"],
        "l1w2": _padw2(inputs["L1_ffn_w2"]), "l1b2": inputs["L1_ffn_b2"],
        "l1w1": _padw1(inputs["L1_ffn_w1"]),
        "l1b1": _padb1(inputs["L1_ffn_b1"]),
    }
    nmap.update(consts)
    nmap = {k: np.ascontiguousarray(v) for k, v in nmap.items()}
    x = np.ascontiguousarray(inputs["L0_attn_x"], dtype=np.float32)
    maps = []
    for c in range(8):
        m = dict(nmap)
        m["x"] = np.ascontiguousarray(x[c])
        maps.append(m)
    return maps


def kernel(**inputs):
    nc = build_program(stages=STAGES, dbg=DBG)
    maps = _in_maps(inputs)
    res = run_bass_kernel_spmd(nc, maps, core_ids=list(range(8)))
    return np.stack([np.asarray(r["y"]) for r in res.results], axis=0)


# version 05
# speedup vs baseline: 1.0057x; 1.0057x over previous
import contextlib
import numpy as np
import concourse.bass as bass
import concourse.mybir as mybir
from concourse.bass_utils import run_bass_kernel_spmd

F32 = mybir.dt.float32
BF16 = mybir.dt.bfloat16
AF = mybir.ActivationFunctionType
ALU = mybir.AluOpType

PART = 128
T = 2048
D = 1024
ND = 8
SECW = T + 1
NTT = 16
EPS = 1e-5

STAGES = 4
DBG = ()


def _host_consts():
    import ml_dtypes
    bf = ml_dtypes.bfloat16
    c = {}
    c["c_ident"] = np.eye(PART, dtype=np.float32)
    c["c_ones_col_bf"] = np.ones((PART, 1), dtype=bf)
    c["c_ones_row_f"] = np.ones((1, PART), dtype=np.float32)
    e2 = np.zeros((PART, 2), dtype=np.float32)
    e2[0:64, 0] = 1.0
    e2[64:128, 1] = 1.0
    c["c_E2"] = e2
    c["c_E2X"] = np.ascontiguousarray(e2.T)
    qi = np.arange(128)[None, :]
    ki = np.arange(128)[:, None]
    c["c_maskP1"] = (qi >= ki).astype(bf)
    c["c_maskP2"] = (qi <= ki).astype(bf)
    DH, WINDOW = 64, 512
    half = np.arange(0, DH, 2, dtype=np.float64)
    inv_freq = 1.0 / 10000.0 ** (half / DH)
    tt = np.arange(T, dtype=np.float64)
    freqs = np.concatenate([tt[:, None] * inv_freq[None, :]] * 2, axis=-1)
    base = (half + 0.4 * DH) / (1.4 * DH)
    power = (tt - T // 2) / float(WINDOW // 2)
    scale = np.concatenate([base[None, :] ** power[:, None]] * 2, axis=-1)
    cos, sin = np.cos(freqs), np.sin(freqs)
    sgn = np.ones(64)
    sgn[:32] = -1.0
    t1q = (cos * scale).T
    t2q = sgn[:, None] * (sin * scale).T
    t1k = (cos / scale).T
    t2k = sgn[:, None] * (sin / scale).T
    c["c_T1Q"] = np.concatenate([t1q, t1q], 0).astype(np.float32)
    c["c_T2Q"] = np.concatenate([t2q, t2q], 0).astype(np.float32)
    c["c_T1K"] = np.concatenate([t1k, t1k], 0).astype(np.float32)
    c["c_T2K"] = np.concatenate([t2k, t2k], 0).astype(np.float32)
    return c


DMAPOOL = 6


class Ctx:
    def __init__(self, nc, blk):
        self.nc = nc
        self.blk = blk
        self.sem = {}
        self.cnt = {}
        self.water = {}
        for name in ("pe", "ve", "sc", "gp"):
            self.sem[name] = nc.alloc_semaphore("sem_" + name)
            self.cnt[name] = 0
        self.dix = {}
        for q in ("dq", "dg", "da"):
            self.dix[q] = 0
            for j in range(DMAPOOL):
                nm = f"{q}{j}"
                self.sem[nm] = nc.alloc_semaphore("sem_" + nm)
                self.cnt[nm] = 0

    def tok(self, name):
        return (name, self.cnt[name])

    def _waits(self, engname, eng, deps):
        need = {}
        for d in deps:
            if d is None:
                continue
            s, v = d
            if v > 0 and need.get(s, 0) < v:
                need[s] = v
        for s, v in need.items():
            if self.water.get((engname, s), 0) < v:
                eng.wait_ge(self.sem[s], v)
                self.water[(engname, s)] = v

    def sec(self, engname, fn, deps=(), tag=None, by=1):
        out_tok = [None]

        def body(eng):
            self._waits(engname, eng, deps)
            last = fn(eng)
            if tag is not None:
                assert last is not None, f"sec({engname}) returned None with tag"
                last.then_inc(self.sem[tag], by)
                self.cnt[tag] += by
                out_tok[0] = (tag, self.cnt[tag])

        getattr(self.blk, engname)(body)
        return out_tok[0]

    def reserve(self, tag, n=1):
        self.cnt[tag] += n
        return (tag, self.cnt[tag])

    def mseq(self, engname, steps):
        def body(eng):
            for deps, fn, tok in steps:
                self._waits(engname, eng, deps)
                last = fn(eng)
                if tok is not None:
                    last.then_inc(self.sem[tok[0]], 1)
        getattr(self.blk, engname)(body)

    def dma(self, qeng, sem, out, in_, deps=(), slow=False, transpose=False):
        i = self.dix[sem]
        self.dix[sem] = i + 1
        nm = f"{sem}{i % DMAPOOL}"
        gate = (nm, 16 * (i // DMAPOOL))

        def body(eng):
            self._waits(qeng, eng, list(deps) + [gate])
            if transpose:
                eng.dma_start(out, in_, transpose=True).then_inc(self.sem[nm], 16)
            else:
                eng.dma_start(out, in_,
                              allow_slow_non_contiguous=slow).then_inc(self.sem[nm], 16)
        getattr(self.blk, qeng)(body)
        self.cnt[nm] += 16
        return (nm, self.cnt[nm])

    def barrier(self):
        toks = [self.tok(s) for s in ("pe", "ve", "sc", "gp")]
        for q in ("dq", "dg", "da"):
            for j in range(DMAPOOL):
                toks.append(self.tok(f"{q}{j}"))

        def mk(engname):
            def body(eng):
                self._waits(engname, eng, toks)
            return body
        for e in ("tensor", "vector", "scalar", "gpsimd", "sync"):
            getattr(self.blk, e)(mk(e))


def build_program(stages=STAGES, dbg=DBG, sub4="full"):
    if isinstance(stages, int):
        stages = set(range(1, stages + 1))
    nc = bass.Bass("TRN2", target_bir_lowering=False)
    NM = {}

    def din(name, shape, dtype=F32):
        NM[name] = nc.dram_tensor(name, shape, dtype, kind="ExternalInput").ap()
        return NM[name]

    x_in = din("x", [T, D])
    y_out = nc.dram_tensor("y", [T, D], F32, kind="ExternalOutput").ap()

    for n, s in [("ln1_w", [D]), ("ln1_b", [D]), ("ln2_w", [D]), ("ln2_b", [D]),
                 ("time_decay", [D]), ("time_first", [D]),
                 ("mix_k", [D]), ("mix_v", [D]), ("mix_r", [D]),
                 ("w_key", [D, D]), ("w_value", [D, D]), ("w_recept", [D, D]),
                 ("w_output", [D, D]),
                 ("f_mix_k", [D]), ("f_mix_r", [D]),
                 ("f_w_key", [D, 4 * D]), ("f_w_recept", [D, D]),
                 ("f_w_value", [4 * D, D]),
                 ("U1", [D, 256]), ("V1", [256, 4 * D]), ("b1", [4 * D]),
                 ("U2", [4 * D, 256]), ("V2", [256, D]), ("b2", [D]),
                 ("Wq", [D, D]), ("Wk", [D, D]), ("Wv", [D, D]), ("Wo", [D, D]),
                 ("bq", [D]), ("bk", [D]), ("bv", [D]), ("bo", [D]),
                 ("wqkv", [2, D, 1536]), ("q_scale", [2, 64]), ("k_scale", [2, 64]),
                 ("l1wo", [2, 512, D]), ("ln_g", [2, D]), ("ln_b", [2, D]),
                 ("l1w1", [2, D, 2 * 2816]), ("l1b1", [2, 2 * 2816]),
                 ("l1w2", [2, 2816, D]), ("l1b2", [2, D])]:
        din(n, s)
    for n, s, dt in [("c_ident", [PART, PART], F32),
                     ("c_ones_col_bf", [PART, 1], BF16),
                     ("c_ones_row_f", [1, PART], F32),
                     ("c_E2", [PART, 2], F32), ("c_E2X", [2, PART], F32),
                     ("c_maskP1", [PART, PART], BF16),
                     ("c_maskP2", [PART, PART], BF16),
                     ("c_T1Q", [PART, T], F32), ("c_T2Q", [PART, T], F32),
                     ("c_T1K", [PART, T], F32), ("c_T2K", [PART, T], F32)]:
        din(n, s, dt)

    x0_spill = nc.dram_tensor("x0_spill", [PART, ND * T], F32).ap()

    dbg_out = {}
    dbg_shapes = {"mixl": [PART, ND * 1024], "qtr4": [PART, 4 * 512], "ktr4": [PART, 4 * T],
                  "otl4": [PART, 4 * 512], "den4": [PART, 2048],
                  "corr": [PART, ND], "ktb": [PART, ND * T], "qtb": [PART, ND * 512],
                  "probs": [PART, NTT * 512], "outT": [PART, ND * 512],
                  "x1": [PART, ND * SECW]}
    dbg_dt = {"mixl": BF16, "qtr4": BF16, "ktr4": BF16, "otl4": BF16, "den4": F32,
              "corr": F32, "ktb": BF16, "qtb": BF16, "probs": BF16, "outT": BF16,
              "x1": F32}
    for n in dbg:
        dbg_out[n] = nc.dram_tensor("dbg_" + n, dbg_shapes[n], dbg_dt[n],
                                    kind="ExternalOutput").ap()

    with contextlib.ExitStack() as _st:
        _e = _st.enter_context
        XB = _e(nc.sbuf_tensor("XB", [PART, ND * SECW], F32))
        IDN = _e(nc.sbuf_tensor("IDN", [PART, PART], F32))
        CCOL = _e(nc.sbuf_tensor("CCOL", [PART, 512], F32))
        CBF = _e(nc.sbuf_tensor("CBF", [PART, 4], BF16))
        CROWF = _e(nc.sbuf_tensor("CROWF", [1, PART], F32))
        E2S = _e(nc.sbuf_tensor("E2S", [PART, 2], F32))
        E2XS = _e(nc.sbuf_tensor("E2XS", [2, PART], F32))
        ROWS = _e(nc.sbuf_tensor("ROWS", [PART, 2048], F32))
        PSA = _e(nc.psum_tensor("PSA", [PART, 2048], F32))
        PSB = _e(nc.psum_tensor("PSB", [PART, 2048], F32))
        blk = _e(nc.Block())
        ctx = Ctx(nc, blk)

        def xsec(dt, t0, n, shift=0, buf=None):
            b = XB if buf is None else buf
            base = dt * SECW + 1 + t0 - shift
            return b[:, base:base + n]

        def xmsec(t0, n, shift=0):
            r = XB[:].rearrange("p (s c) -> p s c", s=ND)
            return r[:, :, 1 + t0 - shift:1 + t0 - shift + n]

        def csec(buf, dt, ch, n=None):
            n = ch if n is None else n
            return buf[:, dt * ch:dt * ch + n]

        def cmsec(buf, ch, n=None, nd=ND):
            n = ch if n is None else n
            return buf[:].rearrange("p (s c) -> p s c", s=nd)[:, :, 0:n]

        tload = []
        tload.append(ctx.dma("sync", "dq", IDN[:], NM["c_ident"]))
        tload.append(ctx.dma("sync", "dq", CBF[:, 0:1], NM["c_ones_col_bf"]))
        tload.append(ctx.dma("sync", "dq", CROWF[:], NM["c_ones_row_f"]))
        tload.append(ctx.dma("sync", "dq", E2S[:], NM["c_E2"]))
        tload.append(ctx.dma("sync", "dq", E2XS[:], NM["c_E2X"]))

        CC = {}
        _ccn = [0]

        def load_dcol(name, src_ap, nsec=ND):
            j = _ccn[0]
            _ccn[0] += nsec
            CC[name] = j
            ap = src_ap.rearrange("(s p) -> p s", p=PART)
            return ctx.dma("sync", "dq", CCOL[:, j:j + nsec], ap, slow=True)

        for n in ("ln1_w", "ln1_b", "ln2_w", "ln2_b", "time_decay", "time_first",
                  "mix_k", "mix_v", "mix_r", "f_mix_k", "f_mix_r"):
            tload.append(load_dcol(n, NM[n]))

        def ccol(name, dt):
            return CCOL[:, CC[name] + dt:CC[name] + dt + 1]

        CC["d"] = _ccn[0]; _ccn[0] += ND
        CC["eu"] = _ccn[0]; _ccn[0] += ND

        def _derived(s):
            s.activation(CCOL[:, CC["d"]:CC["d"] + ND],
                         CCOL[:, CC["time_decay"]:CC["time_decay"] + ND],
                         AF.Exp, bias=0.0, scale=1.0)
            s.activation(CCOL[:, CC["d"]:CC["d"] + ND],
                         CCOL[:, CC["d"]:CC["d"] + ND], AF.Exp, bias=0.0,
                         scale=-1.0)
            return s.activation(CCOL[:, CC["eu"]:CC["eu"] + ND],
                                CCOL[:, CC["time_first"]:CC["time_first"] + ND],
                                AF.Exp, bias=0.0, scale=1.0)
        t_der = ctx.sec("scalar", _derived, deps=tload, tag="sc")
        t_eps = ctx.sec("vector",
                        lambda v: v.memset(CCOL[:, 511:512], EPS),
                        deps=tload, tag="ve")
        eps_ap = CCOL[0:1, 511:512]

        _rowptr = [0]

        def rows3(n):
            o = _rowptr[0]
            if o + 3 * n > 2048:
                o = 0
            _rowptr[0] = o + 3 * n
            return (ROWS[0:1, o:o + n], ROWS[0:1, o + n:o + 2 * n],
                    ROWS[0:1, o + 2 * n:o + 3 * n])

        def ln_chunk(deps, src3d, n, xbf, sqb, ch):
            tc = ctx.sec("vector",
                         lambda v: v.tensor_copy(cmsec(xbf, ch, n), src3d),
                         deps=deps, tag="ve")
            tsq = ctx.sec("vector",
                          lambda v: v.tensor_mul(cmsec(sqb, ch, n),
                                                 cmsec(xbf, ch, n),
                                                 cmsec(xbf, ch, n)),
                          deps=[tc], tag="ve")
            p_sum = PSB[0:1, 0:n]
            p_sq = PSB[0:1, 512:512 + n]

            def mm(t):
                for dt in range(ND):
                    t.matmul(p_sum, CBF[:, 0:1], csec(xbf, dt, ch, n),
                             start=(dt == 0), stop=(dt == ND - 1))
                for dt in range(ND):
                    last = t.matmul(p_sq, CBF[:, 0:1], csec(sqb, dt, ch, n),
                                    start=(dt == 0), stop=(dt == ND - 1))
                return last
            tmm = ctx.sec("tensor", mm, deps=[tc, tsq], tag="pe")

            r_m, r_rs, r_mrs = rows3(n)
            def rowv(v):
                v.tensor_scalar(r_m, p_sum, 1.0 / D, None, ALU.mult)
                v.tensor_mul(r_rs, r_m, r_m)
                return v.scalar_tensor_tensor(r_mrs, p_sq, 1.0 / D, r_rs,
                                              ALU.mult, ALU.subtract)
            t3 = ctx.sec("vector", rowv, deps=[tmm], tag="ve")
            def rowsc(s):
                s.activation(r_rs, r_mrs, AF.Ln, bias=eps_ap, scale=1.0)
                return s.activation(r_rs, r_rs, AF.Exp, bias=0.0, scale=-0.5)
            t5 = ctx.sec("scalar", rowsc, deps=[t3, t_eps], tag="sc")
            t6 = ctx.sec("vector", lambda v: v.tensor_mul(r_mrs, r_m, r_rs),
                         deps=[t5], tag="ve")
            rs_b = PSB[:, 512:512 + n]
            mrs_b = PSB[:, 1024:1024 + n]

            def bc(t):
                t.matmul(rs_b, CROWF[:], r_rs, start=True, stop=True)
                return t.matmul(mrs_b, CROWF[:], r_mrs, start=True, stop=True)
            tbc = ctx.sec("tensor", bc, deps=[t5, t6], tag="pe")
            return tbc, rs_b, mrs_b, tc

        def ln_apply_wb(n, xn_buf, chw, t0, rs_b, mrs_b, wname, bname, deps):
            def ap1(v):
                for dt in range(ND):
                    v.scalar_tensor_tensor(
                        xn_buf[:, dt * (chw + 1) + 1:dt * (chw + 1) + 1 + n],
                        xsec(dt, t0, n), ccol(wname, dt), rs_b,
                        ALU.mult, ALU.mult)
                for dt in range(ND):
                    last = v.scalar_tensor_tensor(
                        xn_buf[:, dt * (chw + 1) + 1:dt * (chw + 1) + 1 + n],
                        mrs_b, ccol(wname, dt),
                        xn_buf[:, dt * (chw + 1) + 1:dt * (chw + 1) + 1 + n],
                        ALU.mult, ALU.subtract)
                return last
            tv = ctx.sec("vector", ap1, deps=deps, tag="ve")
            ts = ctx.sec("scalar",
                         lambda s: [s.activation(
                             xn_buf[:, dt * (chw + 1) + 1:dt * (chw + 1) + 1 + n],
                             xn_buf[:, dt * (chw + 1) + 1:dt * (chw + 1) + 1 + n],
                             AF.Identity, bias=ccol(bname, dt), scale=-1.0)
                             for dt in range(ND)][-1],
                         deps=[tv], tag="sc")
            return tv, ts

        with contextlib.ExitStack() as _s4:
            tstage = _s4.enter_context(nc.sbuf_tensor("tstage", [PART, 2 * D], F32))
            tts = []
            for tt in range(NTT):
                sl = tstage[:, (tt % 2) * D:(tt % 2) * D + D]
                dep = [tts[tt - 2][1]] if tt >= 2 else []
                tdma = ctx.dma("sync", "dq", sl,
                               x_in[tt * PART:(tt + 1) * PART, :], deps=dep)

                def tr(t, sl=sl):
                    for dt in range(ND):
                        last = t.transpose(PSA[:, dt * 128:dt * 128 + 128],
                                           sl[:, dt * 128:(dt + 1) * 128],
                                           IDN[:])
                    return last
                dtr = [tdma, tload[0]] + ([tts[tt - 1][1]] if tt else [])
                tpe = ctx.sec("tensor", tr, deps=dtr, tag="pe")

                def cp2(v, tt=tt):
                    dst = xmsec(tt * PART, PART)
                    src = PSA[:, 0:ND * 128].rearrange("p (s c) -> p s c", s=ND)
                    return v.tensor_copy(dst, src)
                tve = ctx.sec("vector", cp2, deps=[tpe], tag="ve")
                tts.append((tpe, tve))

            def zpad(v):
                r = XB[:].rearrange("p (s c) -> p s c", s=ND)
                return v.memset(r[:, :, 0:1], 0.0)
            ctx.sec("vector", zpad, deps=[], tag="ve")
        ctx.barrier()

        if 1 in stages:
            CH = 256
            with contextlib.ExitStack() as _s1:
                _f = _s1.enter_context
                wk_sb = _f(nc.sbuf_tensor("wk_sb", [PART, ND * D], BF16))
                wv_sb = _f(nc.sbuf_tensor("wv_sb", [PART, ND * D], BF16))
                wr_sb = _f(nc.sbuf_tensor("wr_sb", [PART, ND * D], BF16))
                wo_sb = _f(nc.sbuf_tensor("wo_sb", [PART, ND * D], BF16))
                xn1 = _f(nc.sbuf_tensor("xn1", [PART, ND * (CH + 1)], F32))
                ek = _f(nc.sbuf_tensor("ek", [PART, ND * CH], F32))
                ekv = _f(nc.sbuf_tensor("ekv", [PART, ND * CH], F32))
                srb = _f(nc.sbuf_tensor("srb", [PART, ND * CH], F32))
                Asc = _f(nc.sbuf_tensor("Asc", [PART, ND * (CH + 1)], F32))
                Bsc = _f(nc.sbuf_tensor("Bsc", [PART, ND * (CH + 1)], F32))
                mixA = _f(nc.sbuf_tensor("mixA", [PART, ND * CH], BF16))
                mixB = _f(nc.sbuf_tensor("mixB", [PART, ND * CH], BF16))
                mixC = _f(nc.sbuf_tensor("mixC", [PART, ND * CH], BF16))
                bnd = _f(nc.sbuf_tensor("bnd", [PART, 16], F32))
                gscr = _f(nc.sbuf_tensor("gscr", [PART, CH], F32))
                wtoks = []
                for wname, wsb in [("w_key", wk_sb), ("w_value", wv_sb),
                                   ("w_recept", wr_sb), ("w_output", wo_sb)]:
                    src = NM[wname].rearrange("(s p) o -> p s o", p=PART)
                    dst = wsb[:].rearrange("p (s o) -> p s o", s=ND)
                    wtoks.append(ctx.dma("gpsimd", "dg", dst, src))

                def slot(i, n=CH):
                    if i < 4:
                        return PSA[:, i * 512:i * 512 + n]
                    return PSB[:, (i - 4) * 512:(i - 4) * 512 + n]

                prev_ta1 = None
                for c in range(T // CH):
                    t0 = c * CH
                    dd = list(wtoks) + ([prev_ta1] if prev_ta1 else [])
                    tbc, rs_b, mrs_b, tc = ln_chunk(dd, xmsec(t0, CH), CH,
                                                    mixC, mixB, CH)

                    def bsave(v):
                        if c == 0:
                            return v.memset(bnd[:, 0:ND], 0.0)
                        r = xn1[:].rearrange("p (s c) -> p s c", s=ND)
                        return v.tensor_copy(
                            bnd[:, 0:ND],
                            r[:, :, CH:CH + 1].squeeze())
                    tbs = ctx.sec("vector", bsave, deps=[], tag="ve")

                    tap1, tap2 = ln_apply_wb(CH, xn1, CH, t0, rs_b, mrs_b,
                                             "ln1_w", "ln1_b", [tbc, tbs])
                    tpad = ctx.sec("vector",
                                   lambda v: v.tensor_copy(
                                       xn1[:].rearrange("p (s c) -> p s c", s=ND)[:, :, 0:1].squeeze(),
                                       bnd[:, 0:ND]),
                                   deps=[tap2], tag="ve")

                    xn1r = xn1[:].rearrange("p (s c) -> p s c", s=ND)
                    tam = ctx.sec("vector",
                                  lambda v: v.tensor_sub(cmsec(ekv, CH),
                                                         xn1r[:, :, 1:1 + CH],
                                                         xn1r[:, :, 0:CH]),
                                  deps=[tap2, tpad], tag="ve")

                    def mix(eng, nm, dst):
                        for dt in range(ND):
                            last = eng.scalar_tensor_tensor(
                                csec(dst, dt, CH), csec(ekv, dt, CH),
                                ccol(nm, dt),
                                xn1[:, dt * (CH + 1):dt * (CH + 1) + CH],
                                ALU.mult, ALU.add)
                        return last

                    def mix_g(g, nm, dst):
                        for dt in range(ND):
                            g.tensor_scalar(gscr[:], csec(ekv, dt, CH),
                                            ccol(nm, dt), None, ALU.mult)
                            last = g.tensor_tensor(
                                csec(dst, dt, CH), gscr[:],
                                xn1[:, dt * (CH + 1):dt * (CH + 1) + CH],
                                ALU.add)
                        return last
                    tmk = ctx.sec("vector", lambda v: mix(v, "mix_k", mixA),
                                  deps=[tam, tc], tag="ve")
                    tmv = ctx.sec("gpsimd", lambda g: mix_g(g, "mix_v", mixB),
                                  deps=[tam, tbc], tag="gp")
                    tmr = ctx.sec("gpsimd", lambda g: mix_g(g, "mix_r", mixC),
                                  deps=[tam, tc, tmv], tag="gp")

                    def mmround(t, wsb, mbf, r):
                        for m in range(r * 4, r * 4 + 4):
                            for kt in range(ND):
                                last = t.matmul(
                                    slot(m),
                                    wsb[:, kt * D + m * 128:kt * D + (m + 1) * 128],
                                    csec(mbf, kt, CH), start=(kt == 0),
                                    stop=(kt == ND - 1))
                        return last

                    tk0 = ctx.sec("tensor", lambda t: mmround(t, wk_sb, mixA, 0),
                                  deps=[tmk, tbc, tap1], tag="pe")
                    tk1 = ctx.sec("tensor", lambda t: mmround(t, wk_sb, mixA, 1),
                                  deps=[tk0], tag="pe")
                    tek0 = ctx.reserve("sc")
                    tek1 = ctx.reserve("sc")
                    sc1_steps = [
                        ([tk0], lambda s: [s.activation(csec(ek, m, CH), slot(m),
                                                        AF.Exp, bias=0.0,
                                                        scale=1.0)
                                           for m in range(4)][-1], tek0),
                        ([tk1], lambda s: [s.activation(csec(ek, m, CH), slot(m),
                                                        AF.Exp, bias=0.0,
                                                        scale=1.0)
                                           for m in range(4, 8)][-1], tek1),
                    ]

                    tv0 = ctx.sec("tensor", lambda t: mmround(t, wv_sb, mixB, 0),
                                  deps=[tmv, tek0, tmk], tag="pe")
                    tev0 = ctx.sec("vector",
                                   lambda v: [v.tensor_mul(csec(ekv, m, CH),
                                                           csec(ek, m, CH),
                                                           slot(m))
                                              for m in range(4)][-1],
                                   deps=[tv0, tek0, tmr], tag="ve")
                    tv1 = ctx.sec("tensor", lambda t: mmround(t, wv_sb, mixB, 1),
                                  deps=[tek1], tag="pe")
                    tev1 = ctx.sec("vector",
                                   lambda v: [v.tensor_mul(csec(ekv, m, CH),
                                                           csec(ek, m, CH),
                                                           slot(m))
                                              for m in range(4, 8)][-1],
                                   deps=[tv1, tek1], tag="ve")

                    tr0 = ctx.sec("tensor", lambda t: mmround(t, wr_sb, mixC, 0),
                                  deps=[tmr, tev0], tag="pe")
                    tr1 = ctx.sec("tensor", lambda t: mmround(t, wr_sb, mixC, 1),
                                  deps=[tev1], tag="pe")
                    tsr0 = ctx.reserve("sc")
                    tsr1 = ctx.reserve("sc")
                    sc1_steps += [
                        ([tr0], lambda s: [s.activation(csec(srb, m, CH), slot(m),
                                                        AF.Sigmoid, bias=0.0,
                                                        scale=1.0)
                                           for m in range(4)][-1], tsr0),
                        ([tr1], lambda s: [s.activation(csec(srb, m, CH), slot(m),
                                                        AF.Sigmoid, bias=0.0,
                                                        scale=1.0)
                                           for m in range(4, 8)][-1], tsr1),
                    ]
                    ctx.mseq("scalar", sc1_steps)

                    def carry(eng, buf):
                        r = buf[:].rearrange("p (s c) -> p s c", s=ND)
                        if c == 0:
                            return eng.memset(r[:, :, 0:1], 0.0)
                        return eng.tensor_copy(
                            r[:, :, 0:1].squeeze(),
                            r[:, :, CH:CH + 1].squeeze())
                    tca = ctx.sec("vector", lambda v: carry(v, Asc), deps=[],
                                  tag="ve")
                    tsa = ctx.sec("vector",
                                  lambda v: [v.tensor_tensor_scan(
                                      Asc[:, dt * (CH + 1) + 1:dt * (CH + 1) + 1 + CH],
                                      ccol("d", dt).broadcast_to([PART, CH]),
                                      csec(ekv, dt, CH),
                                      Asc[:, dt * (CH + 1):dt * (CH + 1) + 1],
                                      ALU.mult, ALU.add) for dt in range(ND)][-1],
                                  deps=[tev1, tca, t_der], tag="ve")
                    tcb = ctx.sec("vector", lambda v: carry(v, Bsc), deps=[],
                                  tag="ve")
                    tsb = ctx.sec("vector",
                                  lambda v: [v.tensor_tensor_scan(
                                      Bsc[:, dt * (CH + 1) + 1:dt * (CH + 1) + 1 + CH],
                                      ccol("d", dt).broadcast_to([PART, CH]),
                                      csec(ek, dt, CH),
                                      Bsc[:, dt * (CH + 1):dt * (CH + 1) + 1],
                                      ALU.mult, ALU.add) for dt in range(ND)][-1],
                                  deps=[tek1, tcb, t_der], tag="ve")

                    tnum = ctx.sec("vector",
                                   lambda v: [v.scalar_tensor_tensor(
                                       csec(ekv, dt, CH), csec(ekv, dt, CH),
                                       ccol("eu", dt),
                                       Asc[:, dt * (CH + 1):dt * (CH + 1) + CH],
                                       ALU.mult, ALU.add) for dt in range(ND)][-1],
                                   deps=[tsa], tag="ve")
                    def den_g(g):
                        for dt in range(ND):
                            g.tensor_scalar(csec(ek, dt, CH), csec(ek, dt, CH),
                                            ccol("eu", dt), None, ALU.mult)
                            last = g.tensor_tensor(
                                csec(ek, dt, CH), csec(ek, dt, CH),
                                Bsc[:, dt * (CH + 1):dt * (CH + 1) + CH],
                                ALU.add)
                        return last
                    tden = ctx.sec("gpsimd", den_g, deps=[tsb, tek1, tmr],
                                   tag="gp")
                    def rden_s(s):
                        s.activation(ek[:], ek[:], AF.Ln, bias=0.0, scale=1.0)
                        return s.activation(ek[:], ek[:], AF.Exp, bias=0.0,
                                            scale=-1.0)
                    trc = ctx.sec("scalar", rden_s, deps=[tden], tag="sc")

                    tyy = ctx.sec("vector",
                                  lambda v: v.tensor_mul(ekv[:], ekv[:], ek[:]),
                                  deps=[trc, tnum], tag="ve")
                    tyb = ctx.sec("vector",
                                  lambda v: v.tensor_mul(mixA[:], ekv[:], srb[:]),
                                  deps=[tyy, tsr1, tk1], tag="ve")

                    tspill = ctx.dma("gpsimd", "dg",
                                     x0_spill[:, c * ND * CH:(c + 1) * ND * CH].rearrange("p (s c) -> p s c", s=ND),
                                     xmsec(t0, CH), deps=[tden])
                    tw0 = ctx.sec("tensor", lambda t: mmround(t, wo_sb, mixA, 0),
                                  deps=[tyb, tsr1], tag="pe")
                    ta0 = ctx.sec("vector",
                                  lambda v: [v.tensor_add(xsec(m, t0, CH),
                                                          xsec(m, t0, CH),
                                                          slot(m))
                                             for m in range(4)][-1],
                                  deps=[tw0, tspill], tag="ve")
                    tw1 = ctx.sec("tensor", lambda t: mmround(t, wo_sb, mixA, 1),
                                  deps=[tw0], tag="pe")
                    ta1 = ctx.sec("vector",
                                  lambda v: [v.tensor_add(xsec(m, t0, CH),
                                                          xsec(m, t0, CH),
                                                          slot(m))
                                             for m in range(4, 8)][-1],
                                  deps=[tw1, tspill, ta0], tag="ve")
                    prev_ta1 = ta1

            ctx.barrier()

            CH2 = 512
            with contextlib.ExitStack() as _s2:
                _f = _s2.enter_context
                fwr_sb = _f(nc.sbuf_tensor("fwr_sb", [PART, ND * D], BF16))
                wstr = _f(nc.sbuf_tensor("wstr", [PART, 8192], BF16))
                relu2 = _f(nc.sbuf_tensor("relu2", [PART, 32 * CH2], BF16))
                relu_f = _f(nc.sbuf_tensor("relu_f", [PART, 2048], F32))
                sfr = _f(nc.sbuf_tensor("sfr", [PART, ND * CH2], F32))
                xn2 = _f(nc.sbuf_tensor("xn2", [PART, ND * (CH2 + 1)], F32))
                fmk = _f(nc.sbuf_tensor("fmk", [PART, ND * CH2], BF16))
                fmr = _f(nc.sbuf_tensor("fmr", [PART, ND * CH2], BF16))
                gsc = _f(nc.sbuf_tensor("gsc", [PART, CH2], F32))
                bnd2 = _f(nc.sbuf_tensor("bnd2", [PART, 16], F32))
                x0c2 = wstr.bitcast(F32)
                tfwr = ctx.dma("gpsimd", "dg",
                               fwr_sb[:].rearrange("p (s o) -> p s o", s=ND),
                               NM["f_w_recept"].rearrange("(s p) o -> p s o",
                                                          p=PART))

                def slot8(i, n=CH2):
                    if i < 4:
                        return PSA[:, i * 512:i * 512 + n]
                    return PSB[:, (i - 4) * 512:(i - 4) * 512 + n]

                prev_ev = None
                for c in range(T // CH2):
                    t0 = c * CH2
                    dd = [tfwr] + ([prev_ev] if prev_ev else [])
                    def bsave2(v):
                        if c == 0:
                            return v.memset(bnd2[:, 0:ND], 0.0)
                        r = xn2[:].rearrange("p (s c) -> p s c", s=ND)
                        return v.tensor_copy(
                            bnd2[:, 0:ND],
                            r[:, :, CH2:CH2 + 1].squeeze())
                    tbs2 = ctx.sec("vector", bsave2, deps=[], tag="ve")

                    tbc2, rs_b2, mrs_b2, tc2 = ln_chunk(dd, xmsec(t0, CH2), CH2,
                                                        fmr, fmk, CH2)
                    tap1b, tap2b = ln_apply_wb(CH2, xn2, CH2, t0, rs_b2, mrs_b2,
                                               "ln2_w", "ln2_b", [tbc2, tbs2])
                    tpad2 = ctx.sec("vector",
                                    lambda v: v.tensor_copy(
                                        xn2[:].rearrange("p (s c) -> p s c", s=ND)[:, :, 0:1].squeeze(),
                                        bnd2[:, 0:ND]),
                                    deps=[tap2b], tag="ve")

                    xn2w = lambda dt, sh=0: xn2[:, dt * (CH2 + 1) + 1 - sh:dt * (CH2 + 1) + 1 - sh + CH2]

                    def fmixv(v):
                        for dt in range(ND):
                            v.tensor_sub(relu_f[:, 0:CH2], xn2w(dt), xn2w(dt, 1))
                            last = v.scalar_tensor_tensor(
                                csec(fmk, dt, CH2), relu_f[:, 0:CH2],
                                ccol("f_mix_k", dt), xn2w(dt, 1), ALU.mult,
                                ALU.add)
                        return last

                    def fmixg(g):
                        for dt in range(ND):
                            g.tensor_sub(gsc[:], xn2w(dt), xn2w(dt, 1))
                            g.tensor_scalar(gsc[:], gsc[:], ccol("f_mix_r", dt),
                                            None, ALU.mult)
                            last = g.tensor_tensor(csec(fmr, dt, CH2), gsc[:],
                                                   xn2w(dt, 1), ALU.add)
                        return last
                    tfmk = ctx.sec("vector", fmixv, deps=[tpad2, tap2b], tag="ve")
                    tfmr = ctx.sec("gpsimd", fmixg, deps=[tpad2, tap2b, tc2],
                                   tag="gp")

                    def frr(t, r):
                        for m in range(r * 4, r * 4 + 4):
                            for kt in range(ND):
                                last = t.matmul(
                                    PSA[:, (m % 4) * 512:(m % 4) * 512 + CH2],
                                    fwr_sb[:, kt * D + m * 128:kt * D + (m + 1) * 128],
                                    csec(fmr, kt, CH2), start=(kt == 0),
                                    stop=(kt == ND - 1))
                        return last
                    tfr0 = ctx.sec("tensor", lambda t: frr(t, 0),
                                   deps=[tfmr, tbc2], tag="pe")
                    tsf0 = ctx.reserve("sc")
                    tsf1 = ctx.reserve("sc")
                    tfr1 = ctx.sec("tensor", lambda t: frr(t, 1), deps=[tsf0],
                                   tag="pe")
                    ctx.mseq("scalar", [
                        ([tfr0], lambda s: [s.activation(csec(sfr, m, CH2),
                                                         PSA[:, (m % 4) * 512:(m % 4) * 512 + CH2],
                                                         AF.Sigmoid, bias=0.0,
                                                         scale=1.0)
                                            for m in range(4)][-1], tsf0),
                        ([tfr1], lambda s: [s.activation(csec(sfr, m, CH2),
                                                         PSA[:, (m % 4) * 512:(m % 4) * 512 + CH2],
                                                         AF.Sigmoid, bias=0.0,
                                                         scale=1.0)
                                            for m in range(4, 8)][-1], tsf1),
                    ])

                    ev = [tsf1, tap1b]
                    grp_ev = {0: tsf1, 1: tap1b}
                    tw_prev = {}
                    tsq_last = None
                    for j in range(8):
                        src = NM["f_w_key"][:, j * 512:(j + 1) * 512]
                        src = src.rearrange("(s p) o -> p s o", p=PART)
                        wslice = wstr[:, (j % 2) * 4096:(j % 2) * 4096 + 4096]
                        depw = [tw_prev[j - 2]] if j >= 2 else []
                        twj = ctx.dma("gpsimd", "dg",
                                      wslice.rearrange("p (s o) -> p s o", s=ND),
                                      src, deps=depw + [tfmr])
                        g = j % 2

                        def fkr(t, j=j, wslice=wslice, g=g):
                            for mi in range(4):
                                ps = (PSA if g == 0 else PSB)[:, mi * 512:mi * 512 + CH2]
                                for kt in range(ND):
                                    last = t.matmul(
                                        ps,
                                        wslice[:, kt * 512 + mi * 128:kt * 512 + (mi + 1) * 128],
                                        csec(fmk, kt, CH2), start=(kt == 0),
                                        stop=(kt == ND - 1))
                            return last
                        tfkr = ctx.sec("tensor", fkr,
                                       deps=[twj, tfmk, grp_ev[g]], tag="pe")
                        def rl(v, g=g):
                            ps = PSA if g == 0 else PSB
                            return v.tensor_scalar(relu_f[:], ps[:], 0.0, None,
                                                   ALU.max)
                        trl = ctx.sec("vector", rl, deps=[tfkr], tag="ve")
                        def sq(g, j=j):
                            return g.tensor_mul(
                                relu2[:, j * 4 * CH2:(j + 1) * 4 * CH2],
                                relu_f[:], relu_f[:])
                        tsq_last = ctx.sec("gpsimd", sq, deps=[trl], tag="gp")
                        grp_ev[g] = tsq_last
                        tw_prev[j] = tsq_last

                    tacc = None
                    twv_prev = {}
                    for j in range(8):
                        src = NM["f_w_value"][j * 512:(j + 1) * 512, :]
                        src = src.rearrange("(s p) o -> p s o", p=PART)
                        wslice = wstr[:, (j % 2) * 4096:(j % 2) * 4096 + 4096]
                        depw = [twv_prev[j - 2]] if j >= 2 else [tw_prev[6 + j]]
                        twv = ctx.dma("gpsimd", "dg",
                                      wslice.rearrange("p (s o) -> p s o", s=4),
                                      src, deps=depw)

                        def vkr(t, j=j, wslice=wslice):
                            for m in range(8):
                                for kt in range(4):
                                    last = t.matmul(
                                        slot8(m),
                                        wslice[:, kt * D + m * 128:kt * D + (m + 1) * 128],
                                        relu2[:, (j * 4 + kt) * CH2:(j * 4 + kt + 1) * CH2],
                                        start=(j == 0 and kt == 0),
                                        stop=(j == 7 and kt == 3))
                            return last
                        dep_r = [twv, tsq_last] + ([tacc] if tacc else [])
                        tacc = ctx.sec("tensor", vkr, deps=dep_r, tag="pe")
                        twv_prev[j] = tacc

                    def vke(v):
                        for grp in range(2):
                            ps = PSA if grp == 0 else PSB
                            sfr3 = cmsec(sfr, CH2)[:, grp * 4:grp * 4 + 4, :]
                            rf3 = relu_f[:].rearrange("p (s c) -> p s c", s=4)
                            v.tensor_mul(rf3, sfr3, ps[:].rearrange("p (s c) -> p s c", s=4))
                            last = v.tensor_add(
                                xmsec(t0, CH2)[:, grp * 4:grp * 4 + 4, :],
                                xmsec(t0, CH2)[:, grp * 4:grp * 4 + 4, :], rf3)
                        return last
                    tvke = ctx.sec("vector", vke, deps=[tacc, tsf1], tag="ve")
                    tx0a = ctx.dma("gpsimd", "dg",
                                   x0c2[:].rearrange("p (a s c) -> p a s c", a=2, s=ND)[:, 0],
                                   x0_spill[:, (2 * c) * ND * 256:(2 * c + 1) * ND * 256].rearrange("p (s c) -> p s c", s=ND),
                                   deps=[tacc])
                    tx0b = ctx.dma("gpsimd", "dg",
                                   x0c2[:].rearrange("p (a s c) -> p a s c", a=2, s=ND)[:, 1],
                                   x0_spill[:, (2 * c + 1) * ND * 256:(2 * c + 2) * ND * 256].rearrange("p (s c) -> p s c", s=ND),
                                   deps=[tacc])
                    x0v = x0c2[:].rearrange("p (a s c) -> p a s c", a=2, s=ND)
                    x0m = x0v.transpose(0, 2, 1, 3) if False else None
                    def zadd(v):
                        for half in range(2):
                            last = v.tensor_add(
                                xmsec(t0 + half * 256, 256),
                                xmsec(t0 + half * 256, 256),
                                x0v[:, half])
                        return last
                    tz2 = ctx.sec("vector", zadd, deps=[tx0a, tx0b, tvke], tag="ve")
                    tbc3, rs_b3, mrs_b3, _tc3 = ln_chunk([tz2], xmsec(t0, CH2),
                                                         CH2, fmr, fmk, CH2)
                    def ap3(v):
                        for dt in range(ND):
                            v.tensor_mul(xsec(dt, t0, CH2), xsec(dt, t0, CH2),
                                         rs_b3)
                        for dt in range(ND):
                            last = v.tensor_sub(xsec(dt, t0, CH2),
                                                xsec(dt, t0, CH2), mrs_b3)
                        return last
                    prev_ev = ctx.sec("vector", ap3, deps=[tbc3, tz2], tag="ve")

            ctx.barrier()

            if "x1" in dbg_out:
                ctx.dma("sync", "dq", dbg_out["x1"], XB[:])


        if 2 in stages:
            CHF = 512
            NCF = T // CHF
            with contextlib.ExitStack() as _sf:
                _f = _sf.enter_context
                u1_sb = _f(nc.sbuf_tensor("u1_sb", [PART, ND * 256], BF16))
                v1_sb = _f(nc.sbuf_tensor("v1_sb", [PART, 2 * 4096], BF16))
                u2_sb = _f(nc.sbuf_tensor("u2_sb", [PART, 32 * 256], BF16))
                v2_sb = _f(nc.sbuf_tensor("v2_sb", [PART, 2 * 1024], BF16))
                xbff = _f(nc.sbuf_tensor("xbff", [PART, 2 * ND * CHF], BF16))
                sqbf = _f(nc.sbuf_tensor("sqbf", [PART, ND * CHF], BF16))
                p_bf = _f(nc.sbuf_tensor("p_bf", [PART, 2 * 2 * CHF], BF16))
                h_bf = _f(nc.sbuf_tensor("h_bf", [PART, 2 * 32 * CHF], BF16))
                fwt = []
                for nm, wsb, sct in [("U1", u1_sb, 8), ("V1", v1_sb, 2),
                                     ("U2", u2_sb, 32), ("V2", v2_sb, 2)]:
                    srcw = NM[nm].rearrange("(s p) o -> p s o", p=PART)
                    dstw = wsb[:].rearrange("p (s o) -> p s o", s=sct)
                    fwt.append(ctx.dma("gpsimd", "dg", dstw, srcw))
                fwt.append(load_dcol("b1", NM["b1"], 32))
                fwt.append(load_dcol("b2", NM["b2"]))

                front_done = {}
                back_q_done = {-2: None, -1: None}
                prev_post = [None]

                def front(c):
                    par = c % 2
                    t0 = c * CHF
                    xv = xbff[:, par * ND * CHF:(par + 1) * ND * CHF]
                    pv_ = p_bf[:, par * 2 * CHF:(par + 1) * 2 * CHF]
                    hv = h_bf[:, par * 32 * CHF:(par + 1) * 32 * CHF]
                    dd = fwt + [back_q_done[c - 2], prev_post[0]]
                    tcf = ctx.sec("vector",
                                  lambda v: v.tensor_copy(
                                      xv.rearrange("p (s c) -> p s c", s=ND),
                                      xmsec(t0, CHF)),
                                  deps=dd, tag="ve")

                    def pmm(t):
                        for m in range(2):
                            for kt in range(ND):
                                last = t.matmul(
                                    PSA[:, m * 512:m * 512 + CHF],
                                    u1_sb[:, kt * 256 + m * 128:kt * 256 + (m + 1) * 128],
                                    xv[:, kt * CHF:(kt + 1) * CHF],
                                    start=(kt == 0), stop=(kt == ND - 1))
                        return last
                    dprev = ([front_done[c - 1][0], front_done[c - 1][1]]
                             if (c - 1) in front_done else [])
                    tpm = ctx.sec("tensor", pmm, deps=[tcf] + fwt + dprev,
                                  tag="pe")
                    tpe_ = ctx.sec("vector",
                                   lambda v: v.tensor_copy(
                                       pv_.rearrange("p (s c) -> p s c", s=2),
                                       PSA[:, 0:1024].rearrange("p (s c) -> p s c", s=2)),
                                   deps=[tpm], tag="ve")
                    grp_ev = {0: tpe_, 1: tpe_}
                    hv_steps = []
                    for rnd in range(8):
                        g = rnd % 2
                        def hmm(t, rnd=rnd, g=g, pv_=pv_):
                            for mi in range(4):
                                m = rnd * 4 + mi
                                for kt in range(2):
                                    last = t.matmul(
                                        PSA[:, (g * 2 + mi % 2) * 512:(g * 2 + mi % 2) * 512 + CHF] if False else PSA[:, ((g * 4 + mi) % 4) * 512 // 1:((g * 4 + mi) % 4) * 512 + CHF],
                                        v1_sb[:, kt * 4096 + m * 128:kt * 4096 + (m + 1) * 128],
                                        pv_[:, kt * CHF:(kt + 1) * CHF],
                                        start=(kt == 0), stop=(kt == 1))
                            return last
                        def hmm2(t, rnd=rnd, pv_=pv_):
                            for mi in range(4):
                                m = rnd * 4 + mi
                                for kt in range(2):
                                    last = t.matmul(
                                        PSA[:, mi * 512:mi * 512 + CHF],
                                        v1_sb[:, kt * 4096 + m * 128:kt * 4096 + (m + 1) * 128],
                                        pv_[:, kt * CHF:(kt + 1) * CHF],
                                        start=(kt == 0), stop=(kt == 1))
                            return last
                        thm = ctx.sec("tensor", hmm2,
                                      deps=[tpe_, grp_ev[1 - g]], tag="pe")
                        def hev(s, rnd=rnd, hv=hv):
                            for mi in range(4):
                                m = rnd * 4 + mi
                                last = s.activation(
                                    hv[:, m * CHF:(m + 1) * CHF],
                                    PSA[:, mi * 512:mi * 512 + CHF],
                                    AF.Gelu_apprx_tanh,
                                    bias=CCOL[:, CC["b1"] + m:CC["b1"] + m + 1],
                                    scale=1.0)
                            return last
                        gtok = ctx.reserve("sc")
                        hv_steps.append(([thm], hev, gtok))
                        grp_ev[g] = gtok
                    ctx.mseq("scalar", hv_steps)
                    front_done[c] = grp_ev
                    return grp_ev

                def back(c):
                    par = c % 2
                    t0 = c * CHF
                    hv = h_bf[:, par * 32 * CHF:(par + 1) * 32 * CHF]
                    pv_ = p_bf[:, par * 2 * CHF:(par + 1) * 2 * CHF]
                    ge = front_done[c]

                    def qmm(t):
                        for m in range(2):
                            for kt in range(32):
                                last = t.matmul(
                                    PSB[:, m * 512:m * 512 + CHF],
                                    u2_sb[:, kt * 256 + m * 128:kt * 256 + (m + 1) * 128],
                                    hv[:, kt * CHF:(kt + 1) * CHF],
                                    start=(kt == 0), stop=(kt == 31))
                        return last
                    tqm = ctx.sec("tensor", qmm,
                                  deps=[ge[0], ge[1], prev_post[0]], tag="pe")
                    tqe = ctx.sec("vector",
                                  lambda v: v.tensor_copy(
                                      pv_.rearrange("p (s c) -> p s c", s=2),
                                      PSB[:, 0:1024].rearrange("p (s c) -> p s c", s=2)),
                                  deps=[tqm], tag="ve")
                    back_q_done[c] = tqe
                    tout_ev = tqe
                    for rnd in range(2):
                        def omm(t, rnd=rnd, pv_=pv_):
                            for mi in range(4):
                                m = rnd * 4 + mi
                                for kt in range(2):
                                    last = t.matmul(
                                        PSB[:, mi * 512:mi * 512 + CHF],
                                        v2_sb[:, kt * 1024 + m * 128:kt * 1024 + (m + 1) * 128],
                                        pv_[:, kt * CHF:(kt + 1) * CHF],
                                        start=(kt == 0), stop=(kt == 1))
                            return last
                        tom = ctx.sec("tensor", omm, deps=[tout_ev], tag="pe")
                        def oev(v, rnd=rnd):
                            for mi in range(4):
                                m = rnd * 4 + mi
                                last = v.scalar_tensor_tensor(
                                    xsec(m, t0, CHF),
                                    PSB[:, mi * 512:mi * 512 + CHF],
                                    CCOL[:, CC["b2"] + m:CC["b2"] + m + 1],
                                    xsec(m, t0, CHF), ALU.add, ALU.add)
                            return last
                        tout_ev = ctx.sec("vector", oev, deps=[tom], tag="ve")
                    xv = xbff[:, par * ND * CHF:(par + 1) * ND * CHF]
                    tbcf, rs_bf_, mrs_bf_, _tcf2 = ln_chunk(
                        [tout_ev], xmsec(t0, CHF), CHF, xv, sqbf, CHF)
                    def apf(v):
                        for dt in range(ND):
                            v.tensor_mul(xsec(dt, t0, CHF), xsec(dt, t0, CHF),
                                         rs_bf_)
                        for dt in range(ND):
                            last = v.tensor_sub(xsec(dt, t0, CHF),
                                                xsec(dt, t0, CHF), mrs_bf_)
                        return last
                    prev_post[0] = ctx.sec("vector", apf, deps=[tbcf, tout_ev],
                                           tag="ve")
                    return prev_post[0]

                for c in range(NCF + 1):
                    if c < NCF:
                        front(c)
                    if c >= 1:
                        back(c - 1)
            ctx.barrier()

        if 3 in stages:
            CHA = 512
            NH = 16
            wrow_bf = nc.dram_tensor("wrow_bf", [4 * D, D], BF16).ap()
            with contextlib.ExitStack() as _sa:
                _f = _sa.enter_context
                wpc = _f(nc.sbuf_tensor("wpc", [PART, ND * 1024], BF16))
                woT = _f(nc.sbuf_tensor("woT", [PART, ND * 1024], BF16))
                KTb = _f(nc.sbuf_tensor("KTb", [PART, ND * T], BF16))
                V65 = _f(nc.sbuf_tensor("V65", [PART, NTT * NH * 65], BF16))
                probs = _f(nc.sbuf_tensor("probs", [PART, 8 * CHA], BF16))
                qtb = _f(nc.sbuf_tensor("qtb", [PART, ND * CHA], BF16))
                xab = _f(nc.sbuf_tensor("xab", [PART, ND * CHA], BF16))
                outT = _f(nc.sbuf_tensor("outT", [PART, ND * CHA], BF16))
                todd = xab
                rdsb = _f(nc.sbuf_tensor("rdsb", [PART, CHA], F32))
                corr = _f(nc.sbuf_tensor("corr", [PART, ND], F32))
                bvb = _f(nc.sbuf_tensor("bvb", [PART, ND], BF16))

                tbounce = []
                tpc = {}
                for wi, wname in enumerate(("Wq", "Wk", "Wv", "Wo")):
                    tw = None
                    for kt in range(ND):
                        idx = wi * ND + kt
                        half = wpc[:, (idx % 2) * 1024:(idx % 2) * 1024 + 1024]
                        dep = [tpc[idx - 2]] if idx >= 2 else []
                        tcast = ctx.dma("gpsimd", "dg", half,
                                        NM[wname][kt * 128:(kt + 1) * 128, :],
                                        deps=dep)
                        tw = ctx.dma("sync", "dq",
                                     wrow_bf[wi * D + kt * 128:wi * D + (kt + 1) * 128, :],
                                     half, deps=[tcast])
                        tpc[idx] = tw
                    tbounce.append(tw)
                t_bq = load_dcol("bq", NM["bq"])
                t_bk = load_dcol("bk", NM["bk"])
                t_bo = load_dcol("bo", NM["bo"])
                t_bv = ctx.dma("gpsimd", "dg", bvb[:],
                               NM["bv"].rearrange("(s p) -> p s", p=PART),
                               slow=True)

                def wpiece(wi, kt, deps, buf=None):
                    b = wpc if buf is None else buf
                    dst = b[:, kt * 1024:(kt + 1) * 1024]
                    srcp = wrow_bf[wi * D:(wi + 1) * D, kt * 128:(kt + 1) * 128]
                    return ctx.dma("sync", "dq", dst, srcp, deps=deps,
                                   transpose=True)

                def wload_full(wi, deps, buf=None):
                    toks = [wpiece(wi, kt, deps if kt == 0 else [], buf=buf)
                            for kt in range(ND)]
                    return toks[-1]

                prev_use = {}
                def proj_dmajor(widx, dstbuf, biasname, extra_dep):
                    toks = []
                    tw = {}
                    for kt in range(ND):
                        dep = [tbounce[widx]] + ([tw[kt - 2]] if kt >= 2 else [])
                        tw[kt] = wpiece(widx, kt, kt % 2, dep)
                    return None
                def proj_full(widx, bias_ap_fn, dst_write, dep0, moving_buf,
                              mtiles=8):
                    twf = wload_full(widx, [tbounce[widx], dep0])
                    tprev_ev = dep0
                    for c in range(T // CHA):
                        t0 = c * CHA
                        tcx = ctx.sec("vector",
                                      lambda v: v.tensor_copy(cmsec(moving_buf, CHA),
                                                              xmsec(t0, CHA)),
                                      deps=[dep0, tprev_ev], tag="ve")

                        def pm(t):
                            for kt in range(ND):
                                for m in range(mtiles):
                                    last = t.matmul(
                                        (PSA if m < 4 else PSB)[:, (m % 4) * 512:(m % 4) * 512 + CHA],
                                        wpc[:, kt * 1024 + m * 128:kt * 1024 + (m + 1) * 128],
                                        csec(moving_buf, kt, CHA),
                                        start=(kt == 0), stop=(kt == ND - 1))
                            return last
                        tmmp = ctx.sec("tensor", pm, deps=[twf, tcx, tprev_ev],
                                       tag="pe")
                        tprev_ev = dst_write(c)
                    return tprev_ev

                def k_write(c):
                    def ev(s):
                        for m in range(ND):
                            ps = (PSA if m < 4 else PSB)[:, (m % 4) * 512:(m % 4) * 512 + CHA]
                            last = s.activation(
                                KTb[:, m * T + c * CHA:m * T + (c + 1) * CHA],
                                ps, AF.Identity,
                                bias=ccol("bk", m), scale=1.0)
                        return last
                    return ctx.sec("scalar", ev, deps=[ctx.tok("pe")], tag="sc")
                tk_done = proj_full(1, None, k_write, t_bk, xab)
                if "ktb" in dbg_out:
                    tk_done = ctx.dma("sync", "dq", dbg_out["ktb"], KTb[:],
                                      deps=[tk_done])

                twvf = wload_full(2, [tbounce[2], tk_done])
                tprev_v = tk_done
                twv = {}
                for c in range(T // CHA):
                    t0 = c * CHA
                    tcx = ctx.sec("vector",
                                  lambda v: v.tensor_copy(cmsec(xab, CHA),
                                                          xmsec(t0, CHA)),
                                  deps=[tprev_v], tag="ve")

                    def vm(t):
                        for kt in range(ND):
                            for ti in range(4):
                                for ec in range(2):
                                    last = t.matmul(
                                        (PSA if ti < 2 else PSB)[:, ((ti % 2) * 2 + ec) * 512:((ti % 2) * 2 + ec) * 512 + 512],
                                        xab[:, kt * CHA + ti * 128:kt * CHA + (ti + 1) * 128],
                                        wpc[:, kt * 1024 + ec * 512:kt * 1024 + (ec + 1) * 512],
                                        start=(kt == 0), stop=(kt == ND - 1))
                        return last
                    twv[c * ND + ND - 1] = ctx.sec("tensor", vm,
                                                   deps=[twvf, tcx], tag="pe")

                    def vev(v, c=c):
                        for ti in range(4):
                            tt = c * 4 + ti
                            for ec in range(2):
                                ps = (PSA if ti < 2 else PSB)[:, ((ti % 2) * 2 + ec) * 512:((ti % 2) * 2 + ec) * 512 + 512]
                                dst = V65[:, tt * NH * 65 + ec * 8 * 65:]
                                dst = dst.rearrange("p (h e) -> p h e", h=8)[:, :, 0:64] if False else V65[:].rearrange(
                                    "p (a h e) -> p a h e", a=NTT, h=NH)[:, tt, ec * 8:(ec + 1) * 8, 0:64]
                                last = v.tensor_copy(dst,
                                                     ps.rearrange("p (h e) -> p h e", h=8))
                        return last
                    tprev_v = ctx.sec("vector", vev, deps=[twv[c * ND + ND - 1]],
                                      tag="ve")
                tones = ctx.sec("vector",
                                lambda v: v.memset(
                                    V65[:].rearrange("p (a h e) -> p a h e",
                                                     a=NTT, h=NH)[:, :, :, 64:65],
                                    1.0),
                                deps=[], tag="ve")

                twoT = wload_full(3, [tbounce[3], tprev_v], buf=woT)
                tzrow = ctx.sec("vector",
                                lambda v: v.memset(ROWS[0:1, 1040:1048], 0.0),
                                deps=[], tag="ve")
                tprime = ctx.sec("tensor",
                                 lambda t: t.matmul(PSB[:, 1536:1536 + ND],
                                                    CROWF[:],
                                                    ROWS[0:1, 1040:1048],
                                                    start=True, stop=False),
                                 deps=[tzrow, tprev_v], tag="pe")

                def cm(t):
                    for kt in range(ND):
                        for m in range(ND):
                            last = t.matmul(PSB[0:PART, 1536 + m:1536 + m + 1],
                                            woT[:, kt * 1024 + m * 128:kt * 1024 + (m + 1) * 128][:, 0:1] if False else woT[:, kt * 1024 + m * 128:kt * 1024 + (m + 1) * 128],
                                            bvb[:, kt:kt + 1],
                                            start=False, stop=(kt == ND - 1))
                    return last
                tcorr_mm = ctx.sec("tensor", cm, deps=[twoT, tprime, t_bv],
                                   tag="pe")
                tcorr = ctx.sec("vector",
                                lambda v: v.tensor_copy(corr[:],
                                                        PSB[:, 1536:1536 + ND]),
                                deps=[tcorr_mm], tag="ve")
                tcorr2 = ctx.sec("vector",
                                 lambda v: [v.tensor_add(corr[:, m:m + 1],
                                                         corr[:, m:m + 1],
                                                         ccol("bo", m))
                                            for m in range(ND)][-1],
                                 deps=[tcorr, t_bo], tag="ve")

                if "corr" in dbg_out:
                    tcorr2 = ctx.dma("sync", "dq", dbg_out["corr"], corr[:],
                                     deps=[tcorr2])
                twqf = wload_full(0, [tbounce[0], tcorr2])
                prev_main = tcorr2
                for c in range(T // CHA):
                    t0 = c * CHA
                    tcq = ctx.sec("vector",
                                  lambda v: v.tensor_copy(cmsec(xab, CHA),
                                                          xmsec(t0, CHA)),
                                  deps=[prev_main], tag="ve")
                    def qm(t):
                        for kt in range(ND):
                            for m in range(ND):
                                last = t.matmul(
                                    (PSA if m < 4 else PSB)[:, (m % 4) * 512:(m % 4) * 512 + CHA],
                                    wpc[:, kt * 1024 + m * 128:kt * 1024 + (m + 1) * 128],
                                    csec(xab, kt, CHA),
                                    start=(kt == 0), stop=(kt == ND - 1))
                        return last
                    twq = {ND - 1: ctx.sec("tensor", qm, deps=[twqf, tcq], tag="pe")}
                    tqe = ctx.sec("scalar",
                                  lambda s: [s.activation(
                                      csec(qtb, m, CHA),
                                      (PSA if m < 4 else PSB)[:, (m % 4) * 512:(m % 4) * 512 + CHA],
                                      AF.Identity, bias=ccol("bq", m), scale=1.0)
                                      for m in range(ND)][-1],
                                  deps=[twq[ND - 1], t_bq], tag="sc")
                    if c == 0 and "qtb" in dbg_out:
                        tqe = ctx.dma("sync", "dq", dbg_out["qtb"], qtb[:],
                                      deps=[tqe])

                    ev_odd = []
                    tprev_pv = tqe
                    tprev_exp = tqe
                    tprev_cpd = [None]
                    fin_par = {0: tqe, 1: tqe}
                    pv_done = {0: None, 1: None}
                    sc_steps = []
                    for h in range(NH):
                        sec_ = h // 2
                        pb = 64 * (h % 2)
                        pvb = (h % 2) * 512
                        tgrp = {}
                        tpv = None
                        for grp in range(4):
                            def sm(t, grp=grp, sec_=sec_, pb=pb):
                                for i in range(4):
                                    kt = grp * 4 + i
                                    last = t.matmul(
                                        PSA[:, i * 512:i * 512 + CHA],
                                        KTb[pb:pb + 64, sec_ * T + kt * 128:sec_ * T + (kt + 1) * 128],
                                        qtb[pb:pb + 64, sec_ * CHA:(sec_ + 1) * CHA],
                                        start=True, stop=True)
                                return last
                            dep_s = [tqe, tk_done] + ([tgrp[grp - 1]] if grp else [tprev_exp])
                            if grp == 3:
                                dep_s.append(tprev_cpd[0])
                            tsg = ctx.sec("tensor", sm, deps=dep_s, tag="pe")
                            texp = ctx.reserve("sc")
                            sc_steps.append(([tsg] + ([pv_done[grp % 2]] if pv_done[grp % 2] else []),
                                             (lambda s, grp=grp: s.activation(
                                                 probs[:, (grp % 2) * 4 * CHA:((grp % 2) + 1) * 4 * CHA],
                                                 PSA[:], AF.Exp, bias=0.0,
                                                 scale=0.125)), texp))
                            tgrp[grp] = texp
                            def pvg(t, h=h, pvb=pvb, grp=grp):
                                for i in range(4):
                                    kt = grp * 4 + i
                                    last = t.matmul(
                                        PSB[0:65, pvb:pvb + CHA],
                                        V65[:, kt * NH * 65 + h * 65:kt * NH * 65 + (h + 1) * 65],
                                        probs[:, ((grp % 2) * 4 + i) * CHA:((grp % 2) * 4 + i + 1) * CHA],
                                        start=(grp == 0 and i == 0),
                                        stop=(grp == 3 and i == 3))
                                return last
                            dpv = [texp, tones, tprev_v]
                            if grp == 0:
                                dpv.append(fin_par[h % 2])
                            if tpv is not None:
                                dpv.append(tpv)
                            tpv = ctx.sec("tensor", pvg, deps=dpv, tag="pe")
                            pv_done[grp % 2] = tpv
                        tprev_exp = tgrp[3]
                        r_rd = ROWS[0:1, 512:512 + CHA]
                        trd2 = ctx.reserve("sc")
                        def rden_fn(s, r_rd=r_rd, pvb=pvb):
                            s.activation(r_rd, PSB[64:65, pvb:pvb + CHA], AF.Ln,
                                         bias=0.0, scale=1.0)
                            return s.activation(r_rd, r_rd, AF.Exp, bias=0.0,
                                                scale=-1.0)
                        sc_steps.append(([tpv], rden_fn, trd2))
                        tbcd = ctx.sec("tensor",
                                       lambda t: t.matmul(PSA[0:64, 1536:1536 + CHA],
                                                          CROWF[0:1, 0:64], r_rd,
                                                          start=True, stop=True),
                                       deps=[trd2, tprev_cpd[0]], tag="pe")
                        tcpd = ctx.sec("vector",
                                       lambda v: v.tensor_copy(rdsb[0:64, :],
                                                               PSA[0:64, 1536:1536 + CHA]),
                                       deps=[tbcd], tag="ve")
                        tprev_cpd[0] = tcpd
                        if h % 2 == 0:
                            def fin(v, sec_=sec_, pvb=pvb):
                                return v.tensor_mul(
                                    outT[0:64, sec_ * CHA:(sec_ + 1) * CHA],
                                    PSB[0:64, pvb:pvb + CHA], rdsb[0:64, :])
                        else:
                            def fin(v, sec_=sec_, pvb=pvb):
                                return v.tensor_mul(
                                    todd[0:64, sec_ * CHA:(sec_ + 1) * CHA],
                                    PSB[0:64, pvb:pvb + CHA], rdsb[0:64, :])
                        tprev_pv = ctx.sec("vector", fin, deps=[tcpd, tpv],
                                           tag="ve")
                        fin_par[h % 2] = tprev_pv
                        if h % 2 == 1:
                            ev_odd.append(tprev_pv)
                    ctx.mseq("scalar", sc_steps)
                    tmv_odd = ctx.dma("gpsimd", "dg",
                                      outT[64:128, :].rearrange("p (s c) -> p s c", s=ND),
                                      todd[0:64, :].rearrange("p (s c) -> p s c", s=ND),
                                      deps=ev_odd)
                    if c == 0 and "outT" in dbg_out:
                        tmv_odd = ctx.dma("sync", "dq", dbg_out["outT"], outT[:],
                                          deps=[tmv_odd, tprev_pv])
                    def om(t):
                        for kt in range(ND):
                            for m in range(ND):
                                last = t.matmul(
                                    (PSA if m < 4 else PSB)[:, (m % 4) * 512:(m % 4) * 512 + CHA],
                                    woT[:, kt * 1024 + m * 128:kt * 1024 + (m + 1) * 128],
                                    csec(outT, kt, CHA),
                                    start=(kt == 0), stop=(kt == ND - 1))
                        return last
                    two = {ND - 1: ctx.sec("tensor", om,
                                           deps=[twoT, tmv_odd, tprev_pv],
                                           tag="pe")}
                    tfin = ctx.sec("vector",
                                   lambda v: [v.scalar_tensor_tensor(
                                       xsec(m, t0, CHA),
                                       (PSA if m < 4 else PSB)[:, (m % 4) * 512:(m % 4) * 512 + CHA],
                                       corr[:, m:m + 1], xsec(m, t0, CHA),
                                       ALU.add, ALU.add) for m in range(ND)][-1],
                                   deps=[two[ND - 1], tcorr2], tag="ve")
                    tbca, rs_ba, mrs_ba, _tca = ln_chunk([tfin], xmsec(t0, CHA),
                                                         CHA, xab, qtb, CHA)
                    def apa(v):
                        for dt in range(ND):
                            v.tensor_mul(xsec(dt, t0, CHA), xsec(dt, t0, CHA),
                                         rs_ba)
                        for dt in range(ND):
                            last = v.tensor_sub(xsec(dt, t0, CHA),
                                                xsec(dt, t0, CHA), mrs_ba)
                        return last
                    prev_main = ctx.sec("vector", apa, deps=[tbca, tfin], tag="ve")
            ctx.barrier()


        if 4 in stages:
            CH4 = 512
            NHL = 8
            HIDT = 22
            with contextlib.ExitStack() as _s4a:
                _f = _s4a.enter_context
                qsc = _f(nc.sbuf_tensor("qsc", [PART, 4], F32))
                mP1 = _f(nc.sbuf_tensor("mP1", [PART, PART], BF16))
                mP2 = _f(nc.sbuf_tensor("mP2", [PART, PART], BF16))
                bndX = _f(nc.sbuf_tensor("bndX", [PART, ND], F32))

                tmsk = [ctx.dma("sync", "dq", mP1[:], NM["c_maskP1"]),
                        ctx.dma("sync", "dq", mP2[:], NM["c_maskP2"])]

                prev_l = None
                _layers = [0] if sub4 in ("a0", "f0") else [0, 1]
                for l in _layers:
                    _satt = contextlib.ExitStack()
                    def _ab(name, shape, dt_):
                        return _satt.enter_context(
                            nc.sbuf_tensor(name + "_%d" % l, shape, dt_))
                    wqkv_sb = _ab("wqkv_sb", [PART, ND * 1536], BF16)
                    wol_sb = _ab("wol_sb", [PART, 4 * D], BF16)
                    Qf = _ab("Qf", [PART, 4 * CH4], F32)
                    Kf = _ab("Kf", [PART, 4 * CH4], F32)
                    Qsw = _ab("Qsw", [PART, 4 * CH4], F32)
                    QTr = _ab("QTr", [PART, 4 * CH4], BF16)
                    KTr = _ab("KTr", [PART, 4 * T], BF16)
                    V65L = _ab("V65L", [PART, NTT * NHL * 65], BF16)
                    hbf4 = _ab("hbf4", [PART, ND * CH4], BF16)
                    probsL = _ab("probsL", [PART, 5 * 128], BF16)
                    outTL = _ab("outTL", [PART, 4 * CH4], BF16)
                    toddL = _ab("toddL", [PART, 4 * CH4], BF16)
                    rdsbL = _ab("rdsbL", [PART, CH4], F32)
                    tbl = _ab("tbl", [PART, 4 * CH4], F32)
                    tones4 = ctx.sec("vector",
                                     lambda v: v.memset(
                                         V65L[:].rearrange("p (a h e) -> p a h e",
                                                           a=NTT, h=NHL)[:, :, :, 64:65],
                                         1.0),
                                     deps=[prev_l] if prev_l else [], tag="ve")
                    twq4 = ctx.dma("gpsimd", "dg",
                                   wqkv_sb[:].rearrange("p (s o) -> p s o", s=ND),
                                   NM["wqkv"][l].rearrange("(s p) o -> p s o", p=PART),
                                   deps=[prev_l] if prev_l else [])
                    two4 = ctx.dma("gpsimd", "dg",
                                   wol_sb[:].rearrange("p (s o) -> p s o", s=4),
                                   NM["l1wo"][l].rearrange("(s p) o -> p s o", p=PART),
                                   deps=[prev_l] if prev_l else [])
                    tql = []
                    for nm, col in (("q_scale", 0), ("k_scale", 2)):
                        for half in range(2):
                            tql.append(ctx.dma("sync", "dq",
                                               qsc[half * 64:(half + 1) * 64, col:col + 1],
                                               NM[nm][l:l + 1, :].rearrange("a b -> b a"),
                                               deps=[prev_l] if prev_l else [],
                                               slow=True))
                        for qtr in range(4):
                            s0q = 32 if qtr % 2 == 0 else 0
                            tql.append(ctx.dma("sync", "dq",
                                               qsc[qtr * 32:(qtr + 1) * 32, col + 1:col + 2],
                                               NM[nm][l:l + 1, s0q:s0q + 32].rearrange("a b -> b a"),
                                               deps=[prev_l] if prev_l else [],
                                               slow=True))
                    tlg = load_dcol("lng_%d" % l, NM["ln_g"][l])
                    tlb = load_dcol("lnb_%d" % l, NM["ln_b"][l])
                    tb1 = load_dcol("b1p_%d" % l, NM["l1b1"][l], 44)
                    tb2 = load_dcol("b2p_%d" % l, NM["l1b2"][l])
                    twl = [twq4, two4, tlg, tlb, tb1, tb2] + tql + tmsk

                    prev_ch = prev_l
                    for c in range(4):
                        t0 = c * CH4
                        dd4 = twl + ([prev_ch] if prev_ch else [])
                        def shcast(v):
                            r = hbf4[:].rearrange("p (s c) -> p s c", s=ND)
                            v.tensor_copy(r[:, 0:4, :], xmsec(t0, CH4)[:, 0:4, :])
                            last = v.tensor_copy(r[:, 4:8, :],
                                                 xmsec(t0, CH4, shift=1)[:, 4:8, :])
                            if c > 0:
                                last = v.tensor_copy(r[:, 4:8, 0:1].squeeze(),
                                                     bndX[:, 4:8])
                            return last
                        thc = ctx.sec("vector", shcast, deps=dd4, tag="ve")
                        def qkmm(t):
                            for m in range(8):
                                for kt in range(ND):
                                    last = t.matmul(
                                        (PSA if m < 4 else PSB)[:, (m % 4) * 512:(m % 4) * 512 + CH4],
                                        wqkv_sb[:, kt * 1536 + m * 128:kt * 1536 + (m + 1) * 128],
                                        csec(hbf4, kt, CH4),
                                        start=(kt == 0), stop=(kt == ND - 1))
                            return last
                        tqk = ctx.sec("tensor", qkmm, deps=[thc] + dd4, tag="pe")
                        tqke = ctx.sec("vector",
                                       lambda v: [v.tensor_copy(
                                           (Qf if m < 4 else Kf)[:, (m % 4) * CH4:(m % 4 + 1) * CH4],
                                           (PSA if m < 4 else PSB)[:, (m % 4) * 512:(m % 4) * 512 + CH4])
                                           for m in range(8)][-1],
                                       deps=[tqk], tag="ve")
                        def vmm(t):
                            for ti in range(4):
                                for kt in range(ND):
                                    last = t.matmul(
                                        PSA[:, ti * 512:ti * 512 + 512],
                                        hbf4[:, kt * CH4 + ti * 128:kt * CH4 + (ti + 1) * 128],
                                        wqkv_sb[:, kt * 1536 + 1024:kt * 1536 + 1536],
                                        start=(kt == 0), stop=(kt == ND - 1))
                            return last
                        tvm = ctx.sec("tensor", vmm, deps=[tqke], tag="pe")
                        def vev4(v):
                            for ti in range(4):
                                tt = c * 4 + ti
                                dst = V65L[:].rearrange("p (a h e) -> p a h e",
                                                        a=NTT, h=NHL)[:, tt, :, 0:64]
                                last = v.tensor_copy(
                                    dst, PSA[:, ti * 512:ti * 512 + 512].rearrange("p (h e) -> p h e", h=NHL))
                            return last
                        tve4 = ctx.sec("vector", vev4, deps=[tvm], tag="ve")

                        def ss_rn(buf, dep_in):
                            tsqr = ctx.sec("vector",
                                           lambda v: v.tensor_mul(Qsw[:], buf[:],
                                                                  buf[:]),
                                           deps=[dep_in], tag="ve")
                            tss_prev = tsqr
                            for dt in range(4):
                                bk = (dt % 2)
                                tssm = ctx.sec("tensor",
                                               lambda t, dt=dt, bk=bk: t.matmul(
                                                   PSB[0:2, bk * 512:bk * 512 + CH4],
                                                   E2S[:], Qsw[:, dt * CH4:(dt + 1) * CH4],
                                                   start=True, stop=True),
                                               deps=[tss_prev, tve4], tag="pe")
                                tss_prev = ctx.sec("vector",
                                                   lambda v, dt=dt, bk=bk: v.tensor_copy(
                                                       ROWS[0:2, dt * CH4:(dt + 1) * CH4],
                                                       PSB[0:2, bk * 512:bk * 512 + CH4]),
                                                   deps=[tssm], tag="ve")
                            tmx4 = ctx.sec("vector",
                                           lambda v: v.tensor_scalar(
                                               ROWS[0:2, 0:2048], ROWS[0:2, 0:2048],
                                               1e-24, None, ALU.max),
                                           deps=[tss_prev], tag="ve")
                            def lnexp(s):
                                s.activation(ROWS[0:2, 0:2048], ROWS[0:2, 0:2048],
                                             AF.Ln, bias=0.0, scale=1.0)
                                return s.activation(ROWS[0:2, 0:2048],
                                                    ROWS[0:2, 0:2048],
                                                    AF.Exp, bias=0.0, scale=-0.5)
                            return ctx.sec("scalar", lnexp, deps=[tmx4], tag="sc")
                        tex4 = ss_rn(Qf, tqke)

                        def rot(tensor_j, srcf, swp, dstb, t1nm, t2nm, scol,
                                dep_in):
                            tt1 = ctx.dma("sync", "dq", tbl[:, 0:CH4],
                                          NM[t1nm][:, t0:t0 + CH4], deps=dep_in)
                            tt2 = ctx.dma("sync", "dq", tbl[:, CH4:2 * CH4],
                                          NM[t2nm][:, t0:t0 + CH4], deps=dep_in)
                            te1 = ctx.sec("vector",
                                          lambda v: v.tensor_scalar(
                                              tbl[:, 2 * CH4:3 * CH4],
                                              tbl[:, 0:CH4],
                                              qsc[:, scol:scol + 1], None,
                                              ALU.mult),
                                          deps=[tt1] + tql, tag="ve")
                            te2 = ctx.sec("vector",
                                          lambda v: v.tensor_scalar(
                                              tbl[:, 3 * CH4:4 * CH4],
                                              tbl[:, CH4:2 * CH4],
                                              qsc[:, scol + 1:scol + 2], None,
                                              ALU.mult),
                                          deps=[tt2] + tql, tag="ve")
                            tsw = []
                            for bb, (d0, s0) in enumerate([(0, 32), (32, 0),
                                                           (64, 96), (96, 64)]):
                                tsw.append(ctx.dma("gpsimd", "dg",
                                                   swp[d0:d0 + 32, :],
                                                   srcf[s0:s0 + 32, :],
                                                   deps=dep_in))
                            t1b = tbl[:, 2 * CH4:3 * CH4].unsqueeze(1).broadcast_to([PART, 4, CH4])
                            t2b = tbl[:, 3 * CH4:4 * CH4].unsqueeze(1).broadcast_to([PART, 4, CH4])
                            ta = ctx.sec("vector",
                                         lambda v: v.tensor_mul(
                                             srcf[:].rearrange("p (s c) -> p s c", s=4),
                                             srcf[:].rearrange("p (s c) -> p s c", s=4),
                                             t1b),
                                         deps=[te1] + tsw, tag="ve")
                            tb_ = ctx.sec("gpsimd",
                                          lambda g: g.tensor_mul(
                                              swp[:].rearrange("p (s c) -> p s c", s=4),
                                              swp[:].rearrange("p (s c) -> p s c", s=4),
                                              t2b),
                                          deps=[te2] + tsw, tag="gp")
                            tc_ = ctx.sec("vector",
                                          lambda v: v.tensor_add(
                                              srcf[:], srcf[:], swp[:]),
                                          deps=[ta, tb_], tag="ve")
                            tprev = tc_
                            for dt in range(4):
                                bk2 = 2 + (dt % 2)
                                rowb = 0 if tensor_j == 0 else 32
                                tbc4 = ctx.sec("tensor",
                                               lambda t, dt=dt, bk2=bk2, rowb=rowb: t.matmul(
                                                   PSB[:, bk2 * 512:bk2 * 512 + CH4],
                                                   E2XS[:],
                                                   ROWS[rowb:rowb + 2, dt * CH4:(dt + 1) * CH4],
                                                   start=True, stop=True),
                                               deps=[tex4, tprev], tag="pe")
                                tprev = ctx.sec("vector",
                                                lambda v, dt=dt, bk2=bk2: v.tensor_mul(
                                                    dstb[:, dt * CH4:(dt + 1) * CH4],
                                                    srcf[:, dt * CH4:(dt + 1) * CH4],
                                                    PSB[:, bk2 * 512:bk2 * 512 + CH4]),
                                                deps=[tbc4], tag="ve")
                            return tprev

                        trq = rot(0, Qf, Qsw, QTr, "c_T1Q", "c_T2Q", 0,
                                  [tqke, tex4])
                        if l == 0 and c == 0 and "qtr4" in dbg_out:
                            trq = ctx.dma("sync", "dq", dbg_out["qtr4"], QTr[:],
                                          deps=[trq])
                        tex4k = ss_rn(Kf, trq)
                        tt1k = ctx.dma("sync", "dq", tbl[:, 0:CH4],
                                       NM["c_T1K"][:, t0:t0 + CH4], deps=[trq])
                        tt2k = ctx.dma("sync", "dq", tbl[:, CH4:2 * CH4],
                                       NM["c_T2K"][:, t0:t0 + CH4], deps=[trq])
                        te1k = ctx.sec("vector",
                                       lambda v: v.tensor_scalar(
                                           tbl[:, 2 * CH4:3 * CH4], tbl[:, 0:CH4],
                                           qsc[:, 2:3], None, ALU.mult),
                                       deps=[tt1k] + tql, tag="ve")
                        te2k = ctx.sec("vector",
                                       lambda v: v.tensor_scalar(
                                           tbl[:, 3 * CH4:4 * CH4],
                                           tbl[:, CH4:2 * CH4],
                                           qsc[:, 3:4], None, ALU.mult),
                                       deps=[tt2k] + tql, tag="ve")
                        tswk = []
                        for bb, (d0, s0) in enumerate([(0, 32), (32, 0),
                                                       (64, 96), (96, 64)]):
                            tswk.append(ctx.dma("gpsimd", "dg",
                                                Qsw[d0:d0 + 32, :],
                                                Kf[s0:s0 + 32, :],
                                                deps=[tex4k]))
                        t1bk = tbl[:, 2 * CH4:3 * CH4].unsqueeze(1).broadcast_to([PART, 4, CH4])
                        t2bk = tbl[:, 3 * CH4:4 * CH4].unsqueeze(1).broadcast_to([PART, 4, CH4])
                        tak = ctx.sec("vector",
                                      lambda v: v.tensor_mul(
                                          Kf[:].rearrange("p (s c) -> p s c", s=4),
                                          Kf[:].rearrange("p (s c) -> p s c", s=4),
                                          t1bk),
                                      deps=[te1k] + tswk, tag="ve")
                        tbk_ = ctx.sec("gpsimd",
                                       lambda g: g.tensor_mul(
                                           Qsw[:].rearrange("p (s c) -> p s c", s=4),
                                           Qsw[:].rearrange("p (s c) -> p s c", s=4),
                                           t2bk),
                                       deps=[te2k] + tswk, tag="gp")
                        tck = ctx.sec("vector",
                                      lambda v: v.tensor_add(Kf[:], Kf[:], Qsw[:]),
                                      deps=[tak, tbk_], tag="ve")
                        tprevk = tck
                        for dt in range(4):
                            bk2 = 2 + (dt % 2)
                            tbc4k = ctx.sec("tensor",
                                            lambda t, dt=dt, bk2=bk2: t.matmul(
                                                PSB[:, bk2 * 512:bk2 * 512 + CH4],
                                                E2XS[:],
                                                ROWS[0:2, dt * CH4:(dt + 1) * CH4],
                                                start=True, stop=True),
                                            deps=[tex4k, tprevk], tag="pe")
                            tprevk = ctx.sec("vector",
                                             lambda v, dt=dt, bk2=bk2: v.tensor_mul(
                                                 KTr[:, dt * T + t0:dt * T + t0 + CH4],
                                                 Kf[:, dt * CH4:(dt + 1) * CH4],
                                                 PSB[:, bk2 * 512:bk2 * 512 + CH4]),
                                             deps=[tbc4k], tag="ve")

                        if l == 0 and c == 0 and "ktr4" in dbg_out:
                            tprevk = ctx.dma("sync", "dq", dbg_out["ktr4"],
                                             KTr[:], deps=[tprevk])
                        tprev_exp4 = tprevk
                        tprev_fin4 = trq
                        ev_odd4 = []
                        sc4_steps = []
                        for h in range(NHL):
                            sec4 = h // 2
                            pb4 = 64 * (h % 2)
                            tpv4 = None
                            for qt in range(4):
                                a = c * 4 + qt
                                kts = list(range(max(0, a - 4), a + 1))
                                def sm4(t, a=a, kts=kts, sec4=sec4, pb4=pb4, qt=qt):
                                    for i, kt in enumerate(kts):
                                        last = t.matmul(
                                            PSA[:, i * 128:(i + 1) * 128],
                                            KTr[pb4:pb4 + 64, sec4 * T + kt * 128:sec4 * T + (kt + 1) * 128],
                                            QTr[pb4:pb4 + 64, sec4 * CH4 + qt * 128:sec4 * CH4 + (qt + 1) * 128],
                                            start=True, stop=True)
                                    return last
                                dep_s4 = [tprevk, trq] + ([tprev_exp4] if tprev_exp4 else []) + ([tprev_fin4] if tprev_fin4 else [])
                                tsg4 = ctx.sec("tensor", sm4, deps=dep_s4, tag="pe")
                                nk = len(kts)
                                texp4 = ctx.reserve("sc")
                                sc4_steps.append(([tsg4],
                                                  (lambda s, nk=nk: s.activation(
                                                      probsL[:, 0:nk * 128],
                                                      PSA[:, 0:nk * 128], AF.Exp,
                                                      bias=0.0, scale=8.0)),
                                                  texp4))
                                def msk(v, nk=nk, a=a):
                                    first = None
                                    if a >= 4:
                                        first = v.tensor_mul(probsL[:, 0:128],
                                                             probsL[:, 0:128],
                                                             mP2[:])
                                    last = v.tensor_mul(
                                        probsL[:, (nk - 1) * 128:nk * 128],
                                        probsL[:, (nk - 1) * 128:nk * 128],
                                        mP1[:])
                                    return last
                                tmk4 = ctx.sec("vector", msk,
                                               deps=[texp4] + tmsk, tag="ve")
                                def pv4(t, kts=kts, h=h, qt=qt):
                                    for i, kt in enumerate(kts):
                                        last = t.matmul(
                                            PSB[0:65, qt * 512:qt * 512 + 128],
                                            V65L[:, kt * NHL * 65 + h * 65:kt * NHL * 65 + (h + 1) * 65],
                                            probsL[:, i * 128:(i + 1) * 128],
                                            start=(i == 0), stop=(i == len(kts) - 1))
                                    return last
                                tpv4 = ctx.sec("tensor", pv4,
                                               deps=[tmk4, tve4, tones4] + ([tpv4] if tpv4 else []),
                                               tag="pe")
                                tprev_exp4 = tmk4
                            r_rd4 = ROWS[0:1, 0:CH4]
                            den_ap = PSB[64:65, 0:2048].rearrange("p (b c) -> p b c", b=4)[:, :, 0:128]
                            trdb = ctx.reserve("sc")
                            def rden4(s, r_rd4=r_rd4, den_ap=den_ap):
                                s.activation(r_rd4.rearrange("p (b c) -> p b c", b=4),
                                             den_ap, AF.Ln, bias=0.0, scale=1.0)
                                return s.activation(r_rd4, r_rd4, AF.Exp,
                                                    bias=0.0, scale=-1.0)
                            sc4_steps.append(([tpv4], rden4, trdb))
                            tbcd4 = ctx.sec("tensor",
                                            lambda t, r_rd4=r_rd4: t.matmul(
                                                PSA[0:64, 1536:1536 + CH4],
                                                CROWF[0:1, 0:64], r_rd4,
                                                start=True, stop=True),
                                            deps=[trdb, tprev_fin4], tag="pe")
                            tcpd4 = ctx.sec("vector",
                                            lambda v: v.tensor_copy(
                                                rdsbL[0:64, :],
                                                PSA[0:64, 1536:1536 + CH4]),
                                            deps=[tbcd4], tag="ve")
                            pv_str = PSB[0:64, 0:2048].rearrange("p (b c) -> p b c", b=4)[:, :, 0:128]
                            if h % 2 == 0:
                                def fin4(v, sec4=sec4, pv_str=pv_str):
                                    return v.tensor_mul(
                                        outTL[0:64, sec4 * CH4:(sec4 + 1) * CH4].rearrange("p (b c) -> p b c", b=4),
                                        pv_str, rdsbL[0:64, :].rearrange("p (b c) -> p b c", b=4))
                            else:
                                def fin4(v, sec4=sec4, pv_str=pv_str):
                                    return v.tensor_mul(
                                        toddL[0:64, sec4 * CH4:(sec4 + 1) * CH4].rearrange("p (b c) -> p b c", b=4),
                                        pv_str, rdsbL[0:64, :].rearrange("p (b c) -> p b c", b=4))
                            tprev_fin4 = ctx.sec("vector", fin4,
                                                 deps=[tcpd4, tpv4], tag="ve")
                            if h % 2 == 1:
                                ev_odd4.append(tprev_fin4)
                        ctx.mseq("scalar", sc4_steps)
                        todd4 = ctx.dma("gpsimd", "dg",
                                        outTL[64:128, :].rearrange("p (s c) -> p s c", s=4),
                                        toddL[0:64, :].rearrange("p (s c) -> p s c", s=4),
                                        deps=ev_odd4)
                        tbsx = ctx.sec("vector",
                                       lambda v: v.tensor_copy(
                                           bndX[:, 0:ND],
                                           xmsec(t0, CH4)[:, :, CH4 - 1:CH4].squeeze()),
                                       deps=[], tag="ve")
                        tsp4 = None
                        if l == 0:
                            tsp4 = ctx.dma("gpsimd", "dg",
                                           x0_spill[:, c * ND * CH4:(c + 1) * ND * CH4].rearrange("p (s c) -> p s c", s=ND),
                                           xmsec(t0, CH4), deps=[tbsx])
                        tfin_ev = None
                        for rnd in range(2):
                            def wom(t, rnd=rnd):
                                for mi in range(4):
                                    m = rnd * 4 + mi
                                    for kt in range(4):
                                        last = t.matmul(
                                            PSA[:, mi * 512:mi * 512 + CH4],
                                            wol_sb[:, kt * D + m * 128:kt * D + (m + 1) * 128],
                                            csec(outTL, kt, CH4),
                                            start=(kt == 0), stop=(kt == 3))
                                return last
                            twom = ctx.sec("tensor", wom,
                                           deps=[todd4, tprev_fin4] +
                                                ([tfin_ev] if tfin_ev else []),
                                           tag="pe")
                            def wev(v, rnd=rnd):
                                for mi in range(4):
                                    m = rnd * 4 + mi
                                    last = v.tensor_add(xsec(m, t0, CH4),
                                                        xsec(m, t0, CH4),
                                                        PSA[:, mi * 512:mi * 512 + CH4])
                                return last
                            tfin_ev = ctx.sec("vector", wev,
                                              deps=[twom, tbsx] +
                                                   ([tsp4] if tsp4 else []),
                                              tag="ve")
                        prev_ch = tfin_ev
                    ctx.barrier()
                    _satt.close()
                    if sub4 == "a0" or (sub4 == "a1" and l == 1):
                        prev_l = prev_ch
                        continue
                    with contextlib.ExitStack() as _s4f:
                        _g = _s4f.enter_context
                        mixL = _g(nc.sbuf_tensor("mixL_%d" % l, [PART, ND * 1024], BF16))
                        prodb = _g(nc.sbuf_tensor("prodb_%d" % l, [PART, HIDT * 1024], BF16))
                        wstr4 = _g(nc.sbuf_tensor("wstr4_%d" % l, [PART, 16384], BF16))
                        ggsb = _g(nc.sbuf_tensor("ggsb_%d" % l, [PART, 1024], F32))
                        hbf2 = _g(nc.sbuf_tensor("hbf2_%d" % l, [PART, ND * 1024], BF16))
                        CHF4 = 1024
                        prev_f = prev_ch
                        for c in range(2):
                            t0 = c * CHF4
                            def shcast2(v, c=c):
                                r = hbf2[:].rearrange("p (s c) -> p s c", s=ND)
                                v.tensor_copy(r[:, 0:4, :],
                                              xmsec(t0, CHF4)[:, 0:4, :])
                                last = v.tensor_copy(r[:, 4:8, :],
                                                     xmsec(t0, CHF4, shift=1)[:, 4:8, :])
                                if c > 0:
                                    last = v.tensor_copy(r[:, 4:8, 0:1].squeeze(),
                                                         bndX[:, 4:8])
                                return last
                            thc2 = ctx.sec("vector", shcast2,
                                           deps=[prev_f], tag="ve")
                            tlnap = thc2
                            for hh in range(2):
                                th0 = t0 + hh * 512
                                hb3 = hbf2[:].rearrange("p (s c) -> p s c", s=ND)[:, :, hh * 512:(hh + 1) * 512]
                                tsq2 = ctx.sec("vector",
                                               lambda v, hb3=hb3: v.tensor_mul(
                                                   cmsec(mixL, 1024)[:, :, hh * 512:(hh + 1) * 512] if False else mixL[:].rearrange("p (s c) -> p s c", s=ND)[:, :, hh * 512:(hh + 1) * 512],
                                                   hb3, hb3),
                                               deps=[thc2, tlnap], tag="ve")
                                def stm2(t, hh=hh, hb3=hb3):
                                    for dt in range(ND):
                                        t.matmul(PSB[0:1, 0:512], CBF[:, 0:1],
                                                 hbf2[:, dt * CHF4 + hh * 512:dt * CHF4 + (hh + 1) * 512],
                                                 start=(dt == 0), stop=(dt == ND - 1))
                                    for dt in range(ND):
                                        last = t.matmul(
                                            PSB[0:1, 512:1024], CBF[:, 0:1],
                                            mixL[:, dt * 1024 + hh * 512:dt * 1024 + (hh + 1) * 512],
                                            start=(dt == 0), stop=(dt == ND - 1))
                                    return last
                                tst2 = ctx.sec("tensor", stm2,
                                               deps=[tsq2], tag="pe")
                                r_m4, r_rs4, r_mrs4 = rows3(512)
                                def rowv4(v):
                                    v.tensor_scalar(r_m4, PSB[0:1, 0:512],
                                                    1.0 / D, None, ALU.mult)
                                    v.tensor_mul(r_rs4, r_m4, r_m4)
                                    return v.scalar_tensor_tensor(
                                        r_mrs4, PSB[0:1, 512:1024], 1.0 / D,
                                        r_rs4, ALU.mult, ALU.subtract)
                                t34 = ctx.sec("vector", rowv4, deps=[tst2],
                                              tag="ve")
                                def rowsc4(s):
                                    s.activation(r_rs4, r_mrs4, AF.Ln,
                                                 bias=eps_ap, scale=1.0)
                                    return s.activation(r_rs4, r_rs4, AF.Exp,
                                                        bias=0.0, scale=-0.5)
                                t54 = ctx.sec("scalar", rowsc4,
                                              deps=[t34, t_eps], tag="sc")
                                t64 = ctx.sec("vector",
                                              lambda v: v.tensor_mul(r_mrs4, r_m4,
                                                                     r_rs4),
                                              deps=[t54], tag="ve")
                                tbc44 = ctx.sec("tensor",
                                                lambda t: [t.matmul(PSB[:, 512:1024],
                                                                    CROWF[:], r_rs4,
                                                                    start=True,
                                                                    stop=True),
                                                           t.matmul(PSB[:, 1024:1536],
                                                                    CROWF[:], r_mrs4,
                                                                    start=True,
                                                                    stop=True)][-1],
                                                deps=[t54, t64], tag="pe")
                                def lap1(v, hh=hh):
                                    for dt in range(ND):
                                        src = xsec(dt, th0, 512, shift=(1 if dt >= 4 else 0))
                                        v.scalar_tensor_tensor(
                                            mixL[:, dt * 1024 + hh * 512:dt * 1024 + (hh + 1) * 512] if False else ggsb[:, 0:512],
                                            src, ccol("lng_%d" % l, dt),
                                            PSB[:, 512:1024], ALU.mult, ALU.mult)
                                        last = v.scalar_tensor_tensor(
                                            ggsb[:, 0:512], PSB[:, 1024:1536],
                                            ccol("lng_%d" % l, dt), ggsb[:, 0:512],
                                            ALU.mult, ALU.subtract)
                                        last = v.tensor_scalar(
                                            ggsb[:, 0:512], ggsb[:, 0:512], -1.0,
                                            None, ALU.mult)
                                        last = v.tensor_scalar(
                                            mixL[:, dt * 1024 + hh * 512:dt * 1024 + (hh + 1) * 512],
                                            ggsb[:, 0:512],
                                            ccol("lnb_%d" % l, dt), None,
                                            ALU.add)
                                    return last
                                tlnap = ctx.sec("vector", lap1,
                                                deps=[tbc44, tsq2], tag="ve")
                                if c > 0 and hh == 0:
                                    def bfix(v):
                                        for dt in range(4, ND):
                                            t1 = ggsb[:, 0:1]
                                            t2 = ggsb[:, 1:2]
                                            v.scalar_tensor_tensor(
                                                t1, bndX[:, dt:dt + 1],
                                                ccol("lng_%d" % l, dt),
                                                PSB[:, 512:513], ALU.mult,
                                                ALU.mult)
                                            v.tensor_scalar(
                                                t2, PSB[:, 1024:1025],
                                                ccol("lng_%d" % l, dt), None,
                                                ALU.mult)
                                            last = v.scalar_tensor_tensor(
                                                mixL[:, dt * 1024:dt * 1024 + 1],
                                                t1, ccol("lnb_%d" % l, dt), t2,
                                                ALU.add, ALU.subtract)
                                        return last
                                    tlnap = ctx.sec("vector", bfix,
                                                    deps=[tlnap], tag="ve")
                            if l == 0 and c == 1 and "mixl" in dbg_out:
                                tlnap = ctx.dma("sync", "dq", dbg_out["mixl"],
                                                mixL[:], deps=[tlnap])
                            tbs4 = ctx.sec("vector",
                                           lambda v: v.tensor_copy(
                                               bndX[:, 0:ND],
                                               xmsec(t0, CHF4)[:, :, CHF4 - 1:CHF4].squeeze()),
                                           deps=[tlnap], tag="ve")
                            tsq_prev = tlnap
                            sq_par = {0: None, 1: None}
                            gg_steps = []
                            twpc = {}
                            for jg in range(6):
                                npair = 4 if jg < 5 else 2
                                srcA = NM["l1w1"][l][:, jg * 512:jg * 512 + npair * 128]
                                srcG = NM["l1w1"][l][:, 2816 + jg * 512:2816 + jg * 512 + npair * 128]
                                half = (jg % 2) * 8192
                                tA = ctx.dma("gpsimd", "dg",
                                             wstr4[:, half:half + npair * 128 * ND].rearrange("p (s o) -> p s o", s=ND),
                                             srcA.rearrange("(s p) o -> p s o", p=PART),
                                             deps=[twpc[jg - 2]] if jg >= 2 else [prev_f])
                                tG = ctx.dma("gpsimd", "dg",
                                             wstr4[:, half + 4096:half + 4096 + npair * 128 * ND].rearrange("p (s o) -> p s o", s=ND),
                                             srcG.rearrange("(s p) o -> p s o", p=PART),
                                             deps=[twpc[jg - 2]] if jg >= 2 else [prev_f])
                                tlast = None
                                for jj in range(npair):
                                    j = jg * 4 + jj
                                    for nc2 in range(2):
                                        jp = (jg * 4 + jj) % 2
                                        psJ = PSA if jp == 0 else PSB
                                        def amm(t, jj=jj, nc2=nc2, half=half, psJ=psJ):
                                            for kt in range(ND):
                                                last = t.matmul(
                                                    psJ[:, nc2 * 512:nc2 * 512 + 512],
                                                    wstr4[:, half + kt * (npair * 128) + jj * 128: half + kt * (npair * 128) + (jj + 1) * 128],
                                                    mixL[:, kt * 1024 + nc2 * 512:kt * 1024 + (nc2 + 1) * 512],
                                                    start=(kt == 0),
                                                    stop=(kt == ND - 1))
                                            return last
                                        def gmm(t, jj=jj, nc2=nc2, half=half, psJ=psJ):
                                            for kt in range(ND):
                                                last = t.matmul(
                                                    psJ[:, 1024 + nc2 * 512:1024 + nc2 * 512 + 512],
                                                    wstr4[:, half + 4096 + kt * (npair * 128) + jj * 128: half + 4096 + kt * (npair * 128) + (jj + 1) * 128],
                                                    mixL[:, kt * 1024 + nc2 * 512:kt * 1024 + (nc2 + 1) * 512],
                                                    start=(kt == 0),
                                                    stop=(kt == ND - 1))
                                            return last
                                        dmm = [tA, tG, tlnap] + ([sq_par[jp]] if sq_par[jp] else [])
                                        ta_mm = ctx.sec("tensor", lambda t, f=amm: f(t),
                                                        deps=dmm, tag="pe")
                                        tg_mm = ctx.sec("tensor", lambda t, f=gmm: f(t),
                                                        deps=[ta_mm], tag="pe")
                                        tgg = ctx.reserve("sc")
                                        gg_half = ggsb[:, jp * 512:(jp + 1) * 512]
                                        gg_steps.append(([tg_mm],
                                                         (lambda s, j=j, nc2=nc2, psJ=psJ, gg_half=gg_half: s.activation(
                                                             gg_half,
                                                             psJ[:, 1024 + nc2 * 512:1024 + nc2 * 512 + 512],
                                                             AF.Gelu,
                                                             bias=CCOL[:, CC["b1p_%d" % l] + 22 + j:CC["b1p_%d" % l] + 22 + j + 1],
                                                             scale=1.0)), tgg))
                                        tsq_prev = ctx.sec("vector",
                                                           lambda v, j=j, nc2=nc2, psJ=psJ, gg_half=gg_half: v.scalar_tensor_tensor(
                                                               prodb[:, j * 1024 + nc2 * 512:j * 1024 + (nc2 + 1) * 512],
                                                               psJ[:, nc2 * 512:nc2 * 512 + 512],
                                                               CCOL[:, CC["b1p_%d" % l] + j:CC["b1p_%d" % l] + j + 1],
                                                               gg_half, ALU.add,
                                                               ALU.mult),
                                                           deps=[tgg], tag="ve")
                                        sq_par[jp] = tsq_prev
                                        tlast = tsq_prev
                                twpc[jg] = tlast
                            ctx.mseq("scalar", gg_steps)
                            twv4 = {}
                            tacc4 = None
                            tevac4 = None
                            for nc2 in range(2):
                                tacc4 = None
                                for pp in range(11):
                                    src2 = NM["l1w2"][l][pp * 256:(pp + 1) * 256, :]
                                    half = (pp % 2) * 8192
                                    dep2 = [twv4[(nc2, pp - 2)]] if pp >= 2 else ([sq_par[0], sq_par[1]] if nc2 == 0 else [tevac4])
                                    t2p = ctx.dma("gpsimd", "dg",
                                                  wstr4[:, half:half + 2048].rearrange("p (s o) -> p s o", s=2),
                                                  src2.rearrange("(s p) o -> p s o", p=PART),
                                                  deps=dep2)
                                    def w2mm(t, pp=pp, half=half, nc2=nc2):
                                        for m in range(8):
                                            for k2 in range(2):
                                                kt = pp * 2 + k2
                                                last = t.matmul(
                                                    (PSA if m < 4 else PSB)[:, (m % 4) * 512:(m % 4) * 512 + 512],
                                                    wstr4[:, half + k2 * 1024 + m * 128:half + k2 * 1024 + (m + 1) * 128],
                                                    prodb[:, kt * 1024 + nc2 * 512:kt * 1024 + (nc2 + 1) * 512],
                                                    start=(pp == 0 and k2 == 0),
                                                    stop=(pp == 10 and k2 == 1))
                                        return last
                                    tacc4 = ctx.sec("tensor", w2mm,
                                                    deps=[t2p, sq_par[0], sq_par[1]] +
                                                         ([tacc4] if tacc4 else []) +
                                                         ([tevac4] if tevac4 else []),
                                                    tag="pe")
                                    twv4[(nc2, pp)] = tacc4
                                tevac4 = ctx.sec("vector",
                                                 lambda v, nc2=nc2: [v.scalar_tensor_tensor(
                                                     xsec(m, t0 + nc2 * 512, 512),
                                                     (PSA if m < 4 else PSB)[:, (m % 4) * 512:(m % 4) * 512 + 512],
                                                     ccol("b2p_%d" % l, m),
                                                     xsec(m, t0 + nc2 * 512, 512),
                                                     ALU.add, ALU.add)
                                                     for m in range(8)][-1],
                                                 deps=[tacc4], tag="ve")
                            if l == 1:
                                x0v4 = wstr4.bitcast(F32)
                                x0r = x0v4[:].rearrange("p (a s c) -> p a s c",
                                                        a=2, s=ND)[:, :, :, 0:512]
                                tx4a = ctx.dma("gpsimd", "dg", x0r[:, 0],
                                               x0_spill[:, (2 * c) * ND * 512:(2 * c + 1) * ND * 512].rearrange("p (s c) -> p s c", s=ND),
                                               deps=[tacc4])
                                tx4b = ctx.dma("gpsimd", "dg", x0r[:, 1],
                                               x0_spill[:, (2 * c + 1) * ND * 512:(2 * c + 2) * ND * 512].rearrange("p (s c) -> p s c", s=ND),
                                               deps=[tacc4])
                                def zadd4(v):
                                    for half in range(2):
                                        last = v.tensor_add(
                                            xmsec(t0 + half * 512, 512),
                                            xmsec(t0 + half * 512, 512),
                                            x0r[:, half])
                                    return last
                                tz4m = ctx.sec("vector", zadd4,
                                               deps=[tx4a, tx4b, tevac4],
                                               tag="ve")
                                hb512 = hbf2[:, 0:ND * 512]
                                ml512 = mixL[:, 0:ND * 512]
                                for half in range(2):
                                    th0p = t0 + half * 512
                                    tbcp, rs_bp, mrs_bp, _tcp = ln_chunk(
                                        [tz4m], xmsec(th0p, 512), 512, hb512,
                                        ml512, 512)
                                    def app4(v, th0p=th0p, rs_bp=rs_bp,
                                             mrs_bp=mrs_bp):
                                        for dt in range(ND):
                                            v.tensor_mul(xsec(dt, th0p, 512),
                                                         xsec(dt, th0p, 512),
                                                         rs_bp)
                                        for dt in range(ND):
                                            last = v.tensor_sub(
                                                xsec(dt, th0p, 512),
                                                xsec(dt, th0p, 512), mrs_bp)
                                        return last
                                    tz4m = ctx.sec("vector", app4,
                                                   deps=[tbcp], tag="ve")
                                tevac4 = tz4m
                            prev_f = tevac4
                        prev_l = prev_f
                if sub4 != "full":
                    pass
            ctx.barrier()

        with contextlib.ExitStack() as _s5:
            ostage = _s5.enter_context(nc.sbuf_tensor("ostage", [PART, 2 * D], F32))
            prevv = []
            for tt in range(NTT):
                def tro(t, tt=tt):
                    for dt in range(ND):
                        last = t.transpose(PSB[:, dt * 128:dt * 128 + 128],
                                           xsec(dt, tt * PART, PART), IDN[:])
                    return last
                tpe = ctx.sec("tensor", tro, deps=(prevv[-1:] if prevv else []),
                              tag="pe")
                sl = ostage[:, (tt % 2) * D:(tt % 2) * D + D]

                def cpo(v, sl=sl):
                    src = PSB[:, 0:ND * 128].rearrange("p (s c) -> p s c", s=ND)
                    return v.tensor_copy(sl.rearrange("p (s c) -> p s c", s=ND),
                                         src)
                tve = ctx.sec("vector", cpo, deps=[tpe], tag="ve")
                ctx.dma("sync", "dq", y_out[tt * PART:(tt + 1) * PART, :], sl,
                        deps=[tve])
                prevv.append(tve)
            ctx.sec("sync", lambda e: None,
                    deps=[ctx.tok(f"dq{j}") for j in range(DMAPOOL)])

    return nc


HID = 2730
HIDP = 2816


def _padw1(w1):
    out = np.zeros((2, 1024, 2 * HIDP), np.float32)
    out[:, :, :HID] = w1[:, :, :HID]
    out[:, :, HIDP:HIDP + HID] = w1[:, :, HID:]
    return out


def _padb1(b1):
    out = np.zeros((2, 2 * HIDP), np.float32)
    out[:, :HID] = b1[:, :HID]
    out[:, HIDP:HIDP + HID] = b1[:, HID:]
    return out


def _padw2(w2):
    out = np.zeros((2, HIDP, 1024), np.float32)
    out[:, :HID, :] = w2
    return out


def _in_maps(inputs, extra_outputs=False):
    consts = _host_consts()
    nmap = {
        "ln1_w": inputs["L0_attn_ln1_w"], "ln1_b": inputs["L0_attn_ln1_b"],
        "ln2_w": inputs["L0_attn_ln2_w"], "ln2_b": inputs["L0_attn_ln2_b"],
        "time_decay": inputs["L0_attn_time_decay"],
        "time_first": inputs["L0_attn_time_first"],
        "mix_k": np.ascontiguousarray(inputs["L0_attn_time_mix_k"]).reshape(-1),
        "mix_v": np.ascontiguousarray(inputs["L0_attn_time_mix_v"]).reshape(-1),
        "mix_r": np.ascontiguousarray(inputs["L0_attn_time_mix_r"]).reshape(-1),
        "w_key": inputs["L0_attn_w_key"], "w_value": inputs["L0_attn_w_value"],
        "w_recept": inputs["L0_attn_w_recept"],
        "w_output": inputs["L0_attn_w_output"],
        "f_mix_k": np.ascontiguousarray(inputs["L0_attn_f_time_mix_k"]).reshape(-1),
        "f_mix_r": np.ascontiguousarray(inputs["L0_attn_f_time_mix_r"]).reshape(-1),
        "f_w_key": inputs["L0_attn_f_w_key"],
        "f_w_recept": inputs["L0_attn_f_w_recept"],
        "f_w_value": inputs["L0_attn_f_w_value"],
        "U1": inputs["L0_ffn_U1"], "V1": inputs["L0_ffn_V1"],
        "b1": inputs["L0_ffn_b1"], "U2": inputs["L0_ffn_U2"],
        "V2": inputs["L0_ffn_V2"], "b2": inputs["L0_ffn_b2"],
        "Wq": inputs["L1_attn_Wq"], "Wk": inputs["L1_attn_Wk"],
        "Wv": inputs["L1_attn_Wv"], "Wo": inputs["L1_attn_Wo"],
        "bq": inputs["L1_attn_bq"], "bk": inputs["L1_attn_bk"],
        "bv": inputs["L1_attn_bv"], "bo": inputs["L1_attn_bo"],
        "wqkv": inputs["L1_ffn_wqkv"], "q_scale": inputs["L1_ffn_q_scale"],
        "k_scale": inputs["L1_ffn_k_scale"], "l1wo": inputs["L1_ffn_wo"],
        "ln_g": inputs["L1_ffn_ln_g"], "ln_b": inputs["L1_ffn_ln_b"],
        "l1w2": _padw2(inputs["L1_ffn_w2"]), "l1b2": inputs["L1_ffn_b2"],
        "l1w1": _padw1(inputs["L1_ffn_w1"]),
        "l1b1": _padb1(inputs["L1_ffn_b1"]),
    }
    nmap.update(consts)
    nmap = {k: np.ascontiguousarray(v) for k, v in nmap.items()}
    x = np.ascontiguousarray(inputs["L0_attn_x"], dtype=np.float32)
    maps = []
    for c in range(8):
        m = dict(nmap)
        m["x"] = np.ascontiguousarray(x[c])
        maps.append(m)
    return maps


def kernel(**inputs):
    nc = build_program(stages=STAGES, dbg=DBG)
    maps = _in_maps(inputs)
    res = run_bass_kernel_spmd(nc, maps, core_ids=list(range(8)))
    return np.stack([np.asarray(r["y"]) for r in res.results], axis=0)
